# version 03
# baseline (speedup 1.0000x reference)
import numpy as np
from contextlib import ExitStack
import concourse.bass as bass
import concourse.mybir as mybir
from concourse.bass_utils import run_bass_kernel_spmd

F32 = mybir.dt.float32
BF16 = mybir.dt.bfloat16
I32 = mybir.dt.int32
AF = mybir.ActivationFunctionType
ALU = mybir.AluOpType

ENGINES = ("sync", "vector", "scalar", "tensor", "gpsimd")


class Sem:
    def __init__(self, h):
        self.h = h
        self.v = 0

    def inc(self, n=1):
        self.v += n
        return self.v


class Prog:
    def __init__(self, nc):
        self.nc = nc
        self.ops = []

    def emit(self, eng, fn):
        self.ops.append((eng, fn))

    def wait(self, eng, sem, val=None):
        v = sem.v if val is None else val
        if v > 0:
            self.ops.append((eng, lambda e, s=sem.h, vv=v: e.wait_ge(s, vv)))

    def barrier(self, sems, engines=ENGINES):
        for eng in engines:
            for s in sems:
                self.wait(eng, s)

    def attach(self, block):
        for name in ENGINES:
            dec = getattr(block, name)

            def body(eng, _name=name):
                for e, fn in self.ops:
                    if e == _name:
                        fn(eng)

            dec(body)


def mk_ctx(nc, es):
    sb = lambda name, shape, dt: es.enter_context(nc.sbuf_tensor(name, shape, dt))
    ps = lambda name, shape, dt: es.enter_context(nc.psum_tensor(name, shape, dt))
    sem = lambda name: Sem(es.enter_context(nc.semaphore(name)))
    return sb, ps, sem




def ln_core(p, E, s_ve, s_sc, halves, sqs, lnc, epscol, denom=1024.0):
    off = 0
    for idx, (h, w) in enumerate(halves):
        last = idx == len(halves) - 1
        E("scalar", lambda e, h=h, w=w, off=off, last=last: (
            e.activation(sqs[:, off:off + w], h, AF.Square).then_inc(s_sc.h, 1)
            if last else e.activation(sqs[:, off:off + w], h, AF.Square)))
        off += w
    s_sc.inc()
    p.wait("vector", s_sc)
    off = 0
    for idx, (h, w) in enumerate(halves):
        last = idx == len(halves) - 1
        E("vector", lambda e, h=h, idx=idx: e.tensor_reduce(lnc[:, idx:idx + 1], h, mybir.AxisListType.X, ALU.add))
        E("vector", lambda e, w=w, off=off, idx=idx, last=last: (
            e.tensor_reduce(lnc[:, 2 + idx:3 + idx], sqs[:, off:off + w], mybir.AxisListType.X, ALU.add).then_inc(s_ve.h, 1)
            if last else
            e.tensor_reduce(lnc[:, 2 + idx:3 + idx], sqs[:, off:off + w], mybir.AxisListType.X, ALU.add)))
        off += w
    s_ve.inc()
    two = len(halves) == 2

    def vfence():
        pass
    if two:
        E("vector", lambda e: e.tensor_scalar(out=lnc[:, 4:5], in0=lnc[:, 0:1], scalar1=1.0 / denom,
                                              scalar2=None, op0=ALU.mult, op1=ALU.bypass))
        E("vector", lambda e: e.tensor_scalar(out=lnc[:, 6:7], in0=lnc[:, 2:3], scalar1=1.0 / denom,
                                              scalar2=None, op0=ALU.mult, op1=ALU.bypass).then_inc(s_ve.h, 1))
        s_ve.inc()
        p.wait("vector", s_ve)
        E("vector", lambda e: e.tensor_scalar(out=lnc[:, 5:6], in0=lnc[:, 1:2], scalar1=1.0 / denom,
                                              scalar2=lnc[:, 4:5], op0=ALU.mult, op1=ALU.add))
        E("vector", lambda e: e.tensor_scalar(out=lnc[:, 7:8], in0=lnc[:, 3:4], scalar1=1.0 / denom,
                                              scalar2=lnc[:, 6:7], op0=ALU.mult, op1=ALU.add).then_inc(s_ve.h, 1))
        s_ve.inc()
        p.wait("vector", s_ve)
    else:
        E("vector", lambda e: e.tensor_scalar(out=lnc[:, 5:6], in0=lnc[:, 0:1], scalar1=1.0 / denom,
                                              scalar2=None, op0=ALU.mult, op1=ALU.bypass))
        E("vector", lambda e: e.tensor_scalar(out=lnc[:, 7:8], in0=lnc[:, 2:3], scalar1=1.0 / denom,
                                              scalar2=None, op0=ALU.mult, op1=ALU.bypass).then_inc(s_ve.h, 1))
        s_ve.inc()
        p.wait("vector", s_ve)
    E("vector", lambda e: e.tensor_mul(lnc[:, 8:9], lnc[:, 5:6], lnc[:, 5:6]).then_inc(s_ve.h, 1))
    s_ve.inc()
    p.wait("vector", s_ve)
    E("vector", lambda e: e.tensor_scalar(out=lnc[:, 10:11], in0=lnc[:, 8:9], scalar1=-1.0,
                                          scalar2=lnc[:, 7:8], op0=ALU.mult, op1=ALU.add).then_inc(s_ve.h, 1))
    s_ve.inc()
    p.wait("vector", s_ve)
    E("vector", lambda e: e.tensor_scalar(out=lnc[:, 11:12], in0=lnc[:, 10:11], scalar1=0.0,
                                          scalar2=None, op0=ALU.max, op1=ALU.bypass).then_inc(s_ve.h, 1))
    s_ve.inc()
    p.wait("scalar", s_ve)
    E("scalar", lambda e: e.activation(lnc[:, 12:13], lnc[:, 11:12], AF.Ln, bias=epscol))
    E("scalar", lambda e: e.activation(lnc[:, 13:14], lnc[:, 12:13], AF.Exp, scale=-0.5).then_inc(s_sc.h, 1))
    s_sc.inc()
    return lnc[:, 5:6], lnc[:, 13:14]

def build_l0_attn(T=4096, TC=512, debug=False):
    DM = 1024
    NCH = T // TC
    NT = TC // 128

    nc = bass.Bass("TRN2", target_bir_lowering=False)
    d = lambda *a, **k: nc.dram_tensor(*a, **k).ap()
    x_h = d("x", [T, DM], F32, kind="ExternalInput")
    wg_h = d("w_in_g", [DM, 1024], F32, kind="ExternalInput")
    wi_h = d("w_in_i", [DM, 1024], F32, kind="ExternalInput")
    wd_h = d("w_delta", [DM, 1024], F32, kind="ExternalInput")
    wo_h = d("w_out", [1024, DM], F32, kind="ExternalInput")
    wb_h = d("w_b", [DM, 16], F32, kind="ExternalInput")
    wc_h = d("w_c", [DM, 16], F32, kind="ExternalInput")
    bdelta_h = d("b_delta", [128, 8], F32, kind="ExternalInput")
    convw_h = d("conv_w", [128, 32], F32, kind="ExternalInput")
    convb_h = d("conv_b", [128, 8], F32, kind="ExternalInput")
    alog_h = d("a_log", [128, 128], F32, kind="ExternalInput")
    dd_h = d("dd", [128, 8], F32, kind="ExternalInput")
    sel_h = d("sel", [16, 16 * 128], BF16, kind="ExternalInput")
    ident_h = d("ident", [128, 128], BF16, kind="ExternalInput")
    part_h = d("part", [DM, T], F32, kind="ExternalOutput")
    if debug:
        dbg_delta_h = d("dbg_delta", [128, 8 * TC], F32, kind="ExternalOutput")
        dbg_xc_h = d("dbg_xc", [128, 8 * TC], F32, kind="ExternalOutput")
        dbg_xnt_h = d("dbg_xnt", [128, 8 * TC], BF16, kind="ExternalOutput")
        dbg_bmt_h = d("dbg_bmt", [16, TC], BF16, kind="ExternalOutput")
        dbg_cmt_h = d("dbg_cmt", [16, TC], BF16, kind="ExternalOutput")
        dbg_ybf_h = d("dbg_ybf", [128, 8 * TC], BF16, kind="ExternalOutput")
        dbg_brep_h = d("dbg_brep", [128, 8 * TC], BF16, kind="ExternalOutput")

    if debug:
        d2_ = dict(dbg_delta=dbg_delta_h, dbg_xc=dbg_xc_h, dbg_xnt=dbg_xnt_h, dbg_bmt=dbg_bmt_h,
                   dbg_cmt=dbg_cmt_h, dbg_ybf=dbg_ybf_h, dbg_brep=dbg_brep_h)
    es = ExitStack()
    sb, ps, sem = mk_ctx(nc, es)

    with es:
        w_g = sb("w_g", [128, 8 * 1024], BF16)
        w_i = sb("w_i", [128, 8 * 1024], BF16)
        w_d = sb("w_d", [128, 8 * 1024], BF16)
        w_o = sb("w_o", [128, 8 * 1024], BF16)
        w_b = sb("s_w_b", [128, 8 * 16], BF16)
        w_c = sb("s_w_c", [128, 8 * 16], BF16)
        sel = sb("s_sel", [16, 16 * 128], BF16)
        ident = sb("s_ident", [128, 128], BF16)
        a_sb = sb("a_sb", [128, 128], F32)
        bdelta = sb("s_bdelta", [128, 8], F32)
        convw = sb("s_convw", [128, 32], F32)
        convb = sb("s_convb", [128, 8], F32)
        dd = sb("s_dd", [128, 8], F32)
        stage = sb("stage", [128, 2048], F32)
        xx = sb("xx", [128, 1024], F32)
        xn = sb("xn", [128, 1024], BF16)
        mv = sb("mv", [128, 2], F32)
        lnc = sb("lnc", [128, 16], F32)
        lnjunk = sb("lnjunk", [128, 1024], F32)
        rstd = sb("rstd", [128, 1], F32)
        xnT = sb("xnT", [128, 8 * TC], BF16)
        delta = sb("delta", [128, 8 * TC], F32)
        sp1 = sb("sp1", [128, TC], F32)
        sp2 = sb("sp2", [128, TC], F32)
        xing = sb("xing", [128, 8 * (TC + 3)], F32)
        xc = sb("xc", [128, 8 * TC], F32)
        gsil = sb("gsil", [128, TC], F32)
        bmt = sb("bmt", [16, TC], BF16)
        cmt = sb("cmt", [16, TC], BF16)
        brep = sb("brep", [128, 8 * TC], BF16)
        crep = sb("crep", [128, 8 * TC], BF16)
        wv = [sb(f"wv{i}", [128, TC], F32) for i in range(2)]
        ab = [sb(f"ab{i}", [128, TC], F32) for i in range(2)]
        bb = [sb("bb0", [128, TC], F32)] * 2
        hb = [sb("hb0", [128, TC], F32)] * 2
        pb = [sb(f"pb{i}", [128, TC], BF16) for i in range(2)]
        state = sb("state", [128, 128], F32)
        cst = sb("cst", [128, 2], F32)
        y_acc = sb("y_acc", [128, 8 * TC], F32)
        y_bf = sb("y_bf", [128, 8 * TC], BF16)

        pp = [ps(f"pp{i}", [128, 512], F32) for i in range(2)]
        py = [ps(f"py{i}", [128, 512], F32) for i in range(2)]
        pbc = [ps(f"pbc{i}", [128, 512], F32) for i in range(2)]
        pbm = ps("pbm", [16, 512], F32)
        ptp = ps("ptp", [128, 512], BF16)

        s_ld = sem("s_ld")
        s_ld2 = sem("s_ld2")
        s_lda = sem("s_lda")
        s_st = sem("s_st")
        s_ve = sem("s_ve")
        s_sc = sem("s_sc")
        s_te = sem("s_te")
        allsems = (s_ld, s_ld2, s_lda, s_st, s_ve, s_sc, s_te)

        p = Prog(nc)
        E = p.emit

        def load_weight(whbm, wsb, ncols):
            for q in range(4):
                src = whbm[q * 256:(q + 1) * 256, :].rearrange("(kc p) o -> p kc o", p=128)
                p.wait("sync", s_sc)
                E("sync", lambda e, s=src, n=ncols: e.dma_start(out=stage[:, 0:2 * n].rearrange("p (a b) -> p a b", a=2), in_=s).then_inc(s_ld.h, 16))
                s_ld.inc(16)
                p.wait("scalar", s_ld)
                for kc in range(2):
                    kca = q * 2 + kc
                    last = kc == 1
                    E("scalar", lambda e, kca=kca, kc=kc, n=ncols, wsb=wsb, last=last: (
                        e.copy(wsb[:, kca * n:(kca + 1) * n], stage[:, kc * n:(kc + 1) * n]).then_inc(s_sc.h, 1)
                        if last else
                        e.copy(wsb[:, kca * n:(kca + 1) * n], stage[:, kc * n:(kc + 1) * n])))
                s_sc.inc()

        load_weight(wg_h, w_g, 1024)
        load_weight(wi_h, w_i, 1024)
        load_weight(wd_h, w_d, 1024)
        load_weight(wo_h, w_o, 1024)
        load_weight(wb_h, w_b, 16)
        load_weight(wc_h, w_c, 16)

        def simple_load(src, dst):
            E("sync", lambda e, s=src, dst=dst: e.dma_start(out=dst, in_=s).then_inc(s_ld2.h, 16))
            s_ld2.inc(16)

        simple_load(sel_h, sel[:, :])
        simple_load(ident_h, ident[:, :])
        simple_load(bdelta_h, bdelta[:, :])
        simple_load(convw_h, convw[:, :])
        simple_load(convb_h, convb[:, :])
        simple_load(dd_h, dd[:, :])
        simple_load(alog_h, a_sb[:, :])
        p.wait("scalar", s_ld)
        p.wait("scalar", s_ld2)
        p.wait("vector", s_ld2)
        p.wait("tensor", s_ld2)
        p.wait("tensor", s_ld)
        E("scalar", lambda e: e.activation(a_sb[:, :], a_sb[:, :], AF.Exp).then_inc(s_sc.h, 1))
        s_sc.inc()
        p.wait("vector", s_sc)
        E("vector", lambda e: e.tensor_scalar_mul(a_sb[:, :], a_sb[:, :], -1.0))
        E("vector", lambda e: e.memset(cst[:, 0:1], 1e-5))
        E("vector", lambda e: e.memset(cst[:, 1:2], 1.0))
        E("vector", lambda e: e.memset(state[:, :], 0.0))
        E("vector", lambda e: e.memset(xing[:, :], 0.0).then_inc(s_ve.h, 1))
        s_ve.inc()
        p.wait("vector", s_ve)

        for c in range(NCH):
            p.barrier(allsems)

            for i in range(NT):
                t0 = c * TC + i * 128
                p.wait("sync", s_ve)
                p.wait("sync", s_te)
                p.wait("sync", s_sc)
                E("sync", lambda e, t0=t0: e.dma_start(out=xx[:, :], in_=x_h[t0:t0 + 128, :]).then_inc(s_lda.h, 16))
                s_lda.inc(16)
                p.wait("vector", s_lda)
                p.wait("vector", s_te)
                p.wait("vector", s_sc)
                p.wait("scalar", s_lda)
                p.wait("scalar", s_ve)
                mean_ap, rstd_ap = ln_core(p, E, s_ve, s_sc, [(xx[:, :], 1024)], lnjunk, lnc, cst[:, 0:1])
                p.wait("vector", s_sc)
                E("vector", lambda e, mean_ap=mean_ap, rstd_ap=rstd_ap: e.tensor_scalar(
                    out=xn[:, :], in0=xx[:, :], scalar1=mean_ap,
                    scalar2=rstd_ap, op0=ALU.subtract, op1=ALU.mult).then_inc(s_ve.h, 1))
                s_ve.inc()
                p.wait("tensor", s_ve)
                for k in range(8):
                    slot = k % 4
                    p.wait("tensor", s_sc)
                    E("tensor", lambda e, k=k, slot=slot: e.transpose(
                        ptp[:, slot * 128:(slot + 1) * 128], xn[:, k * 128:(k + 1) * 128],
                        ident[:, :]).then_inc(s_te.h, 1))
                    s_te.inc()
                    p.wait("scalar", s_te)
                    E("scalar", lambda e, k=k, slot=slot, i=i: e.copy(
                        xnT[:, k * TC + i * 128:k * TC + (i + 1) * 128],
                        ptp[:, slot * 128:(slot + 1) * 128]).then_inc(s_sc.h, 1))
                    s_sc.inc()
            xnT_ready = s_sc.v

            def mm_proj(wsb, oc, bank, extra_wait=()):
                for kc in range(8):
                    if kc == 0:
                        p.wait("tensor", s_sc, xnT_ready)
                        p.wait("tensor", s_ve)
                        p.wait("tensor", s_sc)
                        for (ss, vv) in extra_wait:
                            p.wait("tensor", ss, vv)
                    E("tensor", lambda e, wsb=wsb, oc=oc, kc=kc, bank=bank, last=(kc == 7): (
                        e.matmul(pp[bank][:, :], wsb[:, kc * 1024 + oc * 128:kc * 1024 + (oc + 1) * 128],
                                 xnT[:, kc * TC:(kc + 1) * TC], start=(kc == 0), stop=last).then_inc(s_te.h, 1)
                        if last else
                        e.matmul(pp[bank][:, :], wsb[:, kc * 1024 + oc * 128:kc * 1024 + (oc + 1) * 128],
                                 xnT[:, kc * TC:(kc + 1) * TC], start=(kc == 0), stop=last)))
                s_te.inc()

            for oc in range(8):
                mm_proj(w_g, oc, 0)
                p.wait("scalar", s_te)
                p.wait("scalar", s_ve)
                E("scalar", lambda e: e.activation(gsil[:, :], pp[0][:, :], AF.Silu).then_inc(s_sc.h, 1))
                s_sc.inc()
                mm_proj(w_i, oc, 1)
                p.wait("vector", s_te)
                p.wait("vector", s_sc)
                E("vector", lambda e, oc=oc: e.tensor_mul(
                    xing[:, oc * (TC + 3) + 3:oc * (TC + 3) + 3 + TC], gsil[:, :],
                    pp[1][:, :]).then_inc(s_ve.h, 1))
                s_ve.inc()

            for oc in range(8):
                bank = oc % 2
                mm_proj(w_d, oc, bank)
                p.wait("scalar", s_te)
                p.wait("scalar", s_ve)
                E("scalar", lambda e, oc=oc, bank=bank: e.activation(sp1[:, :], pp[bank][:, :], AF.Relu, bias=bdelta[:, oc:oc + 1]))
                E("scalar", lambda e, oc=oc, bank=bank: e.activation(sp2[:, :], pp[bank][:, :], AF.Abs, bias=bdelta[:, oc:oc + 1]))
                E("scalar", lambda e: e.activation(sp2[:, :], sp2[:, :], AF.Exp, scale=-1.0))
                E("scalar", lambda e: e.activation(sp2[:, :], sp2[:, :], AF.Ln, bias=cst[:, 1:2]).then_inc(s_sc.h, 1))
                s_sc.inc()
                p.wait("vector", s_sc)
                E("vector", lambda e, oc=oc: e.tensor_add(delta[:, oc * TC:(oc + 1) * TC], sp1[:, :], sp2[:, :]).then_inc(s_ve.h, 1))
                s_ve.inc()

            for wsb, dstt in ((w_b, bmt), (w_c, cmt)):
                for kc in range(8):
                    if kc == 0:
                        p.wait("tensor", s_sc)
                    E("tensor", lambda e, wsb=wsb, kc=kc, last=(kc == 7): (
                        e.matmul(pbm[:, :TC], wsb[:, kc * 16:(kc + 1) * 16],
                                 xnT[:, kc * TC:(kc + 1) * TC], start=(kc == 0), stop=last).then_inc(s_te.h, 1)
                        if last else
                        e.matmul(pbm[:, :TC], wsb[:, kc * 16:(kc + 1) * 16],
                                 xnT[:, kc * TC:(kc + 1) * TC], start=(kc == 0), stop=last)))
                s_te.inc()
                p.wait("scalar", s_te)
                E("scalar", lambda e, dstt=dstt: e.copy(dstt[:, :], pbm[:, :TC]).then_inc(s_sc.h, 1))
                s_sc.inc()

            p.wait("vector", s_ve)
            for dt in range(8):
                base = dt * (TC + 3)
                E("vector", lambda e, dt=dt, base=base: e.tensor_scalar(
                    out=xc[:, dt * TC:(dt + 1) * TC], in0=xing[:, base:base + TC],
                    scalar1=convw[:, dt * 4:dt * 4 + 1], scalar2=convb[:, dt:dt + 1],
                    op0=ALU.mult, op1=ALU.add))
                for j in range(1, 4):
                    E("vector", lambda e, dt=dt, base=base, j=j: e.scalar_tensor_tensor(
                        out=xc[:, dt * TC:(dt + 1) * TC], in0=xing[:, base + j:base + j + TC],
                        scalar=convw[:, dt * 4 + j:dt * 4 + j + 1], in1=xc[:, dt * TC:(dt + 1) * TC],
                        op0=ALU.mult, op1=ALU.add))
            for dt in range(8):
                base = dt * (TC + 3)
                E("vector", lambda e, base=base: e.tensor_copy(xing[:, base:base + 3], xing[:, base + TC:base + TC + 3]))
            E("vector", lambda e: e.nop().then_inc(s_ve.h, 1))
            s_ve.inc()
            xc_ready = s_ve.v

            for g in range(2):
                for k8 in range(8):
                    n = g * 8 + k8
                    for which, src, dstrep in ((0, bmt, brep), (1, cmt, crep)):
                        bank = (2 * k8 + which) % 2
                        p.wait("tensor", s_sc)
                        E("tensor", lambda e, n=n, bank=bank, src=src: e.matmul(
                            pbc[bank][:, :TC], sel[:, n * 128:(n + 1) * 128], src[:, :],
                            start=True, stop=True).then_inc(s_te.h, 1))
                        s_te.inc()
                        p.wait("scalar", s_te)
                        E("scalar", lambda e, k8=k8, bank=bank, dstrep=dstrep: e.copy(
                            dstrep[:, k8 * TC:(k8 + 1) * TC], pbc[bank][:, :TC]).then_inc(s_sc.h, 1))
                        s_sc.inc()
                rep_ready = s_sc.v

                for dt in range(8):
                    ybank = dt % 2
                    wbuf = dt % 2
                    p.wait("vector", s_ve, xc_ready)
                    p.wait("vector", s_sc, rep_ready)
                    E("vector", lambda e, dt=dt, wbuf=wbuf: e.tensor_mul(
                        wv[wbuf][:, :], delta[:, dt * TC:(dt + 1) * TC], xc[:, dt * TC:(dt + 1) * TC]))
                    for k8 in range(8):
                        n = g * 8 + k8
                        buf = k8 % 2
                        col = dt * 16 + n
                        p.wait("scalar", s_ve)
                        E("scalar", lambda e, dt=dt, col=col, buf=buf: e.activation(
                            ab[buf][:, :], delta[:, dt * TC:(dt + 1) * TC], AF.Exp,
                            scale=a_sb[:, col:col + 1]).then_inc(s_sc.h, 1))
                        s_sc.inc()
                        E("vector", lambda e, k8=k8, buf=buf, wbuf=wbuf: e.tensor_mul(
                            bb[buf][:, :], wv[wbuf][:, :], brep[:, k8 * TC:(k8 + 1) * TC]))
                        p.wait("vector", s_sc)
                        E("vector", lambda e, buf=buf, col=col: e.tensor_tensor_scan(
                            hb[buf][:, :], ab[buf][:, :], bb[buf][:, :], state[:, col:col + 1],
                            ALU.mult, ALU.add))
                        p.wait("vector", s_te)
                        E("vector", lambda e, buf=buf, k8=k8: e.tensor_mul(
                            pb[buf][:, :], crep[:, k8 * TC:(k8 + 1) * TC], hb[buf][:, :]).then_inc(s_ve.h, 1))
                        s_ve.inc()
                        E("vector", lambda e, buf=buf, col=col: e.tensor_copy(
                            state[:, col:col + 1], hb[buf][:, TC - 1:TC]))
                        p.wait("tensor", s_ve)
                        E("tensor", lambda e, buf=buf, ybank=ybank, k8=k8: e.matmul(
                            py[ybank][:, :TC], ident[:, :], pb[buf][:, :],
                            start=(k8 == 0), stop=(k8 == 7)).then_inc(s_te.h, 1))
                        s_te.inc()
                    p.wait("vector", s_te)
                    if g == 0:
                        E("vector", lambda e, dt=dt, ybank=ybank: e.scalar_tensor_tensor(
                            out=y_acc[:, dt * TC:(dt + 1) * TC], in0=xc[:, dt * TC:(dt + 1) * TC],
                            scalar=dd[:, dt:dt + 1], in1=py[ybank][:, :TC],
                            op0=ALU.mult, op1=ALU.add).then_inc(s_ve.h, 1))
                    else:
                        E("vector", lambda e, dt=dt, ybank=ybank: e.tensor_add(
                            y_bf[:, dt * TC:(dt + 1) * TC], y_acc[:, dt * TC:(dt + 1) * TC],
                            py[ybank][:, :TC]).then_inc(s_ve.h, 1))
                    s_ve.inc()
            y_ready = s_ve.v

            if debug and c == 0:
                p.wait("gpsimd", s_ve, y_ready)
                p.wait("gpsimd", s_sc)
                for nm, srcb in (("dbg_delta", delta), ("dbg_xc", xc), ("dbg_xnt", xnT),
                                  ("dbg_ybf", y_bf), ("dbg_brep", brep)):
                    E("gpsimd", lambda e, nm=nm, srcb=srcb: e.dma_start(
                        d2_[nm], srcb[:, :]).then_inc(s_st.h, 16))
                    s_st.inc(16)
                for nm, srcb in (("dbg_bmt", bmt), ("dbg_cmt", cmt)):
                    E("gpsimd", lambda e, nm=nm, srcb=srcb: e.dma_start(
                        d2_[nm], srcb[:, :]).then_inc(s_st.h, 16))
                    s_st.inc(16)

            for oc in range(8):
                bank = oc % 2
                for kc in range(8):
                    if kc == 0:
                        p.wait("tensor", s_ve, y_ready)
                        p.wait("tensor", s_sc)
                        p.wait("tensor", s_st)
                    E("tensor", lambda e, oc=oc, kc=kc, bank=bank, last=(kc == 7): (
                        e.matmul(pp[bank][:, :], w_o[:, kc * 1024 + oc * 128:kc * 1024 + (oc + 1) * 128],
                                 y_bf[:, kc * TC:(kc + 1) * TC], start=(kc == 0), stop=last).then_inc(s_te.h, 1)
                        if last else
                        e.matmul(pp[bank][:, :], w_o[:, kc * 1024 + oc * 128:kc * 1024 + (oc + 1) * 128],
                                 y_bf[:, kc * TC:(kc + 1) * TC], start=(kc == 0), stop=last)))
                s_te.inc()
                p.wait("scalar", s_te)
                E("scalar", lambda e, oc=oc, bank=bank: e.copy(stage[:, (oc % 4) * TC:(oc % 4 + 1) * TC], pp[bank][:, :]).then_inc(s_sc.h, 1))
                s_sc.inc()
                if oc % 4 == 3:
                    p.wait("gpsimd", s_sc)
                    grp = oc // 4
                    dst = part_h[grp * 512:(grp + 1) * 512, c * TC:(c + 1) * TC].rearrange("(oc p) t -> p oc t", p=128)
                    E("gpsimd", lambda e, dst=dst: e.dma_start(dst, stage[:, :].rearrange("p (a b) -> p a b", a=4)).then_inc(s_st.h, 16))
                    s_st.inc(16)

        p.wait("gpsimd", s_st)

        with nc.Block() as block:
            p.attach(block)
    return nc


def l0_attn_inputs(inp, core):
    b, h = core // 2, core % 2
    sl = slice(h * 1024, (h + 1) * 1024)
    x = np.ascontiguousarray(inp["L0_attn_x"][b])
    w_in = inp["L0_attn_W_in"]
    w_in_g = np.ascontiguousarray(w_in[:, sl])
    w_in_i = np.ascontiguousarray(w_in[:, 2048 + h * 1024:2048 + (h + 1) * 1024])
    w_delta = np.ascontiguousarray(inp["L0_attn_W_delta"][:, sl])
    w_out = np.ascontiguousarray(inp["L0_attn_W_out"][sl, :])
    w_b = np.ascontiguousarray(inp["L0_attn_W_B"])
    w_c = np.ascontiguousarray(inp["L0_attn_W_C"])

    def col128(v):
        return np.ascontiguousarray(v.reshape(8, 128).T)

    b_delta = col128(inp["L0_attn_b_delta"][sl])
    conv_b = col128(inp["L0_attn_conv_b"][sl])
    ddv = col128(inp["L0_attn_D"][sl])
    cw = inp["L0_attn_conv_w"][sl, 0, :]
    conv_w = np.ascontiguousarray(cw.reshape(8, 128, 4).transpose(1, 0, 2).reshape(128, 32))
    al = inp["L0_attn_A_log"][sl, :]
    a_log = np.ascontiguousarray(al.reshape(8, 128, 16).transpose(1, 0, 2).reshape(128, 128))
    import ml_dtypes
    selm = np.zeros((16, 16 * 128), np.float32)
    for n in range(16):
        selm[n, n * 128:(n + 1) * 128] = 1.0
    sel = selm.astype(ml_dtypes.bfloat16)
    ident = np.eye(128, dtype=np.float32).astype(ml_dtypes.bfloat16)
    return dict(x=x, w_in_g=w_in_g, w_in_i=w_in_i, w_delta=w_delta, w_out=w_out,
                w_b=w_b, w_c=w_c, b_delta=b_delta, conv_w=conv_w, conv_b=conv_b,
                a_log=a_log, dd=ddv, sel=sel, ident=ident)


def bcast_free(tile_ap, p, reps, inner, col0=0):
    t = tile_ap.tensor if hasattr(tile_ap, "tensor") else tile_ap
    row = t.shape[1] if hasattr(t, "shape") else None
    return bass.AP(t, col0, [[row, p], [0, reps], [1, inner]])


def build_ffn_a(do_ln, resid3, TE=3584, OWN=2048):
    NW = TE // 512
    NT = TE // 128
    nc = bass.Bass("TRN2", target_bir_lowering=False)
    d = lambda *a, **k: nc.dram_tensor(*a, **k).ap()
    xin_h = d("xin", [TE, 1024], F32, kind="ExternalInput")
    if resid3:
        p0_h = d("p0", [TE, 1024], F32, kind="ExternalInput")
        p1_h = d("p1", [TE, 1024], F32, kind="ExternalInput")
        x1out_h = d("x1out", [OWN, 1024], F32, kind="ExternalOutput")
    wqkv_h = d("wqkv", [1024, 1536], F32, kind="ExternalInput")
    wo_h = d("wo", [512, 1024], F32, kind="ExternalInput")
    qsr_h = d("qsr", [128, 64], F32, kind="ExternalInput")
    ksr_h = d("ksr", [128, 64], F32, kind="ExternalInput")
    qsrr_h = d("qsrr", [128, 64], F32, kind="ExternalInput")
    ksrr_h = d("ksrr", [128, 64], F32, kind="ExternalInput")
    csq_h = d("csq", [128, NT * 64], BF16, kind="ExternalInput")
    snq_h = d("snq", [128, NT * 64], BF16, kind="ExternalInput")
    csk_h = d("csk", [128, NT * 64], BF16, kind="ExternalInput")
    snk_h = d("snk", [128, NT * 64], BF16, kind="ExternalInput")
    lmask_h = d("lmask", [128, 8 * 512], BF16, kind="ExternalInput")
    kpad_h = d("kpad", [128, NT], F32, kind="ExternalInput")
    ident_h = d("ident", [128, 128], BF16, kind="ExternalInput")
    za_h = d("za", [1024, 5 * 512], F32, kind="ExternalOutput")

    es = ExitStack()
    sb, ps, sem = mk_ctx(nc, es)
    with es:
        wqkv = sb("s_wqkv", [128, 8 * 1536], BF16)
        wo = sb("s_wo", [128, 4 * 1024], BF16)
        qsr = sb("s_qsr", [128, 64], F32)
        ksr = sb("s_ksr", [128, 64], F32)
        qsrr = sb("s_qsrr", [128, 64], F32)
        ksrr = sb("s_ksrr", [128, 64], F32)
        csq = sb("s_csq", [128, NT * 64], BF16)
        snq = sb("s_snq", [128, NT * 64], BF16)
        csk = sb("s_csk", [128, NT * 64], BF16)
        snk = sb("s_snk", [128, NT * 64], BF16)
        lmask = sb("s_lmask", [128, 8 * 512], BF16)
        kpad = sb("s_kpad", [128, NT], F32)
        kbias = sb("s_kbias", [128, NT], F32)
        ident = sb("s_ident", [128, 128], BF16)
        ncol = sb("ncol", [128, 1], F32)
        mq = sb("mq", [128, 1], F32)
        mk_ = sb("mk_", [128, 1], F32)
        ones64 = sb("ones64", [1, 64], BF16)
        onesc = sb("onesc", [128, 1], BF16)
        stage = sb("stage", [128, 1536], F32)
        cst = sb("cst", [128, 4], F32)
        xx = sb("xx", [128, 1024], F32)
        xx2 = sb("xx2", [128, 1024], F32)
        lnc = sb("lnc", [128, 16], F32)
        lnjunk = sb("lnjunk", [128, 1024], F32)
        mv = sb("mv", [128, 2], F32)
        rstd = sb("rstd", [128, 1], F32)
        zn = sb("zn", [128, 1024], BF16)
        zT = sb("zT", [128, 8 * TE], BF16)
        q_tm = sb("q_tm", [128, 512], F32)
        k_tm = sb("k_tm", [128, 512], F32)
        sq = sb("sq", [128, 512], F32)
        nrm = sb("nrm", [128, 8], F32)
        rs = sb("rs", [128, 8], F32)
        rotb = sb("rotb", [128, 512], F32)
        t1b = sb("t1b", [128, 512], F32)
        qkb = sb("qkb", [128, 512], BF16)
        qT = sb("qT", [128, 4 * 512], BF16)
        kT = sb("kT", [128, 2 * 4 * 512], BF16)
        v_sb = sb("v_sb", [128, 8 * 512], BF16)
        eb = [sb(f"eb{i}", [128, 512], BF16) for i in range(2)]
        ctxf = sb("ctxf", [128, 512], F32)
        rdf = sb("rdf", [1, 512], F32)
        rdb = sb("rdb", [1, 512], BF16)
        ctx = sb("ctx", [128, 4 * 512], BF16)
        za_ring = sb("za_ring", [128, 8 * 512], F32)

        pqk = [ps(f"pqk{i}", [128, 512], F32) for i in range(2)]
        psc = [ps(f"psc{i}", [128, 512], F32) for i in range(2)]
        pav = [ps(f"pav{i}", [128, 512], F32) for i in range(2)]
        pbc = ps("pbc", [128, 512], F32)
        ptp = ps("ptp", [128, 512], BF16)

        s_ld = sem("s_ld")
        s_ld2 = sem("s_ld2")
        s_lda = sem("s_lda")
        s_st = sem("s_st")
        s_st2 = sem("s_st2")
        s_ve = sem("s_ve")
        s_sc = sem("s_sc")
        s_te = sem("s_te")

        p = Prog(nc)
        E = p.emit

        def vgap():
            E("vector", lambda e: e.nop().then_inc(s_ve.h, 1))
            s_ve.inc()
            p.wait("vector", s_ve)

        def load_cast(whbm, wsb, nrows, ncols):
            kcs = nrows // 128
            done = 0
            while done < kcs:
                step = max(1, min(kcs - done, 1536 // ncols))
                srcv = whbm[done * 128:(done + step) * 128, :].rearrange("(kc p) o -> p kc o", p=128)
                p.wait("sync", s_sc)
                E("sync", lambda e, srcv=srcv, step=step, ncols=ncols: e.dma_start(
                    out=stage[:, 0:step * ncols].rearrange("p (a b) -> p a b", a=step), in_=srcv).then_inc(s_ld.h, 16))
                s_ld.inc(16)
                p.wait("scalar", s_ld)
                for j in range(step):
                    E("scalar", lambda e, j=j, done=done, ncols=ncols, wsb=wsb, last=(j == step - 1): (
                        e.copy(wsb[:, (done + j) * ncols:(done + j + 1) * ncols],
                               stage[:, j * ncols:(j + 1) * ncols]).then_inc(s_sc.h, 1)
                        if last else
                        e.copy(wsb[:, (done + j) * ncols:(done + j + 1) * ncols],
                               stage[:, j * ncols:(j + 1) * ncols])))
                s_sc.inc()
                done += step

        load_cast(wqkv_h, wqkv, 1024, 1536)
        load_cast(wo_h, wo, 512, 1024)

        def simple_load(src, dst):
            E("sync", lambda e, s=src, dst=dst: e.dma_start(out=dst, in_=s).then_inc(s_ld2.h, 16))
            s_ld2.inc(16)

        for srch, dst in ((qsr_h, qsr), (ksr_h, ksr), (qsrr_h, qsrr), (ksrr_h, ksrr), (csq_h, csq), (snq_h, snq), (csk_h, csk),
                          (snk_h, snk), (lmask_h, lmask), (kpad_h, kpad), (ident_h, ident)):
            simple_load(srch, dst[:, :])
        p.wait("vector", s_ld)
        p.wait("vector", s_ld2)
        p.wait("scalar", s_ld2)
        p.wait("tensor", s_ld2)
        E("vector", lambda e: e.memset(cst[:, 0:1], 1e-5))
        E("vector", lambda e: e.memset(cst[:, 1:2], 1e-18))
        E("vector", lambda e: e.memset(cst[:, 2:3], 1e-19))
        E("vector", lambda e: e.memset(ones64[:, :], 1.0))
        E("vector", lambda e: e.memset(onesc[:, :], 1.0).then_inc(s_ve.h, 1))
        s_ve.inc()
        p.wait("vector", s_ve)
        E("vector", lambda e: e.tensor_reduce(mq[:, :], qsr[:, :], mybir.AxisListType.X, ALU.max, apply_absolute_value=True).then_inc(s_ve.h, 1))
        s_ve.inc()
        E("vector", lambda e: e.tensor_reduce(mk_[:, :], ksr[:, :], mybir.AxisListType.X, ALU.max, apply_absolute_value=True).then_inc(s_ve.h, 1))
        s_ve.inc()
        p.wait("vector", s_ve)
        E("vector", lambda e: e.tensor_scalar(out=ncol[:, :], in0=mq[:, :], scalar1=mk_[:, 0:1],
                                              scalar2=-8.0, op0=ALU.mult, op1=ALU.mult).then_inc(s_ve.h, 1))
        s_ve.inc()
        p.wait("vector", s_ve)
        E("vector", lambda e: e.tensor_scalar(out=kbias[:, :], in0=kpad[:, :], scalar1=ncol[:, 0:1],
                                              scalar2=None, op0=ALU.add, op1=ALU.bypass).then_inc(s_ve.h, 1))
        s_ve.inc()
        E("vector", lambda e: e.tensor_tensor(csq[:, :].rearrange("p (t d) -> p t d", t=NT), csq[:, :].rearrange("p (t d) -> p t d", t=NT), bcast_free(qsr, 128, NT, 64), ALU.mult))
        E("vector", lambda e: e.tensor_tensor(snq[:, :].rearrange("p (t d) -> p t d", t=NT), snq[:, :].rearrange("p (t d) -> p t d", t=NT), bcast_free(qsrr, 128, NT, 64), ALU.mult))
        E("vector", lambda e: e.tensor_tensor(csk[:, :].rearrange("p (t d) -> p t d", t=NT), csk[:, :].rearrange("p (t d) -> p t d", t=NT), bcast_free(ksr, 128, NT, 64), ALU.mult))
        E("vector", lambda e: e.tensor_tensor(snk[:, :].rearrange("p (t d) -> p t d", t=NT), snk[:, :].rearrange("p (t d) -> p t d", t=NT), bcast_free(ksrr, 128, NT, 64), ALU.mult).then_inc(s_ve.h, 1))
        s_ve.inc()
        p.wait("vector", s_ve)

        for i in range(NT):
            p.wait("sync", s_ve)
            p.wait("sync", s_te)
            p.wait("sync", s_st)
            p.wait("sync", s_sc)
            E("sync", lambda e, i=i: e.dma_start(out=xx[:, :], in_=xin_h[i * 128:(i + 1) * 128, :]).then_inc(s_lda.h, 16))
            s_lda.inc(16)
            if resid3:
                E("sync", lambda e, i=i: e.dma_start(out=xx2[:, :], in_=p0_h[i * 128:(i + 1) * 128, :]).then_inc(s_lda.h, 16))
                s_lda.inc(16)
                p.wait("vector", s_lda)
                E("vector", lambda e: e.tensor_add(xx[:, :], xx[:, :], xx2[:, :]).then_inc(s_ve.h, 1))
                s_ve.inc()
                p.wait("sync", s_ve)
                E("sync", lambda e, i=i: e.dma_start(out=xx2[:, :], in_=p1_h[i * 128:(i + 1) * 128, :]).then_inc(s_lda.h, 16))
                s_lda.inc(16)
                p.wait("vector", s_lda)
                E("vector", lambda e: e.tensor_add(xx[:, :], xx[:, :], xx2[:, :]).then_inc(s_ve.h, 1))
                s_ve.inc()
                if i >= NT - OWN // 128:
                    io = i - (NT - OWN // 128)
                    p.wait("gpsimd", s_ve)
                    E("gpsimd", lambda e, io=io: e.dma_start(x1out_h[io * 128:(io + 1) * 128, :], xx[:, :]).then_inc(s_st.h, 16))
                    s_st.inc(16)
                    p.wait("gpsimd", s_st)
            p.wait("vector", s_lda)
            p.wait("vector", s_te)
            p.wait("vector", s_sc)
            if do_ln:
                p.wait("scalar", s_lda)
                p.wait("scalar", s_ve)
                mean_ap, rstd_ap = ln_core(p, E, s_ve, s_sc, [(xx[:, :], 1024)], lnjunk, lnc, cst[:, 0:1])
                p.wait("vector", s_sc)
                E("vector", lambda e, mean_ap=mean_ap, rstd_ap=rstd_ap: e.tensor_scalar(
                    out=zn[:, :], in0=xx[:, :], scalar1=mean_ap,
                    scalar2=rstd_ap, op0=ALU.subtract, op1=ALU.mult).then_inc(s_ve.h, 1))
                s_ve.inc()
            else:
                E("vector", lambda e: e.tensor_copy(zn[:, :], xx[:, :]).then_inc(s_ve.h, 1))
                s_ve.inc()
            p.wait("tensor", s_ve)
            for k in range(8):
                slot = k % 4
                p.wait("tensor", s_sc)
                E("tensor", lambda e, k=k, slot=slot: e.transpose(
                    ptp[:, slot * 128:(slot + 1) * 128], zn[:, k * 128:(k + 1) * 128],
                    ident[:, :]).then_inc(s_te.h, 1))
                s_te.inc()
                p.wait("scalar", s_te)
                E("scalar", lambda e, k=k, slot=slot, i=i: e.copy(
                    zT[:, k * TE + i * 128:k * TE + (i + 1) * 128],
                    ptp[:, slot * 128:(slot + 1) * 128]).then_inc(s_sc.h, 1))
                s_sc.inc()
        zT_ready = s_sc.v

        def qkv_mm(ext_tile, dg, bank):
            for kc in range(8):
                if kc == 0:
                    p.wait("tensor", s_sc, zT_ready)
                    p.wait("tensor", s_ve)
                    p.wait("tensor", s_sc)
                off = -1 if kc >= 4 else 0
                col = kc * TE + ext_tile * 128 + off
                E("tensor", lambda e, kc=kc, col=col, dg=dg, bank=bank, last=(kc == 7): (
                    e.matmul(pqk[bank][:, :], zT[:, col:col + 128],
                             wqkv[:, kc * 1536 + dg * 512:kc * 1536 + (dg + 1) * 512],
                             start=(kc == 0), stop=last).then_inc(s_te.h, 1)
                    if last else
                    e.matmul(pqk[bank][:, :], zT[:, col:col + 128],
                             wqkv[:, kc * 1536 + dg * 512:kc * 1536 + (dg + 1) * 512],
                             start=(kc == 0), stop=last)))
            s_te.inc()

        def l2norm_rotary(srcbuf, tile_idx, cs_t, sn_t):
            p.wait("scalar", s_ve)
            E("scalar", lambda e, srcbuf=srcbuf: e.activation(sq[:, :], srcbuf[:, :], AF.Square).then_inc(s_sc.h, 1))
            s_sc.inc()
            p.wait("vector", s_sc)
            E("vector", lambda e: e.tensor_reduce(nrm[:, :], sq[:, :].rearrange("p (h d) -> p h d", h=8), mybir.AxisListType.X, ALU.add).then_inc(s_ve.h, 1))
            s_ve.inc()
            p.wait("scalar", s_ve)
            E("scalar", lambda e: e.activation(rs[:, :], nrm[:, :], AF.Ln, bias=cst[:, 1:2]))
            E("scalar", lambda e: e.activation(rs[:, :], rs[:, :], AF.Exp, scale=-0.5).then_inc(s_sc.h, 1))
            s_sc.inc()
            p.wait("vector", s_sc)
            E("vector", lambda e, srcbuf=srcbuf: e.tensor_tensor(
                sq[:, :].rearrange("p (h d) -> p h d", h=8),
                srcbuf[:, :].rearrange("p (h d) -> p h d", h=8),
                bass.AP(rs.tensor if hasattr(rs, "tensor") else rs, 0, [[8, 128], [1, 8], [0, 64]]),
                ALU.mult))
            E("vector", lambda e: e.tensor_scalar(
                out=rotb[:, :].rearrange("p (h d) -> p h d", h=8)[:, :, 0:32],
                in0=sq[:, :].rearrange("p (h d) -> p h d", h=8)[:, :, 32:64],
                scalar1=-1.0, scalar2=None, op0=ALU.mult, op1=ALU.bypass))
            E("vector", lambda e: e.tensor_scalar(
                out=rotb[:, :].rearrange("p (h d) -> p h d", h=8)[:, :, 32:64],
                in0=sq[:, :].rearrange("p (h d) -> p h d", h=8)[:, :, 0:32],
                scalar1=1.0, scalar2=None, op0=ALU.mult, op1=ALU.bypass))
            E("vector", lambda e, cs_t=cs_t, tile_idx=tile_idx: e.tensor_tensor(
                t1b[:, :].rearrange("p (h d) -> p h d", h=8),
                sq[:, :].rearrange("p (h d) -> p h d", h=8),
                bass.AP(cs_t.tensor if hasattr(cs_t, "tensor") else cs_t, tile_idx * 64, [[NT * 64, 128], [0, 8], [1, 64]]),
                ALU.mult))
            E("vector", lambda e, sn_t=sn_t, tile_idx=tile_idx: e.tensor_tensor(
                rotb[:, :].rearrange("p (h d) -> p h d", h=8),
                rotb[:, :].rearrange("p (h d) -> p h d", h=8),
                bass.AP(sn_t.tensor if hasattr(sn_t, "tensor") else sn_t, tile_idx * 64, [[NT * 64, 128], [0, 8], [1, 64]]),
                ALU.mult))
            E("vector", lambda e: e.tensor_add(sq[:, :], t1b[:, :], rotb[:, :]).then_inc(s_ve.h, 1))
            s_ve.inc()

        def cast_transpose(dst, dstcol_fn, j):
            p.wait("scalar", s_ve)
            p.wait("scalar", s_te)
            E("scalar", lambda e: e.copy(qkb[:, :], sq[:, :]).then_inc(s_sc.h, 1))
            s_sc.inc()
            for cb in range(4):
                slot = cb
                p.wait("tensor", s_sc)
                E("tensor", lambda e, cb=cb, slot=slot: e.transpose(
                    ptp[:, slot * 128:(slot + 1) * 128], qkb[:, cb * 128:(cb + 1) * 128],
                    ident[:, :]).then_inc(s_te.h, 1))
                s_te.inc()
                p.wait("scalar", s_te)
                dcol = dstcol_fn(cb, j)
                E("scalar", lambda e, dcol=dcol, slot=slot, dst=dst: e.copy(
                    dst[:, dcol:dcol + 128], ptp[:, slot * 128:(slot + 1) * 128]).then_inc(s_sc.h, 1))
                s_sc.inc()

        for w in range(1, NW):
            for j in range(4):
                ext_tile = w * 4 + j
                qkv_mm(ext_tile, 2, 0)
                vslot = ext_tile % 8
                p.wait("scalar", s_te)
                E("scalar", lambda e, vslot=vslot: e.copy(v_sb[:, vslot * 512:(vslot + 1) * 512], pqk[0][:, :]).then_inc(s_sc.h, 1))
                s_sc.inc()
                qkv_mm(ext_tile, 1, 1)
                p.wait("scalar", s_te)
                p.wait("scalar", s_ve)
                E("scalar", lambda e: e.copy(k_tm[:, :], pqk[1][:, :]).then_inc(s_sc.h, 1))
                s_sc.inc()
                p.wait("vector", s_sc)
                l2norm_rotary(k_tm, ext_tile, csk, snk)
                cast_transpose(kT, lambda cb, j=j, w=w: (w % 2) * 2048 + cb * 512 + j * 128, j)
                if w >= 2:
                    qkv_mm(ext_tile, 0, 0)
                    p.wait("scalar", s_te)
                    p.wait("scalar", s_ve)
                    E("scalar", lambda e: e.copy(q_tm[:, :], pqk[0][:, :]).then_inc(s_sc.h, 1))
                    s_sc.inc()
                    p.wait("vector", s_sc)
                    l2norm_rotary(q_tm, ext_tile, csq, snq)
                    cast_transpose(qT, lambda cb, j=j: cb * 512 + j * 128, j)
            if w < 2:
                continue
            kvready = s_sc.v

            for hp in range(4):
                for half in range(2):
                    pavb = pav[half]
                    head = 2 * hp + half
                    rowbase = 64 * half
                    for kt in range(8):
                        wk = w - 1 + kt // 4
                        jk = kt % 4
                        ktile_ext = (w - 1) * 4 + kt
                        kcol = (wk % 2) * 2048 + hp * 512 + jk * 128
                        ebuf = eb[kt % 2]
                        scb = psc[kt % 2]
                        p.wait("tensor", s_sc, kvready)
                        p.wait("tensor", s_sc)
                        p.wait("tensor", s_ve)
                        E("tensor", lambda e, kcol=kcol, rowbase=rowbase, scb=scb, hp=hp: e.matmul(
                            scb[:, :], kT[rowbase:rowbase + 64, kcol:kcol + 128],
                            qT[rowbase:rowbase + 64, hp * 512:(hp + 1) * 512],
                            start=True, stop=True).then_inc(s_te.h, 1))
                        s_te.inc()
                        p.wait("vector", s_te)
                        E("vector", lambda e, kt=kt, scb=scb: e.tensor_tensor(
                            scb[:, :], scb[:, :], lmask[:, kt * 512:(kt + 1) * 512], ALU.add).then_inc(s_ve.h, 1))
                        s_ve.inc()
                        p.wait("scalar", s_ve)
                        p.wait("scalar", s_te)
                        E("scalar", lambda e, ebuf=ebuf, scb=scb, ktile_ext=ktile_ext: e.activation(
                            ebuf[:, :], scb[:, :], AF.Exp, scale=8.0, bias=kbias[:, ktile_ext:ktile_ext + 1]).then_inc(s_sc.h, 1))
                        s_sc.inc()
                        p.wait("tensor", s_sc)
                        vslot = ktile_ext % 8
                        E("tensor", lambda e, vslot=vslot, head=head, ebuf=ebuf, pavb=pavb, kt=kt: e.matmul(
                            pavb[0:64, :], v_sb[:, vslot * 512 + head * 64:vslot * 512 + head * 64 + 64],
                            ebuf[:, :], start=(kt == 0), stop=(kt == 7)))
                        E("tensor", lambda e, ebuf=ebuf, kt=kt: e.matmul(
                            pbc[0:1, :], onesc[:, :], ebuf[:, :], start=(kt == 0), stop=(kt == 7)).then_inc(s_te.h, 1))
                        s_te.inc()
                    p.wait("scalar", s_te)
                    p.wait("scalar", s_ve)
                    E("scalar", lambda e, pavb=pavb: e.copy(ctxf[0:64, :], pavb[0:64, :]))
                    E("scalar", lambda e: e.activation(rdf[:, :], pbc[0:1, :], AF.Ln, bias=cst[0:1, 2:3]))
                    E("scalar", lambda e: e.activation(rdb[:, :], rdf[:, :], AF.Exp, scale=-1.0).then_inc(s_sc.h, 1))
                    s_sc.inc()
                    p.wait("tensor", s_sc)
                    p.wait("tensor", s_ve)
                    E("tensor", lambda e: e.matmul(pbc[0:64, :], ones64[:, :], rdb[:, :], start=True, stop=True).then_inc(s_te.h, 1))
                    s_te.inc()
                    p.wait("vector", s_te)
                    E("vector", lambda e, hp=hp, half=half: e.tensor_mul(
                        ctx[64 * half:64 * half + 64, hp * 512:(hp + 1) * 512],
                        ctxf[0:64, :], pbc[0:64, :]).then_inc(s_ve.h, 1))
                    s_ve.inc()
            ctx_ready = s_ve.v

            for oc in range(8):
                bank = oc % 2
                for kc in range(4):
                    if kc == 0:
                        p.wait("tensor", s_ve, ctx_ready)
                        p.wait("tensor", s_ve)
                        p.wait("tensor", s_sc)
                    E("tensor", lambda e, oc=oc, kc=kc, bank=bank, last=(kc == 3): (
                        e.matmul(pqk[bank][:, :], wo[:, kc * 1024 + oc * 128:kc * 1024 + (oc + 1) * 128],
                                 ctx[:, kc * 512:(kc + 1) * 512], start=(kc == 0), stop=last).then_inc(s_te.h, 1)
                        if last else
                        e.matmul(pqk[bank][:, :], wo[:, kc * 1024 + oc * 128:kc * 1024 + (oc + 1) * 128],
                                 ctx[:, kc * 512:(kc + 1) * 512], start=(kc == 0), stop=last)))
                s_te.inc()
                p.wait("vector", s_te)
                p.wait("vector", s_st)
                E("vector", lambda e, oc=oc, bank=bank, w=w: e.tensor_add(
                    za_ring[:, oc * 512:(oc + 1) * 512],
                    pqk[bank][:, :],
                    zT[:, oc * TE + w * 512:oc * TE + (w + 1) * 512]).then_inc(s_ve.h, 1))
                s_ve.inc()
            p.wait("gpsimd", s_ve)
            dst = za_h[:, (w - 2) * 512:(w - 1) * 512].rearrange("(oc p) t -> p oc t", p=128)
            E("gpsimd", lambda e, dst=dst: e.dma_start(
                dst, za_ring[:, :].rearrange("p (oc c) -> p oc c", oc=8)).then_inc(s_st.h, 16))
            s_st.inc(16)
            p.wait("gpsimd", s_st)

        p.wait("gpsimd", s_st)
        with nc.Block() as block:
            p.attach(block)
    return nc


def rotary_tables(TE, estart, NT=None):
    import ml_dtypes
    NT = TE // 128
    pos = np.maximum(np.arange(estart, estart + TE), 0).astype(np.float64)
    half = np.arange(0, 64, 2, dtype=np.float64)
    inv_freq = 1.0 / 10000.0 ** (half / 64.0)
    freqs = pos[:, None] * inv_freq[None, :]
    base = (half + 0.4 * 64) / (1.4 * 64)
    power = (pos - 2048.0) / 256.0
    scale = base[None, :] ** power[:, None]
    cos = np.cos(freqs); sin = np.sin(freqs)
    def dup(a):
        return np.concatenate([a, a], axis=1)
    csq = dup(cos * scale); snq = dup(sin * scale)
    csk = dup(cos / scale); snk = dup(sin / scale)
    def tiles(a):
        return np.ascontiguousarray(a.reshape(NT, 128, 64).transpose(1, 0, 2).reshape(128, NT * 64).astype(np.float32)).astype(ml_dtypes.bfloat16)
    return tiles(csq), tiles(snq), tiles(csk), tiles(snk)


def ffn_a_inputs(inp, ri_or_none, core, layer, TE=3584, OWN=2048):
    import ml_dtypes
    b, h = core // 2, core % 2
    NT = TE // 128
    estart = h * OWN - 1536
    out = {}
    def ext_rows(full):
        r = np.zeros((TE, 1024), np.float32)
        lo = max(estart, 0)
        r[lo - estart:, :] = full[lo:estart + TE, :]
        return r
    if layer == 0:
        out["xin"] = ext_rows(np.asarray(inp["L0_attn_x"][b]))
        out["p0"] = ext_rows(np.ascontiguousarray(inp["_part0"][b]))
        out["p1"] = ext_rows(np.ascontiguousarray(inp["_part1"][b]))
    else:
        out["xin"] = ext_rows(np.ascontiguousarray(inp["_zf0"][b]))
    out["wqkv"] = np.ascontiguousarray(inp["L0_ffn_wqkv"][layer])
    out["wo"] = np.ascontiguousarray(inp["L0_ffn_wo"][layer])
    out["qsr"] = np.ascontiguousarray(np.tile(inp["L0_ffn_q_scale"][layer][None, :], (128, 1)))
    out["ksr"] = np.ascontiguousarray(np.tile(inp["L0_ffn_k_scale"][layer][None, :], (128, 1)))
    qsrot = np.concatenate([inp["L0_ffn_q_scale"][layer][32:], inp["L0_ffn_q_scale"][layer][:32]])
    ksrot = np.concatenate([inp["L0_ffn_k_scale"][layer][32:], inp["L0_ffn_k_scale"][layer][:32]])
    out["qsrr"] = np.ascontiguousarray(np.tile(qsrot[None, :], (128, 1)))
    out["ksrr"] = np.ascontiguousarray(np.tile(ksrot[None, :], (128, 1)))
    csq, snq, csk, snk = rotary_tables(TE, estart)
    out["csq"], out["snq"], out["csk"], out["snk"] = csq, snq, csk, snk
    lm = np.zeros((8, 128, 512), np.float32)
    for kt in range(8):
        i = np.arange(128)[:, None]
        j = np.arange(512)[None, :]
        dd = 512 - kt * 128 - i + j
        lm[kt] = np.where((dd >= 0) & (dd <= 512), 0.0, -30000.0)
    out["lmask"] = np.ascontiguousarray(lm.transpose(1, 0, 2).reshape(128, 8 * 512)).astype(ml_dtypes.bfloat16)
    kp = np.zeros((NT, 128), np.float32)
    npad = max(0, -estart)
    for t in range(NT):
        toks = np.arange(t * 128, t * 128 + 128)
        kp[t] = np.where(toks < npad, -30000.0, 0.0)
    out["kpad"] = np.ascontiguousarray(kp.T)
    out["ident"] = np.eye(128, dtype=np.float32).astype(ml_dtypes.bfloat16)
    return out


def build_ffn_b(final_resid, OWN=2048):
    HID2 = 5460
    HID = 2730
    ACH = [(i * 128, min(128, HID - i * 128)) for i in range((HID + 127) // 128)]
    NCH = OWN // 512
    nc = bass.Bass("TRN2", target_bir_lowering=False)
    d = lambda *a, **k: nc.dram_tensor(*a, **k).ap()
    za_h = d("zain", [OWN + 1, 1024], F32, kind="ExternalInput")
    w1_h = d("w1", [1024, HID2], F32, kind="ExternalInput")
    w2_h = d("w2", [HID, 1024], F32, kind="ExternalInput")
    b1_h = d("b1c", [128, 44], F32, kind="ExternalInput")
    b2_h = d("b2c", [128, 8], F32, kind="ExternalInput")
    lng_h = d("lngc", [128, 8], F32, kind="ExternalInput")
    lnb_h = d("lnbc", [128, 8], F32, kind="ExternalInput")
    ident_h = d("ident", [128, 128], BF16, kind="ExternalInput")
    identf_h = d("identf", [128, 128], F32, kind="ExternalInput")
    if final_resid:
        x1_h = d("x1in", [OWN, 1024], F32, kind="ExternalInput")
    zf_h = d("zf", [1024, OWN], F32, kind="ExternalOutput")


    es = ExitStack()
    sb, ps, sem = mk_ctx(nc, es)
    with es:
        w1 = sb("s_w1", [128, 8 * HID2], BF16)
        w2 = sb("s_w2", [128, 22 * 1024], BF16)
        b1c = sb("s_b1c", [128, 44], F32)
        b2c = sb("s_b2c", [128, 8], F32)
        lngc = sb("s_lngc", [128, 8], F32)
        lnbc = sb("s_lnbc", [128, 8], F32)
        ident = sb("s_ident", [128, 128], BF16)
        identf = sb("s_identf", [128, 128], F32)
        stage = sb("stage", [128, 1408], F32)
        cst = sb("cst", [128, 2], F32)
        za_t = sb("za_t", [128, 1024], F32)
        zsh_t = sb("zsh_t", [128, 1024], F32)
        x1_t = sb("x1_t", [128, 1024], F32)
        zn = sb("zn", [128, 1024], BF16)
        lnjunk = sb("lnjunk", [128, 1024], F32)
        lnc = sb("lnc", [128, 16], F32)
        mv = sb("mv", [128, 4], F32)
        rstd = sb("rstd", [128, 1], F32)
        znT = sb("znT", [128, 8 * 512], BF16)
        zaT = sb("zaT", [128, 8 * 512], F32)
        gl = sb("gl", [128, 512], F32)
        ag = sb("ag", [128, 22 * 512], BF16)
        outb = [sb(f"outb{i}", [128, 512], F32) for i in range(2)]

        pa = ps("pa", [128, 512], F32)
        pg = ps("pg", [128, 512], F32)
        po = [ps(f"po{i}", [128, 512], F32) for i in range(2)]
        ptp = ps("ptp", [128, 512], BF16)
        ptpf = ps("ptpf", [128, 512], F32)

        s_ld = sem("s_ld")
        s_ld2 = sem("s_ld2")
        s_lda = sem("s_lda")
        s_st = sem("s_st")
        s_ve = sem("s_ve")
        s_sc = sem("s_sc")
        s_te = sem("s_te")

        p = Prog(nc)
        E = p.emit

        def load_cast2(whbm, wsb, rowchunks, ncols, colchunk, wsb_rowstride):
            for idx, (r0, nr) in enumerate(rowchunks):
                c0 = 0
                while c0 < ncols:
                    cw = min(colchunk, ncols - c0)
                    p.wait("sync", s_sc)
                    E("sync", lambda e, r0=r0, nr=nr, c0=c0, cw=cw: e.dma_start(
                        out=stage[0:nr, 0:cw], in_=whbm[r0:r0 + nr, c0:c0 + cw]).then_inc(s_ld.h, 16))
                    s_ld.inc(16)
                    p.wait("scalar", s_ld)
                    E("scalar", lambda e, idx=idx, nr=nr, c0=c0, cw=cw, wsb=wsb, stride=wsb_rowstride: e.copy(
                        wsb[0:nr, idx * stride + c0:idx * stride + c0 + cw], stage[0:nr, 0:cw]).then_inc(s_sc.h, 1))
                    s_sc.inc()
                    c0 += cw

        load_cast2(w1_h, w1, [(i * 128, 128) for i in range(8)], HID2, 1408, HID2)
        load_cast2(w2_h, w2, ACH, 1024, 1024, 1024)

        def simple_load(src, dst):
            E("sync", lambda e, s=src, dst=dst: e.dma_start(out=dst, in_=s).then_inc(s_ld2.h, 16))
            s_ld2.inc(16)

        for srch, dst in ((b1_h, b1c), (b2_h, b2c), (lng_h, lngc), (lnb_h, lnbc), (ident_h, ident), (identf_h, identf)):
            simple_load(srch, dst[:, :])
        p.wait("vector", s_ld2)
        p.wait("scalar", s_ld2)
        p.wait("tensor", s_ld2)
        p.wait("tensor", s_ld)
        E("vector", lambda e: e.memset(cst[:, 0:1], 1e-5))
        E("vector", lambda e: e.memset(ag[:, :], 0.0).then_inc(s_ve.h, 1))
        s_ve.inc()
        p.wait("vector", s_ve)

        for tc in range(NCH):
            for i in range(4):
                r0 = tc * 512 + i * 128
                p.wait("sync", s_ve)
                p.wait("sync", s_te)
                p.wait("sync", s_sc)
                E("sync", lambda e, r0=r0: e.dma_start(out=za_t[:, :], in_=za_h[r0 + 1:r0 + 129, :]).then_inc(s_lda.h, 16))
                s_lda.inc(16)
                E("sync", lambda e, r0=r0: e.dma_start(out=zsh_t[:, :], in_=za_h[r0:r0 + 128, :]).then_inc(s_lda.h, 16))
                s_lda.inc(16)
                if final_resid:
                    E("sync", lambda e, r0=r0: e.dma_start(out=x1_t[:, :], in_=x1_h[r0:r0 + 128, :]).then_inc(s_lda.h, 16))
                    s_lda.inc(16)
                p.wait("vector", s_lda)
                p.wait("vector", s_sc)
                p.wait("vector", s_te)
                p.wait("scalar", s_lda)
                p.wait("scalar", s_ve)
                mean_ap, rstd_ap = ln_core(p, E, s_ve, s_sc,
                                           [(za_t[:, 0:512], 512), (zsh_t[:, 512:1024], 512)],
                                           lnjunk, lnc, cst[:, 0:1])
                p.wait("vector", s_sc)
                E("vector", lambda e, mean_ap=mean_ap, rstd_ap=rstd_ap: e.tensor_scalar(
                    out=zn[:, 0:512], in0=za_t[:, 0:512], scalar1=mean_ap,
                    scalar2=rstd_ap, op0=ALU.subtract, op1=ALU.mult))
                E("vector", lambda e, mean_ap=mean_ap, rstd_ap=rstd_ap: e.tensor_scalar(
                    out=zn[:, 512:1024], in0=zsh_t[:, 512:1024], scalar1=mean_ap,
                    scalar2=rstd_ap, op0=ALU.subtract, op1=ALU.mult).then_inc(s_ve.h, 1))
                s_ve.inc()
                if final_resid:
                    E("vector", lambda e: e.tensor_add(za_t[:, :], za_t[:, :], x1_t[:, :]).then_inc(s_ve.h, 1))
                    s_ve.inc()
                p.wait("tensor", s_ve)
                for k in range(8):
                    slot = k % 4
                    p.wait("tensor", s_sc)
                    E("tensor", lambda e, k=k, slot=slot: e.transpose(
                        ptp[:, slot * 128:(slot + 1) * 128], zn[:, k * 128:(k + 1) * 128],
                        ident[:, :]).then_inc(s_te.h, 1))
                    s_te.inc()
                    p.wait("scalar", s_te)
                    E("scalar", lambda e, k=k, slot=slot, i=i: e.copy(
                        znT[:, k * 512 + i * 128:k * 512 + (i + 1) * 128],
                        ptp[:, slot * 128:(slot + 1) * 128]).then_inc(s_sc.h, 1))
                    s_sc.inc()
                for k in range(8):
                    slot = k % 4
                    p.wait("tensor", s_sc)
                    E("tensor", lambda e, k=k, slot=slot: e.transpose(
                        ptpf[:, slot * 128:(slot + 1) * 128], za_t[:, k * 128:(k + 1) * 128],
                        identf[:, :]).then_inc(s_te.h, 1))
                    s_te.inc()
                    p.wait("scalar", s_te)
                    E("scalar", lambda e, k=k, slot=slot, i=i: e.copy(
                        zaT[:, k * 512 + i * 128:k * 512 + (i + 1) * 128],
                        ptpf[:, slot * 128:(slot + 1) * 128]).then_inc(s_sc.h, 1))
                    s_sc.inc()
            p.wait("vector", s_sc)
            p.wait("vector", s_te)
            for kc in range(8):
                E("vector", lambda e, kc=kc, last=(kc == 7): (
                    e.tensor_scalar(out=znT[:, kc * 512:(kc + 1) * 512], in0=znT[:, kc * 512:(kc + 1) * 512],
                                    scalar1=lngc[:, kc:kc + 1], scalar2=lnbc[:, kc:kc + 1],
                                    op0=ALU.mult, op1=ALU.add).then_inc(s_ve.h, 1)
                    if last else
                    e.tensor_scalar(out=znT[:, kc * 512:(kc + 1) * 512], in0=znT[:, kc * 512:(kc + 1) * 512],
                                    scalar1=lngc[:, kc:kc + 1], scalar2=lnbc[:, kc:kc + 1],
                                    op0=ALU.mult, op1=ALU.add)))
            s_ve.inc()
            znT_ready_v = s_ve.v
            znT_ready = s_sc.v

            for idx, (c0, cw) in enumerate(ACH):
                for kc in range(8):
                    if kc == 0:
                        p.wait("tensor", s_sc, znT_ready)
                        p.wait("tensor", s_ve, znT_ready_v)
                        p.wait("tensor", s_sc)
                        p.wait("tensor", s_ve)
                    E("tensor", lambda e, kc=kc, c0=c0, cw=cw, last=(kc == 7): (
                        e.matmul(pg[0:cw, :], w1[:, kc * HID2 + HID + c0:kc * HID2 + HID + c0 + cw],
                                 znT[:, kc * 512:(kc + 1) * 512], start=(kc == 0), stop=last).then_inc(s_te.h, 1)
                        if last else
                        e.matmul(pg[0:cw, :], w1[:, kc * HID2 + HID + c0:kc * HID2 + HID + c0 + cw],
                                 znT[:, kc * 512:(kc + 1) * 512], start=(kc == 0), stop=last)))
                s_te.inc()
                p.wait("scalar", s_te)
                p.wait("scalar", s_ve)
                E("scalar", lambda e, idx=idx, cw=cw: e.activation(
                    gl[0:cw, :], pg[0:cw, :], AF.Gelu, bias=b1c[0:cw, 22 + idx:22 + idx + 1]).then_inc(s_sc.h, 1))
                s_sc.inc()
                for kc in range(8):
                    if kc == 0:
                        p.wait("tensor", s_ve)
                    E("tensor", lambda e, kc=kc, c0=c0, cw=cw, last=(kc == 7): (
                        e.matmul(pa[0:cw, :], w1[:, kc * HID2 + c0:kc * HID2 + c0 + cw],
                                 znT[:, kc * 512:(kc + 1) * 512], start=(kc == 0), stop=last).then_inc(s_te.h, 1)
                        if last else
                        e.matmul(pa[0:cw, :], w1[:, kc * HID2 + c0:kc * HID2 + c0 + cw],
                                 znT[:, kc * 512:(kc + 1) * 512], start=(kc == 0), stop=last)))
                s_te.inc()
                p.wait("vector", s_te)
                p.wait("vector", s_sc)
                E("vector", lambda e, idx=idx, cw=cw: e.scalar_tensor_tensor(
                    out=ag[0:cw, idx * 512:(idx + 1) * 512], in0=pa[0:cw, :],
                    scalar=b1c[0:cw, idx:idx + 1], in1=gl[0:cw, :],
                    op0=ALU.add, op1=ALU.mult).then_inc(s_ve.h, 1))
                s_ve.inc()
            ag_ready = s_ve.v

            for oc in range(8):
                bank = oc % 2
                for idx, (c0, cw) in enumerate(ACH):
                    if idx == 0:
                        p.wait("tensor", s_ve, ag_ready)
                        p.wait("tensor", s_ve)
                        p.wait("tensor", s_st)
                    E("tensor", lambda e, idx=idx, cw=cw, oc=oc, bank=bank, last=(idx == len(ACH) - 1): (
                        e.matmul(po[bank][:, :], w2[0:cw, idx * 1024 + oc * 128:idx * 1024 + (oc + 1) * 128],
                                 ag[0:cw, idx * 512:(idx + 1) * 512], start=(idx == 0), stop=last).then_inc(s_te.h, 1)
                        if last else
                        e.matmul(po[bank][:, :], w2[0:cw, idx * 1024 + oc * 128:idx * 1024 + (oc + 1) * 128],
                                 ag[0:cw, idx * 512:(idx + 1) * 512], start=(idx == 0), stop=last)))
                s_te.inc()
                p.wait("vector", s_te)
                p.wait("vector", s_st)
                E("vector", lambda e, oc=oc, bank=bank: e.scalar_tensor_tensor(
                    out=outb[bank][:, :], in0=po[bank][:, :], scalar=b2c[:, oc:oc + 1],
                    in1=zaT[:, oc * 512:(oc + 1) * 512], op0=ALU.add, op1=ALU.add).then_inc(s_ve.h, 1))
                s_ve.inc()
                p.wait("gpsimd", s_ve)
                E("gpsimd", lambda e, oc=oc, bank=bank, tc=tc: e.dma_start(
                    zf_h[oc * 128:(oc + 1) * 128, tc * 512:(tc + 1) * 512], outb[bank][:, :]).then_inc(s_st.h, 16))
                s_st.inc(16)
                p.wait("gpsimd", s_st)

        p.wait("gpsimd", s_st)
        with nc.Block() as block:
            p.attach(block)
    return nc


def ffn_b_inputs(inp, core, layer, final_resid, OWN=2048):
    import ml_dtypes
    b, h = core // 2, core % 2
    s = h * OWN
    out = {}
    za_full = np.ascontiguousarray(inp["_za%d" % layer][b])
    rows = np.zeros((OWN + 1, 1024), np.float32)
    lo = max(s - 1, 0)
    rows[(lo - (s - 1)):, :] = za_full[lo:s + OWN, :]
    out["zain"] = rows
    out["w1"] = np.ascontiguousarray(inp["L0_ffn_w1"][layer])
    out["w2"] = np.ascontiguousarray(inp["L0_ffn_w2"][layer])
    HID = 2730
    b1 = inp["L0_ffn_b1"][layer]
    b1c = np.zeros((128, 44), np.float32)
    for i in range(22):
        c0 = i * 128
        cw = min(128, HID - c0)
        b1c[0:cw, i] = b1[c0:c0 + cw]
        b1c[0:cw, 22 + i] = b1[HID + c0:HID + c0 + cw]
    out["b1c"] = b1c
    out["b2c"] = np.ascontiguousarray(inp["L0_ffn_b2"][layer].reshape(8, 128).T)
    out["lngc"] = np.ascontiguousarray(inp["L0_ffn_ln_g"][layer].reshape(8, 128).T)
    out["lnbc"] = np.ascontiguousarray(inp["L0_ffn_ln_b"][layer].reshape(8, 128).T)
    out["ident"] = np.eye(128, dtype=np.float32).astype(ml_dtypes.bfloat16)
    out["identf"] = np.eye(128, dtype=np.float32)
    if final_resid:
        out["x1in"] = np.ascontiguousarray(inp["_x1"][b][s:s + OWN])
    return out


def build_l1_attn(T=4096, QC=256):
    NQ = 8
    nc = bass.Bass("TRN2", target_bir_lowering=False)
    d = lambda *a, **k: nc.dram_tensor(*a, **k).ap()
    x2_h = d("x2", [T, 1024], F32, kind="ExternalInput")
    x2q_h = d("x2q", [NQ * QC, 1024], F32, kind="ExternalInput")
    wq_h = d("wq", [1024, 512], F32, kind="ExternalInput")
    wk_h = d("wk", [1024, 128], F32, kind="ExternalInput")
    wv_h = d("wv", [1024, 256], F32, kind="ExternalInput")
    wp_h = d("wp", [1024, 1024], F32, kind="ExternalInput")
    bp_h = d("bpc", [128, 8], F32, kind="ExternalInput")
    biasall_h = d("biasall", [128, 8 * 16 * 36], F32, kind="ExternalInput")
    ltz_h = d("ltz", [128, 8 * 4 * 256], BF16, kind="ExternalInput")
    ident_h = d("ident", [128, 128], BF16, kind="ExternalInput")
    identf_h = d("identf", [128, 128], F32, kind="ExternalInput")
    x3_h = d("x3", [1024, NQ * QC], F32, kind="ExternalOutput")

    es = ExitStack()
    sb, ps, sem = mk_ctx(nc, es)
    with es:
        wq = sb("s_wq", [128, 8 * 512], BF16)
        wk = sb("s_wk", [128, 8 * 128], BF16)
        wv = sb("s_wv", [128, 8 * 256], BF16)
        wp = sb("s_wp", [128, 8 * 1024], BF16)
        bpc = sb("s_bpc", [128, 8], F32)
        biasall = sb("s_biasall", [128, 8 * 16 * 36], F32)
        ltz = sb("s_ltz", [128, 8 * 4 * 256], BF16)
        ident = sb("s_ident", [128, 128], BF16)
        identf = sb("s_identf", [128, 128], F32)
        onesc = sb("onesc", [128, 1], BF16)
        ones64 = sb("ones64", [1, 64], BF16)
        stage = sb("stage", [128, 1024], F32)
        cst = sb("cst", [128, 2], F32)
        xx = sb("xx", [128, 1024], F32)
        lnjunk = sb("lnjunk", [128, 1024], F32)
        lnc = sb("lnc", [128, 16], F32)
        mv = sb("mv", [128, 2], F32)
        rstd = sb("rstd", [128, 1], F32)
        zn = sb("zn", [128, 1024], BF16)
        z2T = sb("z2T", [128, 8 * T], BF16)
        kTl = sb("kTl", [64, T], BF16)
        kTh = sb("kTh", [64, T], BF16)
        v_sb = sb("v_sb", [128, 32 * 260], BF16)
        znq = sb("znq", [128, 2 * 1024], BF16)
        znqT = sb("znqT", [128, 8 * 256], BF16)
        qTl = sb("qTl", [64, 4 * 256], BF16)
        qTh = sb("qTh", [64, 4 * 256], BF16)
        x2qT = sb("x2qT", [128, 8 * 256], F32)
        eb = [sb(f"eb{i}", [128, 2 * 256], BF16) for i in range(2)]
        ctxf = sb("ctxf", [128, 256], F32)
        rdf = sb("rdf", [1, 256], F32)
        rdb = sb("rdb", [1, 256], BF16)
        ctx = sb("ctx", [128, 8 * 256], BF16)
        outb = [sb(f"outb{i}", [128, 256], F32) for i in range(2)]

        ps0 = ps("ps0", [128, 512], F32)
        ps1 = ps("ps1", [128, 512], F32)
        pav = [ps(f"pav{i}", [128, 512], F32) for i in range(2)]
        pbc = ps("pbc", [128, 512], F32)
        ptp = ps("ptp", [128, 512], BF16)
        ptpf = ps("ptpf", [128, 512], F32)

        s_ld = sem("s_ld")
        s_ld2 = sem("s_ld2")
        s_lda = sem("s_lda")
        s_st = sem("s_st")
        s_ve = sem("s_ve")
        s_sc = sem("s_sc")
        s_te = sem("s_te")

        p = Prog(nc)
        E = p.emit

        def load_cast(whbm, wsb, nrows, ncols):
            for kc in range(nrows // 128):
                p.wait("sync", s_sc)
                E("sync", lambda e, kc=kc, ncols=ncols: e.dma_start(
                    out=stage[:, 0:ncols], in_=whbm[kc * 128:(kc + 1) * 128, :]).then_inc(s_ld.h, 16))
                s_ld.inc(16)
                p.wait("scalar", s_ld)
                E("scalar", lambda e, kc=kc, ncols=ncols, wsb=wsb: e.copy(
                    wsb[:, kc * ncols:(kc + 1) * ncols], stage[:, 0:ncols]).then_inc(s_sc.h, 1))
                s_sc.inc()

        load_cast(wq_h, wq, 1024, 512)
        load_cast(wk_h, wk, 1024, 128)
        load_cast(wv_h, wv, 1024, 256)
        load_cast(wp_h, wp, 1024, 1024)

        def simple_load(src, dst):
            E("sync", lambda e, s=src, dst=dst: e.dma_start(out=dst, in_=s).then_inc(s_ld2.h, 16))
            s_ld2.inc(16)

        for srch, dst in ((bp_h, bpc), (biasall_h, biasall), (ltz_h, ltz), (ident_h, ident), (identf_h, identf)):
            simple_load(srch, dst[:, :])
        p.wait("vector", s_ld2)
        p.wait("scalar", s_ld2)
        p.wait("tensor", s_ld2)
        p.wait("tensor", s_ld)
        E("vector", lambda e: e.memset(cst[:, 0:1], 1e-5))
        E("vector", lambda e: e.memset(cst[:, 1:2], 1e-19))
        E("vector", lambda e: e.memset(onesc[:, :], 1.0))
        E("vector", lambda e: e.memset(ones64[:, :], 1.0))
        E("vector", lambda e: e.memset(v_sb[:, :], 1.0).then_inc(s_ve.h, 1))
        s_ve.inc()
        p.wait("vector", s_ve)

        def ln_tile(xsrc, dst, i_note=0):
            p.wait("scalar", s_lda)
            p.wait("scalar", s_ve)
            mean_ap, rstd_ap = ln_core(p, E, s_ve, s_sc, [(xsrc[:, :], 1024)], lnjunk, lnc, cst[:, 0:1])
            p.wait("vector", s_sc)
            E("vector", lambda e, dst=dst, mean_ap=mean_ap, rstd_ap=rstd_ap: e.tensor_scalar(
                out=dst, in0=xsrc[:, :], scalar1=mean_ap,
                scalar2=rstd_ap, op0=ALU.subtract, op1=ALU.mult).then_inc(s_ve.h, 1))
            s_ve.inc()

        def transpose8(src_bf, dstfn, use_f32=False):
            idm = identf if use_f32 else ident
            bank = ptpf if use_f32 else ptp
            for k in range(8):
                slot = k % 4
                p.wait("tensor", s_sc)
                E("tensor", lambda e, k=k, slot=slot, src_bf=src_bf, idm=idm, bank=bank: e.transpose(
                    bank[:, slot * 128:(slot + 1) * 128], src_bf[:, k * 128:(k + 1) * 128],
                    idm[:, :]).then_inc(s_te.h, 1))
                s_te.inc()
                p.wait("scalar", s_te)
                dst = dstfn(k)
                E("scalar", lambda e, dst=dst, slot=slot, bank=bank: e.copy(
                    dst, bank[:, slot * 128:(slot + 1) * 128]).then_inc(s_sc.h, 1))
                s_sc.inc()

        for i in range(32):
            p.wait("sync", s_ve)
            p.wait("sync", s_te)
            p.wait("sync", s_sc)
            E("sync", lambda e, i=i: e.dma_start(out=xx[:, :], in_=x2_h[i * 128:(i + 1) * 128, :]).then_inc(s_lda.h, 16))
            s_lda.inc(16)
            p.wait("vector", s_lda)
            p.wait("vector", s_te)
            p.wait("vector", s_sc)
            ln_tile(xx, zn[:, :])
            p.wait("tensor", s_ve)
            transpose8(zn, lambda k, i=i: z2T[:, k * T + i * 128:k * T + (i + 1) * 128])
        z2T_ready = s_sc.v

        for cc in range(8):
            for kc in range(8):
                if kc == 0:
                    p.wait("tensor", s_sc, z2T_ready)
                    p.wait("tensor", s_sc)
                    p.wait("tensor", s_ve)
                E("tensor", lambda e, kc=kc, cc=cc, last=(kc == 7): (
                    e.matmul(ps0[:, :], wk[:, kc * 128:(kc + 1) * 128],
                             z2T[:, kc * T + cc * 512:kc * T + (cc + 1) * 512],
                             start=(kc == 0), stop=last).then_inc(s_te.h, 1)
                    if last else
                    e.matmul(ps0[:, :], wk[:, kc * 128:(kc + 1) * 128],
                             z2T[:, kc * T + cc * 512:kc * T + (cc + 1) * 512],
                             start=(kc == 0), stop=last)))
            s_te.inc()
            p.wait("scalar", s_te)
            E("scalar", lambda e, cc=cc: e.copy(kTl[:, cc * 512:(cc + 1) * 512], ps0[0:64, :]))
            E("scalar", lambda e, cc=cc: e.copy(kTh[:, cc * 512:(cc + 1) * 512], ps0[64:128, :]).then_inc(s_sc.h, 1))
            s_sc.inc()
        for vt in range(32):
            for kc in range(8):
                if kc == 0:
                    p.wait("tensor", s_sc)
                    p.wait("tensor", s_ve)
                E("tensor", lambda e, kc=kc, vt=vt, last=(kc == 7): (
                    e.matmul(ps1[0:128, 0:256], z2T[:, kc * T + vt * 128:kc * T + (vt + 1) * 128],
                             wv[:, kc * 256:(kc + 1) * 256], start=(kc == 0), stop=last).then_inc(s_te.h, 1)
                    if last else
                    e.matmul(ps1[0:128, 0:256], z2T[:, kc * T + vt * 128:kc * T + (vt + 1) * 128],
                             wv[:, kc * 256:(kc + 1) * 256], start=(kc == 0), stop=last)))
            s_te.inc()
            p.wait("scalar", s_te)
            E("scalar", lambda e, vt=vt: e.copy(
                v_sb[:, vt * 260:vt * 260 + 260].rearrange("p (g c) -> p g c", g=4)[:, :, 0:64],
                ps1[0:128, 0:256].rearrange("p (g c) -> p g c", g=4)).then_inc(s_sc.h, 1))
            s_sc.inc()
        kv_ready = s_sc.v

        for qi in range(NQ):
            ktmax = 4 * qi + 4
            for t2 in range(2):
                p.wait("sync", s_ve)
                p.wait("sync", s_te)
                p.wait("sync", s_st)
                p.wait("sync", s_sc)
                E("sync", lambda e, qi=qi, t2=t2: e.dma_start(
                    out=xx[:, :], in_=x2q_h[qi * 256 + t2 * 128:qi * 256 + (t2 + 1) * 128, :]).then_inc(s_lda.h, 16))
                s_lda.inc(16)
                p.wait("vector", s_lda)
                p.wait("vector", s_te)
                p.wait("vector", s_sc)
                ln_tile(xx, znq[:, t2 * 1024:(t2 + 1) * 1024])
                p.wait("tensor", s_ve)
                transpose8(znq[:, t2 * 1024:(t2 + 1) * 1024].rearrange("p c -> p c") if False else znq[:, t2 * 1024:(t2 + 1) * 1024],
                           lambda k, t2=t2: znqT[:, k * 256 + t2 * 128:k * 256 + (t2 + 1) * 128])
                transpose8(xx, lambda k, t2=t2: x2qT[:, k * 256 + t2 * 128:k * 256 + (t2 + 1) * 128], use_f32=True)
            for quad in range(4):
                for kc in range(8):
                    if kc == 0:
                        p.wait("tensor", s_sc)
                        p.wait("tensor", s_ve)
                    E("tensor", lambda e, kc=kc, quad=quad, last=(kc == 7): (
                        e.matmul(ps0[0:128, 0:256], wq[:, kc * 512 + quad * 128:kc * 512 + (quad + 1) * 128],
                                 znqT[:, kc * 256:(kc + 1) * 256], start=(kc == 0), stop=last).then_inc(s_te.h, 1)
                        if last else
                        e.matmul(ps0[0:128, 0:256], wq[:, kc * 512 + quad * 128:kc * 512 + (quad + 1) * 128],
                                 znqT[:, kc * 256:(kc + 1) * 256], start=(kc == 0), stop=last)))
                s_te.inc()
                p.wait("scalar", s_te)
                E("scalar", lambda e, quad=quad: e.copy(qTl[:, quad * 256:(quad + 1) * 256], ps0[0:64, 0:256]))
                E("scalar", lambda e, quad=quad: e.copy(qTh[:, quad * 256:(quad + 1) * 256], ps0[64:128, 0:256]).then_inc(s_sc.h, 1))
                s_sc.inc()
            q_ready = s_sc.v

            for quad in range(4):
                for pr in range(2):
                    heads = (4 * quad + 2 * pr, 4 * quad + 2 * pr + 1)
                    ktt = kTl if pr == 0 else kTh
                    qtt = qTl if pr == 0 else qTh
                    rowb = (0, 32)
                    for hi in range(2):
                        p.wait("tensor", s_ve)
                        p.wait("tensor", s_sc)
                    for kt in range(ktmax):
                        ebuf = eb[kt % 2]
                        for hi in range(2):
                            head = heads[hi]
                            rb = rowb[hi]
                            scb = (ps0, ps1)[hi]
                            p.wait("tensor", s_sc)
                            p.wait("tensor", s_ve)
                            E("tensor", lambda e, rb=rb, kt=kt, quad=quad, scb=scb, ktt=ktt, qtt=qtt: e.matmul(
                                scb[0:128, 0:256], ktt[rb:rb + 32, kt * 128:(kt + 1) * 128],
                                qtt[rb:rb + 32, quad * 256:(quad + 1) * 256],
                                start=True, stop=True).then_inc(s_te.h, 1))
                            s_te.inc()
                        dkt = kt - 4 * qi
                        for hi in range(2):
                            head = heads[hi]
                            scb = (ps0, ps1)[hi]
                            if dkt >= 0:
                                p.wait("vector", s_te)
                                E("vector", lambda e, scb=scb, qi=qi, dkt=dkt: e.tensor_tensor(
                                    scb[0:128, 0:256], scb[0:128, 0:256],
                                    ltz[:, qi * 1024 + dkt * 256:qi * 1024 + (dkt + 1) * 256], ALU.add).then_inc(s_ve.h, 1))
                                s_ve.inc()
                            bcol = qi * 576 + head * 36 + kt
                            p.wait("scalar", s_te)
                            p.wait("scalar", s_ve)
                            E("scalar", lambda e, ebuf=ebuf, hi=hi, scb=scb, bcol=bcol: e.activation(
                                ebuf[:, hi * 256:(hi + 1) * 256], scb[0:128, 0:256], AF.Exp,
                                scale=0.17677669529663687, bias=biasall[:, bcol:bcol + 1]).then_inc(s_sc.h, 1))
                            s_sc.inc()
                        p.wait("tensor", s_sc)
                        for hi in range(2):
                            head = heads[hi]
                            g = head % 4
                            E("tensor", lambda e, kt=kt, g=g, hi=hi, ebuf=ebuf, last=(kt == ktmax - 1): e.matmul(
                                pav[hi][0:65, 0:256], v_sb[:, kt * 260 + g * 65:kt * 260 + g * 65 + 65],
                                ebuf[:, hi * 256:(hi + 1) * 256], start=(kt == 0), stop=last).then_inc(s_te.h, 1))
                            s_te.inc()
                    for hi in range(2):
                        head = heads[hi]
                        p.wait("scalar", s_te)
                        p.wait("scalar", s_ve)
                        E("scalar", lambda e, hi=hi: e.copy(ctxf[0:64, :], pav[hi][0:64, 0:256]))
                        E("scalar", lambda e, hi=hi: e.activation(rdf[:, :], pav[hi][64:65, 0:256], AF.Ln, bias=cst[0:1, 1:2]))
                        E("scalar", lambda e: e.activation(rdb[:, :], rdf[:, :], AF.Exp, scale=-1.0).then_inc(s_sc.h, 1))
                        s_sc.inc()
                        p.wait("tensor", s_sc)
                        p.wait("tensor", s_ve)
                        E("tensor", lambda e: e.matmul(pbc[0:64, 0:256], ones64[:, :], rdb[:, :], start=True, stop=True).then_inc(s_te.h, 1))
                        s_te.inc()
                        p.wait("vector", s_te)
                        cb = head // 2
                        rr = (head % 2) * 64
                        E("vector", lambda e, cb=cb, rr=rr: e.tensor_mul(
                            ctx[rr:rr + 64, cb * 256:(cb + 1) * 256],
                            ctxf[0:64, :], pbc[0:64, 0:256]).then_inc(s_ve.h, 1))
                        s_ve.inc()
            ctx_ready = s_ve.v

            for oc in range(8):
                bank = oc % 2
                pob = (ps0, ps1)[bank]
                for kc in range(8):
                    if kc == 0:
                        p.wait("tensor", s_ve, ctx_ready)
                        p.wait("tensor", s_ve)
                        p.wait("tensor", s_sc)
                        p.wait("tensor", s_st)
                    E("tensor", lambda e, oc=oc, kc=kc, pob=pob, last=(kc == 7): (
                        e.matmul(pob[0:128, 0:256], wp[:, kc * 1024 + oc * 128:kc * 1024 + (oc + 1) * 128],
                                 ctx[:, kc * 256:(kc + 1) * 256], start=(kc == 0), stop=last).then_inc(s_te.h, 1)
                        if last else
                        e.matmul(pob[0:128, 0:256], wp[:, kc * 1024 + oc * 128:kc * 1024 + (oc + 1) * 128],
                                 ctx[:, kc * 256:(kc + 1) * 256], start=(kc == 0), stop=last)))
                s_te.inc()
                p.wait("vector", s_te)
                p.wait("vector", s_st)
                E("vector", lambda e, oc=oc, bank=bank, pob=pob: e.scalar_tensor_tensor(
                    out=outb[bank][:, :], in0=pob[0:128, 0:256], scalar=bpc[:, oc:oc + 1],
                    in1=x2qT[:, oc * 256:(oc + 1) * 256], op0=ALU.add, op1=ALU.add).then_inc(s_ve.h, 1))
                s_ve.inc()
                p.wait("gpsimd", s_ve)
                E("gpsimd", lambda e, oc=oc, bank=bank, qi=qi: e.dma_start(
                    x3_h[oc * 128:(oc + 1) * 128, qi * 256:(qi + 1) * 256], outb[bank][:, :]).then_inc(s_st.h, 16))
                s_st.inc(16)
                p.wait("gpsimd", s_st)

        p.wait("gpsimd", s_st)
        with nc.Block() as block:
            p.attach(block)
    return nc


def l1_attn_inputs(inp, core, T=4096, QC=256):
    import ml_dtypes
    b, pp = core // 2, core % 2
    x2 = np.ascontiguousarray(inp["_x2"][b])
    out = {"x2": x2}
    qcs = [2 * i + pp for i in range(8)]
    out["x2q"] = np.ascontiguousarray(np.concatenate([x2[qc * QC:(qc + 1) * QC] for qc in qcs], axis=0))
    out["wq"] = np.ascontiguousarray(inp["L1_attn_Wq"])
    out["wk"] = np.ascontiguousarray(inp["L1_attn_Wk"])
    out["wv"] = np.ascontiguousarray(inp["L1_attn_Wv"])
    out["wp"] = np.ascontiguousarray(inp["L1_attn_Wp"])
    out["bpc"] = np.ascontiguousarray(inp["L1_attn_bp"].reshape(8, 128).T)
    H = 16
    slopes = 2.0 ** (-8.0 * np.arange(1, H + 1) / H)
    rt32 = np.sqrt(32.0)
    biasall = np.full((8, 16, 36, 128), -30000.0, np.float32)
    ltz = np.zeros((8, 4, 128, 256), np.float32)
    for qi in range(8):
        qc = 2 * qi + pp
        ref = qc * QC + QC - 1
        ktreal = (qc + 1) * QC // 128
        for kt in range(4 * qi + 4):
            if kt >= ktreal:
                continue
            posk = kt * 128 + np.arange(128)
            for h in range(H):
                biasall[qi, h, kt, :] = slopes[h] * (posk - ref) / rt32
        for dkt in range(4):
            kt = 4 * qi + dkt
            posk = kt * 128 + np.arange(128)[:, None]
            posq = qc * QC + np.arange(256)[None, :]
            ltz[qi, dkt] = np.where(posk <= posq, 0.0, -30000.0)
    out["biasall"] = np.ascontiguousarray(biasall.transpose(3, 0, 1, 2).reshape(128, 8 * 16 * 36))
    out["ltz"] = np.ascontiguousarray(ltz.transpose(2, 0, 1, 3).reshape(128, 8 * 4 * 256)).astype(ml_dtypes.bfloat16)
    out["ident"] = np.eye(128, dtype=np.float32).astype(ml_dtypes.bfloat16)
    out["identf"] = np.eye(128, dtype=np.float32)
    return out


def build_l1_ffn(OWN=2048):
    NCH = OWN // 512
    nc = bass.Bass("TRN2", target_bir_lowering=False)
    d = lambda *a, **k: nc.dram_tensor(*a, **k).ap()
    x3_h = d("x3in", [OWN + 1, 1024], F32, kind="ExternalInput")
    wr_h = d("wrt", [1024, 1024], F32, kind="ExternalInput")
    wk_h = d("wkt", [1024, 1024], F32, kind="ExternalInput")
    wv_h = d("wvt", [1024, 1024], F32, kind="ExternalInput")
    mur_h = d("murc", [128, 8], F32, kind="ExternalInput")
    muk_h = d("mukc", [128, 8], F32, kind="ExternalInput")
    ident_h = d("ident", [128, 128], BF16, kind="ExternalInput")
    identf_h = d("identf", [128, 128], F32, kind="ExternalInput")
    out_h = d("xout", [OWN, 1024], F32, kind="ExternalOutput")

    es = ExitStack()
    sb, ps, sem = mk_ctx(nc, es)
    with es:
        wr = sb("s_wr", [128, 8 * 1024], BF16)
        wk = sb("s_wk", [128, 8 * 1024], BF16)
        wv = sb("s_wv", [128, 8 * 1024], BF16)
        murc = sb("s_murc", [128, 8], F32)
        mukc = sb("s_mukc", [128, 8], F32)
        ident = sb("s_ident", [128, 128], BF16)
        identf = sb("s_identf", [128, 128], F32)
        stage = sb("stage", [128, 1024], F32)
        cst = sb("cst", [128, 2], F32)
        xx = sb("xx", [128, 1024], F32)
        xsh = sb("xsh", [128, 1024], F32)
        lnjunk = sb("lnjunk", [128, 1024], F32)
        lnc = sb("lnc", [128, 16], F32)
        mv = sb("mv", [128, 2], F32)
        rstd = sb("rstd", [128, 1], F32)
        zc = sb("zc", [128, 1024], F32)
        zp = sb("zp", [128, 1024], F32)
        zcT = sb("zcT", [128, 8 * 512], F32)
        zpT = sb("zpT", [128, 8 * 512], F32)
        x3T = sb("x3T", [128, 8 * 512], F32)
        dT = sb("dT", [128, 512], F32)
        xr = sb("xr", [128, 8 * 512], BF16)
        xk = sb("xk", [128, 8 * 512], BF16)
        rsg = sb("rsg", [128, 8 * 512], F32)
        kact = sb("kact", [128, 8 * 512], BF16)
        t2b = sb("t2b", [128, 512], F32)
        outf = sb("outf", [128, 8 * 512], F32)
        outt = sb("outt", [128, 4 * 1024], F32)

        pm = [ps(f"pm{i}", [128, 512], F32) for i in range(2)]
        ptp = ps("ptp", [128, 512], BF16)
        ptpf = ps("ptpf", [128, 512], F32)

        s_ld = sem("s_ld")
        s_ld2 = sem("s_ld2")
        s_lda = sem("s_lda")
        s_st = sem("s_st")
        s_ve = sem("s_ve")
        s_sc = sem("s_sc")
        s_te = sem("s_te")

        p = Prog(nc)
        E = p.emit

        def load_cast(whbm, wsb, nrows, ncols):
            for kc in range(nrows // 128):
                p.wait("sync", s_sc)
                E("sync", lambda e, kc=kc, ncols=ncols, whbm=whbm: e.dma_start(
                    out=stage[:, 0:ncols], in_=whbm[kc * 128:(kc + 1) * 128, :]).then_inc(s_ld.h, 16))
                s_ld.inc(16)
                p.wait("scalar", s_ld)
                E("scalar", lambda e, kc=kc, ncols=ncols, wsb=wsb: e.copy(
                    wsb[:, kc * ncols:(kc + 1) * ncols], stage[:, 0:ncols]).then_inc(s_sc.h, 1))
                s_sc.inc()

        load_cast(wr_h, wr, 1024, 1024)
        load_cast(wk_h, wk, 1024, 1024)
        load_cast(wv_h, wv, 1024, 1024)

        def simple_load(src, dst):
            E("sync", lambda e, s=src, dst=dst: e.dma_start(out=dst, in_=s).then_inc(s_ld2.h, 16))
            s_ld2.inc(16)

        for srch, dst in ((mur_h, murc), (muk_h, mukc), (ident_h, ident), (identf_h, identf)):
            simple_load(srch, dst[:, :])
        p.wait("vector", s_ld2)
        p.wait("scalar", s_ld2)
        p.wait("tensor", s_ld2)
        p.wait("tensor", s_ld)
        E("vector", lambda e: e.memset(cst[:, 0:1], 1e-5).then_inc(s_ve.h, 1))
        s_ve.inc()
        p.wait("vector", s_ve)

        def ln_halves(xa, xb, dst):
            pass

        for tc in range(NCH):
            for i in range(4):
                r0 = tc * 512 + i * 128
                p.wait("sync", s_ve)
                p.wait("sync", s_te)
                p.wait("sync", s_sc)
                E("sync", lambda e, r0=r0: e.dma_start(out=xx[:, :], in_=x3_h[r0 + 1:r0 + 129, :]).then_inc(s_lda.h, 16))
                s_lda.inc(16)
                E("sync", lambda e, r0=r0: e.dma_start(out=xsh[:, :], in_=x3_h[r0:r0 + 128, :]).then_inc(s_lda.h, 16))
                s_lda.inc(16)
                p.wait("vector", s_lda)
                p.wait("vector", s_te)
                p.wait("vector", s_sc)
                for src_t, dst_t in ((xx, zc), (xsh, zp)):
                    p.wait("scalar", s_lda)
                    p.wait("scalar", s_ve)
                    mean_ap, rstd_ap = ln_core(p, E, s_ve, s_sc, [(src_t[:, :], 1024)], lnjunk, lnc, cst[:, 0:1])
                    p.wait("vector", s_sc)
                    E("vector", lambda e, src_t=src_t, dst_t=dst_t, mean_ap=mean_ap, rstd_ap=rstd_ap: e.tensor_scalar(
                        out=dst_t[:, :], in0=src_t[:, :], scalar1=mean_ap,
                        scalar2=rstd_ap, op0=ALU.subtract, op1=ALU.mult).then_inc(s_ve.h, 1))
                    s_ve.inc()
                p.wait("tensor", s_ve)
                for src_t, dst_t in ((zc, zcT), (zp, zpT), (xx, x3T)):
                    for k in range(8):
                        slot = k % 4
                        p.wait("tensor", s_sc)
                        E("tensor", lambda e, k=k, slot=slot, src_t=src_t: e.transpose(
                            ptpf[:, slot * 128:(slot + 1) * 128], src_t[:, k * 128:(k + 1) * 128],
                            identf[:, :]).then_inc(s_te.h, 1))
                        s_te.inc()
                        p.wait("scalar", s_te)
                        E("scalar", lambda e, k=k, slot=slot, i=i, dst_t=dst_t: e.copy(
                            dst_t[:, k * 512 + i * 128:k * 512 + (i + 1) * 128],
                            ptpf[:, slot * 128:(slot + 1) * 128]).then_inc(s_sc.h, 1))
                        s_sc.inc()
            zT_ready = s_sc.v

            p.wait("vector", s_sc, zT_ready)
            p.wait("vector", s_te)
            for kc in range(8):
                E("vector", lambda e, kc=kc: e.tensor_sub(dT[:, :], zcT[:, kc * 512:(kc + 1) * 512], zpT[:, kc * 512:(kc + 1) * 512]))
                E("vector", lambda e, kc=kc: e.scalar_tensor_tensor(
                    out=xr[:, kc * 512:(kc + 1) * 512], in0=dT[:, :], scalar=murc[:, kc:kc + 1],
                    in1=zpT[:, kc * 512:(kc + 1) * 512], op0=ALU.mult, op1=ALU.add))
                E("vector", lambda e, kc=kc: e.scalar_tensor_tensor(
                    out=xk[:, kc * 512:(kc + 1) * 512], in0=dT[:, :], scalar=mukc[:, kc:kc + 1],
                    in1=zpT[:, kc * 512:(kc + 1) * 512], op0=ALU.mult, op1=ALU.add))
            E("vector", lambda e: e.nop())
            E("vector", lambda e: e.tensor_copy(t2b[:, 0:1], xr[:, 0:1]).then_inc(s_ve.h, 1))
            s_ve.inc()
            mix_ready = s_ve.v

            for oc in range(8):
                bank = oc % 2
                for kc in range(8):
                    if kc == 0:
                        p.wait("tensor", s_ve, mix_ready)
                        p.wait("tensor", s_sc)
                        p.wait("tensor", s_ve)
                    E("tensor", lambda e, oc=oc, kc=kc, bank=bank, last=(kc == 7): e.matmul(
                        pm[bank][:, :], wr[:, kc * 1024 + oc * 128:kc * 1024 + (oc + 1) * 128],
                        xr[:, kc * 512:(kc + 1) * 512], start=(kc == 0), stop=last).then_inc(s_te.h, 1))
                    s_te.inc()
                p.wait("scalar", s_te)
                p.wait("scalar", s_ve)
                E("scalar", lambda e, oc=oc, bank=bank: e.activation(
                    rsg[:, oc * 512:(oc + 1) * 512], pm[bank][:, :], AF.Sigmoid).then_inc(s_sc.h, 1))
                s_sc.inc()
            for oc in range(8):
                bank = oc % 2
                for kc in range(8):
                    if kc == 0:
                        p.wait("tensor", s_sc)
                        p.wait("tensor", s_ve)
                    E("tensor", lambda e, oc=oc, kc=kc, bank=bank, last=(kc == 7): e.matmul(
                        pm[bank][:, :], wk[:, kc * 1024 + oc * 128:kc * 1024 + (oc + 1) * 128],
                        xk[:, kc * 512:(kc + 1) * 512], start=(kc == 0), stop=last).then_inc(s_te.h, 1))
                    s_te.inc()
                p.wait("scalar", s_te)
                p.wait("scalar", s_ve)
                E("scalar", lambda e, oc=oc, bank=bank: e.activation(
                    t2b[:, :], pm[bank][:, :], AF.Relu))
                E("scalar", lambda e, oc=oc: e.activation(
                    kact[:, oc * 512:(oc + 1) * 512], t2b[:, :], AF.Square).then_inc(s_sc.h, 1))
                s_sc.inc()
            kact_ready = s_sc.v
            for oc in range(8):
                bank = oc % 2
                for kc in range(8):
                    if kc == 0:
                        p.wait("tensor", s_sc, kact_ready)
                        p.wait("tensor", s_sc)
                        p.wait("tensor", s_ve)
                    E("tensor", lambda e, oc=oc, kc=kc, bank=bank, last=(kc == 7): e.matmul(
                        pm[bank][:, :], wv[:, kc * 1024 + oc * 128:kc * 1024 + (oc + 1) * 128],
                        kact[:, kc * 512:(kc + 1) * 512], start=(kc == 0), stop=last).then_inc(s_te.h, 1))
                    s_te.inc()
                p.wait("vector", s_te)
                E("vector", lambda e, oc=oc, bank=bank: e.tensor_mul(
                    t2b[:, :], rsg[:, oc * 512:(oc + 1) * 512], pm[bank][:, :]))
                E("vector", lambda e, oc=oc, bank=bank: e.tensor_add(
                    outf[:, oc * 512:(oc + 1) * 512], x3T[:, oc * 512:(oc + 1) * 512], t2b[:, :]).then_inc(s_ve.h, 1))
                s_ve.inc()
            out_ready = s_ve.v

            p.wait("tensor", s_ve, out_ready)
            for i in range(4):
                for k in range(8):
                    slot = k % 4
                    p.wait("tensor", s_sc)
                    p.wait("tensor", s_st)
                    E("tensor", lambda e, i=i, k=k, slot=slot: e.transpose(
                        ptpf[:, slot * 128:(slot + 1) * 128], outf[:, k * 512 + i * 128:k * 512 + (i + 1) * 128],
                        identf[:, :]).then_inc(s_te.h, 1))
                    s_te.inc()
                    p.wait("scalar", s_te)
                    E("scalar", lambda e, i=i, k=k, slot=slot: e.copy(
                        outt[:, i * 1024 + k * 128:i * 1024 + (k + 1) * 128],
                        ptpf[:, slot * 128:(slot + 1) * 128]).then_inc(s_sc.h, 1))
                    s_sc.inc()
                p.wait("gpsimd", s_sc)
                E("gpsimd", lambda e, tc=tc, i=i: e.dma_start(
                    out_h[tc * 512 + i * 128:tc * 512 + (i + 1) * 128, :], outt[:, i * 1024:(i + 1) * 1024]).then_inc(s_st.h, 16))
                s_st.inc(16)
                p.wait("gpsimd", s_st)

        p.wait("gpsimd", s_st)
        with nc.Block() as block:
            p.attach(block)
    return nc


def l1_ffn_inputs(inp, core, OWN=2048):
    import ml_dtypes
    b, h = core // 2, core % 2
    s = h * OWN
    x3 = np.ascontiguousarray(inp["_x3"][b])
    rows = np.zeros((OWN + 1, 1024), np.float32)
    lo = max(s - 1, 0)
    rows[(lo - (s - 1)):, :] = x3[lo:s + OWN, :]
    out = {"x3in": rows}
    out["wrt"] = np.ascontiguousarray(inp["L1_ffn_W_r"].T)
    out["wkt"] = np.ascontiguousarray(inp["L1_ffn_W_k"].T)
    out["wvt"] = np.ascontiguousarray(inp["L1_ffn_W_v"].T)
    out["murc"] = np.ascontiguousarray(inp["L1_ffn_mu_r"].reshape(8, 128).T)
    out["mukc"] = np.ascontiguousarray(inp["L1_ffn_mu_k"].reshape(8, 128).T)
    out["ident"] = np.eye(128, dtype=np.float32).astype(ml_dtypes.bfloat16)
    out["identf"] = np.eye(128, dtype=np.float32)
    return out


def _run2(nc, maps, ncores):
    run_bass_kernel_spmd(nc, maps, core_ids=ncores)
    return run_bass_kernel_spmd(nc, maps, core_ids=ncores)


def kernel(**inputs):
    inp = {k: np.asarray(v) for k, v in inputs.items()}
    ncores = list(range(8))

    nc1 = build_l0_attn()
    res1 = _run2(nc1, [l0_attn_inputs(inp, c) for c in range(8)], ncores)
    inp["_part0"] = np.stack([res1.results[2 * b]["part"].T for b in range(4)])
    inp["_part1"] = np.stack([res1.results[2 * b + 1]["part"].T for b in range(4)])

    nc2 = build_ffn_a(do_ln=True, resid3=True)
    res2 = _run2(nc2, [ffn_a_inputs(inp, None, c, 0) for c in range(8)], ncores)
    za0 = np.zeros((4, 4096, 1024), np.float32)
    x1 = np.zeros((4, 4096, 1024), np.float32)
    for c in range(8):
        b, h = c // 2, c % 2
        za0[b, h * 2048:(h + 1) * 2048] = res2.results[c]["za"][:, 512:2560].T
        x1[b, h * 2048:(h + 1) * 2048] = res2.results[c]["x1out"]
    inp["_za0"] = za0
    inp["_x1"] = x1

    nc3 = build_ffn_b(final_resid=False)
    res3 = _run2(nc3, [ffn_b_inputs(inp, c, 0, False) for c in range(8)], ncores)
    zf0 = np.zeros((4, 4096, 1024), np.float32)
    for c in range(8):
        b, h = c // 2, c % 2
        zf0[b, h * 2048:(h + 1) * 2048] = res3.results[c]["zf"].T
    inp["_zf0"] = zf0

    nc4 = build_ffn_a(do_ln=False, resid3=False)
    res4 = _run2(nc4, [ffn_a_inputs(inp, None, c, 1) for c in range(8)], ncores)
    za1 = np.zeros((4, 4096, 1024), np.float32)
    for c in range(8):
        b, h = c // 2, c % 2
        za1[b, h * 2048:(h + 1) * 2048] = res4.results[c]["za"][:, 512:2560].T
    inp["_za1"] = za1

    nc5 = build_ffn_b(final_resid=True)
    res5 = _run2(nc5, [ffn_b_inputs(inp, c, 1, True) for c in range(8)], ncores)
    x2 = np.zeros((4, 4096, 1024), np.float32)
    for c in range(8):
        b, h = c // 2, c % 2
        x2[b, h * 2048:(h + 1) * 2048] = res5.results[c]["zf"].T
    inp["_x2"] = x2

    nc6 = build_l1_attn()
    res6 = _run2(nc6, [l1_attn_inputs(inp, c) for c in range(8)], ncores)
    x3 = np.zeros((4, 4096, 1024), np.float32)
    for c in range(8):
        b, pp = c // 2, c % 2
        r = res6.results[c]["x3"].T
        for i in range(8):
            qc = 2 * i + pp
            x3[b, qc * 256:(qc + 1) * 256] = r[i * 256:(i + 1) * 256]
    inp["_x3"] = x3

    nc7 = build_l1_ffn()
    res7 = _run2(nc7, [l1_ffn_inputs(inp, c) for c in range(8)], ncores)
    out = np.zeros((4, 4096, 1024), np.float32)
    for c in range(8):
        b, h = c // 2, c % 2
        out[b, h * 2048:(h + 1) * 2048] = res7.results[c]["xout"]
    return out


# version 04
# speedup vs baseline: 1.9914x; 1.9914x over previous
import numpy as np
from contextlib import ExitStack
import concourse.bass as bass
import concourse.mybir as mybir
from concourse.bass_utils import run_bass_kernel_spmd

F32 = mybir.dt.float32
BF16 = mybir.dt.bfloat16
I32 = mybir.dt.int32
AF = mybir.ActivationFunctionType
ALU = mybir.AluOpType

ENGINES = ("sync", "vector", "scalar", "tensor", "gpsimd")


class Sem:
    def __init__(self, h):
        self.h = h
        self.v = 0

    def inc(self, n=1):
        self.v += n
        return self.v


class Prog:
    def __init__(self, nc):
        self.nc = nc
        self.ops = []

    def emit(self, eng, fn):
        self.ops.append((eng, fn))

    def wait(self, eng, sem, val=None):
        v = sem.v if val is None else val
        if v > 0:
            self.ops.append((eng, lambda e, s=sem.h, vv=v: e.wait_ge(s, vv)))

    def barrier(self, sems, engines=ENGINES):
        for eng in engines:
            for s in sems:
                self.wait(eng, s)

    def attach(self, block):
        for name in ENGINES:
            dec = getattr(block, name)

            def body(eng, _name=name):
                for e, fn in self.ops:
                    if e == _name:
                        fn(eng)

            dec(body)


def mk_ctx(nc, es):
    sb = lambda name, shape, dt: es.enter_context(nc.sbuf_tensor(name, shape, dt))
    ps = lambda name, shape, dt: es.enter_context(nc.psum_tensor(name, shape, dt))
    sem = lambda name: Sem(es.enter_context(nc.semaphore(name)))
    return sb, ps, sem




def ln_core(p, E, s_ve, s_sc, halves, sqs, lnc, epscol, denom=1024.0):
    off = 0
    for idx, (h, w) in enumerate(halves):
        last = idx == len(halves) - 1
        E("scalar", lambda e, h=h, w=w, off=off, last=last: (
            e.activation(sqs[:, off:off + w], h, AF.Square).then_inc(s_sc.h, 1)
            if last else e.activation(sqs[:, off:off + w], h, AF.Square)))
        off += w
    s_sc.inc()
    p.wait("vector", s_sc)
    off = 0
    for idx, (h, w) in enumerate(halves):
        last = idx == len(halves) - 1
        E("vector", lambda e, h=h, idx=idx: e.tensor_reduce(lnc[:, idx:idx + 1], h, mybir.AxisListType.X, ALU.add))
        E("vector", lambda e, w=w, off=off, idx=idx, last=last: (
            e.tensor_reduce(lnc[:, 2 + idx:3 + idx], sqs[:, off:off + w], mybir.AxisListType.X, ALU.add).then_inc(s_ve.h, 1)
            if last else
            e.tensor_reduce(lnc[:, 2 + idx:3 + idx], sqs[:, off:off + w], mybir.AxisListType.X, ALU.add)))
        off += w
    s_ve.inc()
    two = len(halves) == 2

    def vfence():
        pass
    if two:
        E("vector", lambda e: e.tensor_scalar(out=lnc[:, 4:5], in0=lnc[:, 0:1], scalar1=1.0 / denom,
                                              scalar2=None, op0=ALU.mult, op1=ALU.bypass))
        E("vector", lambda e: e.tensor_scalar(out=lnc[:, 6:7], in0=lnc[:, 2:3], scalar1=1.0 / denom,
                                              scalar2=None, op0=ALU.mult, op1=ALU.bypass).then_inc(s_ve.h, 1))
        s_ve.inc()
        p.wait("vector", s_ve)
        E("vector", lambda e: e.tensor_scalar(out=lnc[:, 5:6], in0=lnc[:, 1:2], scalar1=1.0 / denom,
                                              scalar2=lnc[:, 4:5], op0=ALU.mult, op1=ALU.add))
        E("vector", lambda e: e.tensor_scalar(out=lnc[:, 7:8], in0=lnc[:, 3:4], scalar1=1.0 / denom,
                                              scalar2=lnc[:, 6:7], op0=ALU.mult, op1=ALU.add).then_inc(s_ve.h, 1))
        s_ve.inc()
        p.wait("vector", s_ve)
    else:
        E("vector", lambda e: e.tensor_scalar(out=lnc[:, 5:6], in0=lnc[:, 0:1], scalar1=1.0 / denom,
                                              scalar2=None, op0=ALU.mult, op1=ALU.bypass))
        E("vector", lambda e: e.tensor_scalar(out=lnc[:, 7:8], in0=lnc[:, 2:3], scalar1=1.0 / denom,
                                              scalar2=None, op0=ALU.mult, op1=ALU.bypass).then_inc(s_ve.h, 1))
        s_ve.inc()
        p.wait("vector", s_ve)
    E("vector", lambda e: e.tensor_mul(lnc[:, 8:9], lnc[:, 5:6], lnc[:, 5:6]).then_inc(s_ve.h, 1))
    s_ve.inc()
    p.wait("vector", s_ve)
    E("vector", lambda e: e.tensor_scalar(out=lnc[:, 10:11], in0=lnc[:, 8:9], scalar1=-1.0,
                                          scalar2=lnc[:, 7:8], op0=ALU.mult, op1=ALU.add).then_inc(s_ve.h, 1))
    s_ve.inc()
    p.wait("vector", s_ve)
    E("vector", lambda e: e.tensor_scalar(out=lnc[:, 11:12], in0=lnc[:, 10:11], scalar1=0.0,
                                          scalar2=None, op0=ALU.max, op1=ALU.bypass).then_inc(s_ve.h, 1))
    s_ve.inc()
    p.wait("scalar", s_ve)
    E("scalar", lambda e: e.activation(lnc[:, 12:13], lnc[:, 11:12], AF.Ln, bias=epscol))
    E("scalar", lambda e: e.activation(lnc[:, 13:14], lnc[:, 12:13], AF.Exp, scale=-0.5).then_inc(s_sc.h, 1))
    s_sc.inc()
    return lnc[:, 5:6], lnc[:, 13:14]



def prewarm_tables(p, E, s_sc, scratch, funcs):
    for f in funcs:
        E("scalar", lambda e, f=f: e.activation(scratch, scratch, f))
    E("scalar", lambda e: e.nop().then_inc(s_sc.h, 1))
    s_sc.inc()
    p.wait("scalar", s_sc)

def build_l0_attn(T=4096, TC=512, debug=False):
    DM = 1024
    NCH = T // TC
    NT = TC // 128

    nc = bass.Bass("TRN2", target_bir_lowering=False)
    d = lambda *a, **k: nc.dram_tensor(*a, **k).ap()
    x_h = d("x", [T, DM], F32, kind="ExternalInput")
    wg_h = d("w_in_g", [DM, 1024], F32, kind="ExternalInput")
    wi_h = d("w_in_i", [DM, 1024], F32, kind="ExternalInput")
    wd_h = d("w_delta", [DM, 1024], F32, kind="ExternalInput")
    wo_h = d("w_out", [1024, DM], F32, kind="ExternalInput")
    wb_h = d("w_b", [DM, 16], F32, kind="ExternalInput")
    wc_h = d("w_c", [DM, 16], F32, kind="ExternalInput")
    bdelta_h = d("b_delta", [128, 8], F32, kind="ExternalInput")
    convw_h = d("conv_w", [128, 32], F32, kind="ExternalInput")
    convb_h = d("conv_b", [128, 8], F32, kind="ExternalInput")
    alog_h = d("a_log", [128, 128], F32, kind="ExternalInput")
    dd_h = d("dd", [128, 8], F32, kind="ExternalInput")
    sel_h = d("sel", [16, 16 * 128], BF16, kind="ExternalInput")
    ident_h = d("ident", [128, 128], BF16, kind="ExternalInput")
    part_h = d("part", [DM, T], F32, kind="ExternalOutput")
    if debug:
        dbg_delta_h = d("dbg_delta", [128, 8 * TC], F32, kind="ExternalOutput")
        dbg_xc_h = d("dbg_xc", [128, 8 * TC], F32, kind="ExternalOutput")
        dbg_xnt_h = d("dbg_xnt", [128, 8 * TC], BF16, kind="ExternalOutput")
        dbg_bmt_h = d("dbg_bmt", [16, TC], BF16, kind="ExternalOutput")
        dbg_cmt_h = d("dbg_cmt", [16, TC], BF16, kind="ExternalOutput")
        dbg_ybf_h = d("dbg_ybf", [128, 8 * TC], BF16, kind="ExternalOutput")
        dbg_brep_h = d("dbg_brep", [128, 8 * TC], BF16, kind="ExternalOutput")

    if debug:
        d2_ = dict(dbg_delta=dbg_delta_h, dbg_xc=dbg_xc_h, dbg_xnt=dbg_xnt_h, dbg_bmt=dbg_bmt_h,
                   dbg_cmt=dbg_cmt_h, dbg_ybf=dbg_ybf_h, dbg_brep=dbg_brep_h)
    es = ExitStack()
    sb, ps, sem = mk_ctx(nc, es)

    with es:
        w_g = sb("w_g", [128, 8 * 1024], BF16)
        w_i = sb("w_i", [128, 8 * 1024], BF16)
        w_d = sb("w_d", [128, 8 * 1024], BF16)
        w_o = sb("w_o", [128, 8 * 1024], BF16)
        w_b = sb("s_w_b", [128, 8 * 16], BF16)
        w_c = sb("s_w_c", [128, 8 * 16], BF16)
        sel = sb("s_sel", [16, 16 * 128], BF16)
        ident = sb("s_ident", [128, 128], BF16)
        a_sb = sb("a_sb", [128, 128], F32)
        bdelta = sb("s_bdelta", [128, 8], F32)
        convw = sb("s_convw", [128, 32], F32)
        convb = sb("s_convb", [128, 8], F32)
        dd = sb("s_dd", [128, 8], F32)
        stage = sb("stage", [128, 2048], F32)
        xx = sb("xx", [128, 1024], F32)
        xn = sb("xn", [128, 1024], BF16)
        mv = sb("mv", [128, 2], F32)
        lnc = sb("lnc", [128, 16], F32)
        lnjunk = sb("lnjunk", [128, 1024], F32)
        rstd = sb("rstd", [128, 1], F32)
        xnT = sb("xnT", [128, 8 * TC], BF16)
        delta = sb("delta", [128, 8 * TC], F32)
        sp1 = sb("sp1", [128, TC], F32)
        sp2 = sb("sp2", [128, TC], F32)
        xing = sb("xing", [128, 8 * (TC + 3)], F32)
        xc = sb("xc", [128, 8 * TC], F32)
        gsil = sb("gsil", [128, TC], F32)
        bmt = sb("bmt", [16, TC], BF16)
        cmt = sb("cmt", [16, TC], BF16)
        brep = sb("brep", [128, 8 * TC], BF16)
        crep = sb("crep", [128, 8 * TC], BF16)
        wv = [sb(f"wv{i}", [128, TC], F32) for i in range(2)]
        ab = [sb(f"ab{i}", [128, TC], F32) for i in range(2)]
        bb = [sb("bb0", [128, TC], F32)] * 2
        hb = [sb("hb0", [128, TC], F32)] * 2
        pb = [sb(f"pb{i}", [128, TC], BF16) for i in range(2)]
        state = sb("state", [128, 128], F32)
        cst = sb("cst", [128, 2], F32)
        y_acc = sb("y_acc", [128, 8 * TC], F32)
        y_bf = sb("y_bf", [128, 8 * TC], BF16)

        pp = [ps(f"pp{i}", [128, 512], F32) for i in range(2)]
        py = [ps(f"py{i}", [128, 512], F32) for i in range(2)]
        pbc = [ps(f"pbc{i}", [128, 512], F32) for i in range(2)]
        pbm = ps("pbm", [16, 512], F32)
        ptp = ps("ptp", [128, 512], BF16)

        s_ld = sem("s_ld")
        s_ld2 = sem("s_ld2")
        s_lda = sem("s_lda")
        s_st = sem("s_st")
        s_ve = sem("s_ve")
        s_sc = sem("s_sc")
        s_te = sem("s_te")
        allsems = (s_ld, s_ld2, s_lda, s_st, s_ve, s_sc, s_te)

        p = Prog(nc)
        E = p.emit

        def load_weight(whbm, wsb, ncols):
            for q in range(4):
                src = whbm[q * 256:(q + 1) * 256, :].rearrange("(kc p) o -> p kc o", p=128)
                p.wait("sync", s_sc)
                E("sync", lambda e, s=src, n=ncols: e.dma_start(out=stage[:, 0:2 * n].rearrange("p (a b) -> p a b", a=2), in_=s).then_inc(s_ld.h, 16))
                s_ld.inc(16)
                p.wait("scalar", s_ld)
                for kc in range(2):
                    kca = q * 2 + kc
                    last = kc == 1
                    E("scalar", lambda e, kca=kca, kc=kc, n=ncols, wsb=wsb, last=last: (
                        e.copy(wsb[:, kca * n:(kca + 1) * n], stage[:, kc * n:(kc + 1) * n]).then_inc(s_sc.h, 1)
                        if last else
                        e.copy(wsb[:, kca * n:(kca + 1) * n], stage[:, kc * n:(kc + 1) * n])))
                s_sc.inc()

        load_weight(wg_h, w_g, 1024)
        load_weight(wi_h, w_i, 1024)
        load_weight(wd_h, w_d, 1024)
        load_weight(wo_h, w_o, 1024)
        load_weight(wb_h, w_b, 16)
        load_weight(wc_h, w_c, 16)

        def simple_load(src, dst):
            E("sync", lambda e, s=src, dst=dst: e.dma_start(out=dst, in_=s).then_inc(s_ld2.h, 16))
            s_ld2.inc(16)

        simple_load(sel_h, sel[:, :])
        simple_load(ident_h, ident[:, :])
        simple_load(bdelta_h, bdelta[:, :])
        simple_load(convw_h, convw[:, :])
        simple_load(convb_h, convb[:, :])
        simple_load(dd_h, dd[:, :])
        simple_load(alog_h, a_sb[:, :])
        p.wait("scalar", s_ld)
        p.wait("scalar", s_ld2)
        p.wait("vector", s_ld2)
        p.wait("tensor", s_ld2)
        p.wait("tensor", s_ld)
        E("scalar", lambda e: e.activation(a_sb[:, :], a_sb[:, :], AF.Exp).then_inc(s_sc.h, 1))
        s_sc.inc()
        p.wait("vector", s_sc)
        E("vector", lambda e: e.tensor_scalar_mul(a_sb[:, :], a_sb[:, :], -1.0))
        E("vector", lambda e: e.memset(cst[:, 0:1], 1e-5))
        E("vector", lambda e: e.memset(cst[:, 1:2], 1.0))
        E("vector", lambda e: e.memset(state[:, :], 0.0))
        E("vector", lambda e: e.memset(xing[:, :], 0.0).then_inc(s_ve.h, 1))
        s_ve.inc()
        p.wait("vector", s_ve)
        p.wait("scalar", s_ve)
        prewarm_tables(p, E, s_sc, sp1[:, 0:1], [AF.Exp, AF.Ln, AF.Silu, AF.Relu, AF.Abs, AF.Square])

        for c in range(NCH):
            p.barrier(allsems)

            for i in range(NT):
                t0 = c * TC + i * 128
                p.wait("sync", s_ve)
                p.wait("sync", s_te)
                p.wait("sync", s_sc)
                E("sync", lambda e, t0=t0: e.dma_start(out=xx[:, :], in_=x_h[t0:t0 + 128, :]).then_inc(s_lda.h, 16))
                s_lda.inc(16)
                p.wait("vector", s_lda)
                p.wait("vector", s_te)
                p.wait("vector", s_sc)
                p.wait("scalar", s_lda)
                p.wait("scalar", s_ve)
                mean_ap, rstd_ap = ln_core(p, E, s_ve, s_sc, [(xx[:, :], 1024)], lnjunk, lnc, cst[:, 0:1])
                p.wait("vector", s_sc)
                E("vector", lambda e, mean_ap=mean_ap, rstd_ap=rstd_ap: e.tensor_scalar(
                    out=xn[:, :], in0=xx[:, :], scalar1=mean_ap,
                    scalar2=rstd_ap, op0=ALU.subtract, op1=ALU.mult).then_inc(s_ve.h, 1))
                s_ve.inc()
                p.wait("tensor", s_ve)
                for k in range(8):
                    slot = k % 4
                    p.wait("tensor", s_sc)
                    E("tensor", lambda e, k=k, slot=slot: e.transpose(
                        ptp[:, slot * 128:(slot + 1) * 128], xn[:, k * 128:(k + 1) * 128],
                        ident[:, :]).then_inc(s_te.h, 1))
                    s_te.inc()
                    p.wait("scalar", s_te)
                    E("scalar", lambda e, k=k, slot=slot, i=i: e.copy(
                        xnT[:, k * TC + i * 128:k * TC + (i + 1) * 128],
                        ptp[:, slot * 128:(slot + 1) * 128]).then_inc(s_sc.h, 1))
                    s_sc.inc()
            xnT_ready = s_sc.v

            def mm_proj(wsb, oc, bank, extra_wait=()):
                for kc in range(8):
                    if kc == 0:
                        p.wait("tensor", s_sc, xnT_ready)
                        p.wait("tensor", s_ve)
                        p.wait("tensor", s_sc)
                        for (ss, vv) in extra_wait:
                            p.wait("tensor", ss, vv)
                    E("tensor", lambda e, wsb=wsb, oc=oc, kc=kc, bank=bank, last=(kc == 7): (
                        e.matmul(pp[bank][:, :], wsb[:, kc * 1024 + oc * 128:kc * 1024 + (oc + 1) * 128],
                                 xnT[:, kc * TC:(kc + 1) * TC], start=(kc == 0), stop=last).then_inc(s_te.h, 1)
                        if last else
                        e.matmul(pp[bank][:, :], wsb[:, kc * 1024 + oc * 128:kc * 1024 + (oc + 1) * 128],
                                 xnT[:, kc * TC:(kc + 1) * TC], start=(kc == 0), stop=last)))
                s_te.inc()

            for oc in range(8):
                mm_proj(w_g, oc, 0)
                p.wait("scalar", s_te)
                p.wait("scalar", s_ve)
                E("scalar", lambda e: e.activation(gsil[:, :], pp[0][:, :], AF.Silu).then_inc(s_sc.h, 1))
                s_sc.inc()
                mm_proj(w_i, oc, 1)
                p.wait("vector", s_te)
                p.wait("vector", s_sc)
                E("vector", lambda e, oc=oc: e.tensor_mul(
                    xing[:, oc * (TC + 3) + 3:oc * (TC + 3) + 3 + TC], gsil[:, :],
                    pp[1][:, :]).then_inc(s_ve.h, 1))
                s_ve.inc()

            for oc in range(8):
                bank = oc % 2
                mm_proj(w_d, oc, bank)
                p.wait("scalar", s_te)
                p.wait("scalar", s_ve)
                E("scalar", lambda e, oc=oc, bank=bank: e.activation(sp1[:, :], pp[bank][:, :], AF.Relu, bias=bdelta[:, oc:oc + 1]))
                E("scalar", lambda e, oc=oc, bank=bank: e.activation(sp2[:, :], pp[bank][:, :], AF.Abs, bias=bdelta[:, oc:oc + 1]))
                E("scalar", lambda e: e.activation(sp2[:, :], sp2[:, :], AF.Exp, scale=-1.0))
                E("scalar", lambda e: e.activation(sp2[:, :], sp2[:, :], AF.Ln, bias=cst[:, 1:2]).then_inc(s_sc.h, 1))
                s_sc.inc()
                p.wait("vector", s_sc)
                E("vector", lambda e, oc=oc: e.tensor_add(delta[:, oc * TC:(oc + 1) * TC], sp1[:, :], sp2[:, :]).then_inc(s_ve.h, 1))
                s_ve.inc()

            for wsb, dstt in ((w_b, bmt), (w_c, cmt)):
                for kc in range(8):
                    if kc == 0:
                        p.wait("tensor", s_sc)
                    E("tensor", lambda e, wsb=wsb, kc=kc, last=(kc == 7): (
                        e.matmul(pbm[:, :TC], wsb[:, kc * 16:(kc + 1) * 16],
                                 xnT[:, kc * TC:(kc + 1) * TC], start=(kc == 0), stop=last).then_inc(s_te.h, 1)
                        if last else
                        e.matmul(pbm[:, :TC], wsb[:, kc * 16:(kc + 1) * 16],
                                 xnT[:, kc * TC:(kc + 1) * TC], start=(kc == 0), stop=last)))
                s_te.inc()
                p.wait("scalar", s_te)
                E("scalar", lambda e, dstt=dstt: e.copy(dstt[:, :], pbm[:, :TC]).then_inc(s_sc.h, 1))
                s_sc.inc()

            p.wait("vector", s_ve)
            for dt in range(8):
                base = dt * (TC + 3)
                E("vector", lambda e, dt=dt, base=base: e.tensor_scalar(
                    out=xc[:, dt * TC:(dt + 1) * TC], in0=xing[:, base:base + TC],
                    scalar1=convw[:, dt * 4:dt * 4 + 1], scalar2=convb[:, dt:dt + 1],
                    op0=ALU.mult, op1=ALU.add))
                for j in range(1, 4):
                    E("vector", lambda e, dt=dt, base=base, j=j: e.scalar_tensor_tensor(
                        out=xc[:, dt * TC:(dt + 1) * TC], in0=xing[:, base + j:base + j + TC],
                        scalar=convw[:, dt * 4 + j:dt * 4 + j + 1], in1=xc[:, dt * TC:(dt + 1) * TC],
                        op0=ALU.mult, op1=ALU.add))
            for dt in range(8):
                base = dt * (TC + 3)
                E("vector", lambda e, base=base: e.tensor_copy(xing[:, base:base + 3], xing[:, base + TC:base + TC + 3]))
            E("vector", lambda e: e.nop().then_inc(s_ve.h, 1))
            s_ve.inc()
            xc_ready = s_ve.v

            for g in range(2):
                for k8 in range(8):
                    n = g * 8 + k8
                    for which, src, dstrep in ((0, bmt, brep), (1, cmt, crep)):
                        bank = (2 * k8 + which) % 2
                        p.wait("tensor", s_sc)
                        E("tensor", lambda e, n=n, bank=bank, src=src: e.matmul(
                            pbc[bank][:, :TC], sel[:, n * 128:(n + 1) * 128], src[:, :],
                            start=True, stop=True).then_inc(s_te.h, 1))
                        s_te.inc()
                        p.wait("scalar", s_te)
                        E("scalar", lambda e, k8=k8, bank=bank, dstrep=dstrep: e.copy(
                            dstrep[:, k8 * TC:(k8 + 1) * TC], pbc[bank][:, :TC]).then_inc(s_sc.h, 1))
                        s_sc.inc()
                rep_ready = s_sc.v

                for dt in range(8):
                    ybank = dt % 2
                    wbuf = dt % 2
                    p.wait("vector", s_ve, xc_ready)
                    p.wait("vector", s_sc, rep_ready)
                    E("vector", lambda e, dt=dt, wbuf=wbuf: e.tensor_mul(
                        wv[wbuf][:, :], delta[:, dt * TC:(dt + 1) * TC], xc[:, dt * TC:(dt + 1) * TC]))
                    for k8 in range(8):
                        n = g * 8 + k8
                        buf = k8 % 2
                        col = dt * 16 + n
                        p.wait("scalar", s_ve)
                        E("scalar", lambda e, dt=dt, col=col, buf=buf: e.activation(
                            ab[buf][:, :], delta[:, dt * TC:(dt + 1) * TC], AF.Exp,
                            scale=a_sb[:, col:col + 1]).then_inc(s_sc.h, 1))
                        s_sc.inc()
                        E("vector", lambda e, k8=k8, buf=buf, wbuf=wbuf: e.tensor_mul(
                            bb[buf][:, :], wv[wbuf][:, :], brep[:, k8 * TC:(k8 + 1) * TC]))
                        p.wait("vector", s_sc)
                        E("vector", lambda e, buf=buf, col=col: e.tensor_tensor_scan(
                            hb[buf][:, :], ab[buf][:, :], bb[buf][:, :], state[:, col:col + 1],
                            ALU.mult, ALU.add))
                        p.wait("vector", s_te)
                        E("vector", lambda e, buf=buf, k8=k8: e.tensor_mul(
                            pb[buf][:, :], crep[:, k8 * TC:(k8 + 1) * TC], hb[buf][:, :]).then_inc(s_ve.h, 1))
                        s_ve.inc()
                        E("vector", lambda e, buf=buf, col=col: e.tensor_copy(
                            state[:, col:col + 1], hb[buf][:, TC - 1:TC]))
                        p.wait("tensor", s_ve)
                        E("tensor", lambda e, buf=buf, ybank=ybank, k8=k8: e.matmul(
                            py[ybank][:, :TC], ident[:, :], pb[buf][:, :],
                            start=(k8 == 0), stop=(k8 == 7)).then_inc(s_te.h, 1))
                        s_te.inc()
                    p.wait("vector", s_te)
                    if g == 0:
                        E("vector", lambda e, dt=dt, ybank=ybank: e.scalar_tensor_tensor(
                            out=y_acc[:, dt * TC:(dt + 1) * TC], in0=xc[:, dt * TC:(dt + 1) * TC],
                            scalar=dd[:, dt:dt + 1], in1=py[ybank][:, :TC],
                            op0=ALU.mult, op1=ALU.add).then_inc(s_ve.h, 1))
                    else:
                        E("vector", lambda e, dt=dt, ybank=ybank: e.tensor_add(
                            y_bf[:, dt * TC:(dt + 1) * TC], y_acc[:, dt * TC:(dt + 1) * TC],
                            py[ybank][:, :TC]).then_inc(s_ve.h, 1))
                    s_ve.inc()
            y_ready = s_ve.v

            if debug and c == 0:
                p.wait("gpsimd", s_ve, y_ready)
                p.wait("gpsimd", s_sc)
                for nm, srcb in (("dbg_delta", delta), ("dbg_xc", xc), ("dbg_xnt", xnT),
                                  ("dbg_ybf", y_bf), ("dbg_brep", brep)):
                    E("gpsimd", lambda e, nm=nm, srcb=srcb: e.dma_start(
                        d2_[nm], srcb[:, :]).then_inc(s_st.h, 16))
                    s_st.inc(16)
                for nm, srcb in (("dbg_bmt", bmt), ("dbg_cmt", cmt)):
                    E("gpsimd", lambda e, nm=nm, srcb=srcb: e.dma_start(
                        d2_[nm], srcb[:, :]).then_inc(s_st.h, 16))
                    s_st.inc(16)

            for oc in range(8):
                bank = oc % 2
                for kc in range(8):
                    if kc == 0:
                        p.wait("tensor", s_ve, y_ready)
                        p.wait("tensor", s_sc)
                        p.wait("tensor", s_st)
                    E("tensor", lambda e, oc=oc, kc=kc, bank=bank, last=(kc == 7): (
                        e.matmul(pp[bank][:, :], w_o[:, kc * 1024 + oc * 128:kc * 1024 + (oc + 1) * 128],
                                 y_bf[:, kc * TC:(kc + 1) * TC], start=(kc == 0), stop=last).then_inc(s_te.h, 1)
                        if last else
                        e.matmul(pp[bank][:, :], w_o[:, kc * 1024 + oc * 128:kc * 1024 + (oc + 1) * 128],
                                 y_bf[:, kc * TC:(kc + 1) * TC], start=(kc == 0), stop=last)))
                s_te.inc()
                p.wait("scalar", s_te)
                E("scalar", lambda e, oc=oc, bank=bank: e.copy(stage[:, (oc % 4) * TC:(oc % 4 + 1) * TC], pp[bank][:, :]).then_inc(s_sc.h, 1))
                s_sc.inc()
                if oc % 4 == 3:
                    p.wait("gpsimd", s_sc)
                    grp = oc // 4
                    dst = part_h[grp * 512:(grp + 1) * 512, c * TC:(c + 1) * TC].rearrange("(oc p) t -> p oc t", p=128)
                    E("gpsimd", lambda e, dst=dst: e.dma_start(dst, stage[:, :].rearrange("p (a b) -> p a b", a=4)).then_inc(s_st.h, 16))
                    s_st.inc(16)

        p.wait("gpsimd", s_st)

        with nc.Block() as block:
            p.attach(block)
    return nc


def l0_attn_inputs(inp, core):
    b, h = core // 2, core % 2
    sl = slice(h * 1024, (h + 1) * 1024)
    x = np.ascontiguousarray(inp["L0_attn_x"][b])
    w_in = inp["L0_attn_W_in"]
    w_in_g = np.ascontiguousarray(w_in[:, sl])
    w_in_i = np.ascontiguousarray(w_in[:, 2048 + h * 1024:2048 + (h + 1) * 1024])
    w_delta = np.ascontiguousarray(inp["L0_attn_W_delta"][:, sl])
    w_out = np.ascontiguousarray(inp["L0_attn_W_out"][sl, :])
    w_b = np.ascontiguousarray(inp["L0_attn_W_B"])
    w_c = np.ascontiguousarray(inp["L0_attn_W_C"])

    def col128(v):
        return np.ascontiguousarray(v.reshape(8, 128).T)

    b_delta = col128(inp["L0_attn_b_delta"][sl])
    conv_b = col128(inp["L0_attn_conv_b"][sl])
    ddv = col128(inp["L0_attn_D"][sl])
    cw = inp["L0_attn_conv_w"][sl, 0, :]
    conv_w = np.ascontiguousarray(cw.reshape(8, 128, 4).transpose(1, 0, 2).reshape(128, 32))
    al = inp["L0_attn_A_log"][sl, :]
    a_log = np.ascontiguousarray(al.reshape(8, 128, 16).transpose(1, 0, 2).reshape(128, 128))
    import ml_dtypes
    selm = np.zeros((16, 16 * 128), np.float32)
    for n in range(16):
        selm[n, n * 128:(n + 1) * 128] = 1.0
    sel = selm.astype(ml_dtypes.bfloat16)
    ident = np.eye(128, dtype=np.float32).astype(ml_dtypes.bfloat16)
    return dict(x=x, w_in_g=w_in_g, w_in_i=w_in_i, w_delta=w_delta, w_out=w_out,
                w_b=w_b, w_c=w_c, b_delta=b_delta, conv_w=conv_w, conv_b=conv_b,
                a_log=a_log, dd=ddv, sel=sel, ident=ident)


def bcast_free(tile_ap, p, reps, inner, col0=0):
    t = tile_ap.tensor if hasattr(tile_ap, "tensor") else tile_ap
    row = t.shape[1] if hasattr(t, "shape") else None
    return bass.AP(t, col0, [[row, p], [0, reps], [1, inner]])


def build_ffn_a(do_ln, resid3, TE=3584, OWN=2048):
    NW = TE // 512
    NT = TE // 128
    nc = bass.Bass("TRN2", target_bir_lowering=False)
    d = lambda *a, **k: nc.dram_tensor(*a, **k).ap()
    xin_h = d("xin", [TE, 1024], F32, kind="ExternalInput")
    if resid3:
        p0_h = d("p0", [TE, 1024], F32, kind="ExternalInput")
        p1_h = d("p1", [TE, 1024], F32, kind="ExternalInput")
        x1out_h = d("x1out", [OWN, 1024], F32, kind="ExternalOutput")
    wqkv_h = d("wqkv", [1024, 1536], F32, kind="ExternalInput")
    wo_h = d("wo", [512, 1024], F32, kind="ExternalInput")
    qsr_h = d("qsr", [128, 64], F32, kind="ExternalInput")
    ksr_h = d("ksr", [128, 64], F32, kind="ExternalInput")
    qsrr_h = d("qsrr", [128, 64], F32, kind="ExternalInput")
    ksrr_h = d("ksrr", [128, 64], F32, kind="ExternalInput")
    csq_h = d("csq", [128, NT * 64], BF16, kind="ExternalInput")
    snq_h = d("snq", [128, NT * 64], BF16, kind="ExternalInput")
    csk_h = d("csk", [128, NT * 64], BF16, kind="ExternalInput")
    snk_h = d("snk", [128, NT * 64], BF16, kind="ExternalInput")
    lmask_h = d("lmask", [128, 8 * 512], BF16, kind="ExternalInput")
    kpad_h = d("kpad", [128, NT], F32, kind="ExternalInput")
    ident_h = d("ident", [128, 128], BF16, kind="ExternalInput")
    za_h = d("za", [1024, 5 * 512], F32, kind="ExternalOutput")

    es = ExitStack()
    sb, ps, sem = mk_ctx(nc, es)
    with es:
        wqkv = sb("s_wqkv", [128, 8 * 1536], BF16)
        wo = sb("s_wo", [128, 4 * 1024], BF16)
        qsr = sb("s_qsr", [128, 64], F32)
        ksr = sb("s_ksr", [128, 64], F32)
        qsrr = sb("s_qsrr", [128, 64], F32)
        ksrr = sb("s_ksrr", [128, 64], F32)
        csq = sb("s_csq", [128, NT * 64], BF16)
        snq = sb("s_snq", [128, NT * 64], BF16)
        csk = sb("s_csk", [128, NT * 64], BF16)
        snk = sb("s_snk", [128, NT * 64], BF16)
        lmask = sb("s_lmask", [128, 8 * 512], BF16)
        kpad = sb("s_kpad", [128, NT], F32)
        kbias = sb("s_kbias", [128, NT], F32)
        ident = sb("s_ident", [128, 128], BF16)
        ncol = sb("ncol", [128, 1], F32)
        mq = sb("mq", [128, 1], F32)
        mk_ = sb("mk_", [128, 1], F32)
        ones64 = sb("ones64", [1, 64], BF16)
        onesc = sb("onesc", [128, 1], BF16)
        stage = sb("stage", [128, 1536], F32)
        cst = sb("cst", [128, 4], F32)
        xx = sb("xx", [128, 1024], F32)
        xx2 = sb("xx2", [128, 1024], F32)
        lnc = sb("lnc", [128, 16], F32)
        lnjunk = sb("lnjunk", [128, 1024], F32)
        mv = sb("mv", [128, 2], F32)
        rstd = sb("rstd", [128, 1], F32)
        zn = sb("zn", [128, 1024], BF16)
        zT = sb("zT", [128, 8 * TE], BF16)
        q_tm = sb("q_tm", [128, 512], F32)
        k_tm = sb("k_tm", [128, 512], F32)
        sq = sb("sq", [128, 512], F32)
        nrm = sb("nrm", [128, 8], F32)
        rs = sb("rs", [128, 8], F32)
        rotb = sb("rotb", [128, 512], F32)
        t1b = sb("t1b", [128, 512], F32)
        qkb = sb("qkb", [128, 512], BF16)
        qT = sb("qT", [128, 4 * 512], BF16)
        kT = sb("kT", [128, 2 * 4 * 512], BF16)
        v_sb = sb("v_sb", [128, 8 * 512], BF16)
        eb = [sb(f"eb{i}", [128, 512], BF16) for i in range(2)]
        ctxf = sb("ctxf", [128, 512], F32)
        rdf = sb("rdf", [1, 512], F32)
        rdb = sb("rdb", [1, 512], BF16)
        ctx = sb("ctx", [128, 4 * 512], BF16)
        za_ring = sb("za_ring", [128, 8 * 512], F32)

        pqk = [ps(f"pqk{i}", [128, 512], F32) for i in range(2)]
        psc = [ps(f"psc{i}", [128, 512], F32) for i in range(2)]
        pav = [ps(f"pav{i}", [128, 512], F32) for i in range(2)]
        pbc = ps("pbc", [128, 512], F32)
        ptp = ps("ptp", [128, 512], BF16)

        s_ld = sem("s_ld")
        s_ld2 = sem("s_ld2")
        s_lda = sem("s_lda")
        s_st = sem("s_st")
        s_st2 = sem("s_st2")
        s_ve = sem("s_ve")
        s_sc = sem("s_sc")
        s_te = sem("s_te")

        p = Prog(nc)
        E = p.emit

        def vgap():
            E("vector", lambda e: e.nop().then_inc(s_ve.h, 1))
            s_ve.inc()
            p.wait("vector", s_ve)

        def load_cast(whbm, wsb, nrows, ncols):
            kcs = nrows // 128
            done = 0
            while done < kcs:
                step = max(1, min(kcs - done, 1536 // ncols))
                srcv = whbm[done * 128:(done + step) * 128, :].rearrange("(kc p) o -> p kc o", p=128)
                p.wait("sync", s_sc)
                E("sync", lambda e, srcv=srcv, step=step, ncols=ncols: e.dma_start(
                    out=stage[:, 0:step * ncols].rearrange("p (a b) -> p a b", a=step), in_=srcv).then_inc(s_ld.h, 16))
                s_ld.inc(16)
                p.wait("scalar", s_ld)
                for j in range(step):
                    E("scalar", lambda e, j=j, done=done, ncols=ncols, wsb=wsb, last=(j == step - 1): (
                        e.copy(wsb[:, (done + j) * ncols:(done + j + 1) * ncols],
                               stage[:, j * ncols:(j + 1) * ncols]).then_inc(s_sc.h, 1)
                        if last else
                        e.copy(wsb[:, (done + j) * ncols:(done + j + 1) * ncols],
                               stage[:, j * ncols:(j + 1) * ncols])))
                s_sc.inc()
                done += step

        load_cast(wqkv_h, wqkv, 1024, 1536)
        load_cast(wo_h, wo, 512, 1024)

        def simple_load(src, dst):
            E("sync", lambda e, s=src, dst=dst: e.dma_start(out=dst, in_=s).then_inc(s_ld2.h, 16))
            s_ld2.inc(16)

        for srch, dst in ((qsr_h, qsr), (ksr_h, ksr), (qsrr_h, qsrr), (ksrr_h, ksrr), (csq_h, csq), (snq_h, snq), (csk_h, csk),
                          (snk_h, snk), (lmask_h, lmask), (kpad_h, kpad), (ident_h, ident)):
            simple_load(srch, dst[:, :])
        p.wait("vector", s_ld)
        p.wait("vector", s_ld2)
        p.wait("scalar", s_ld2)
        p.wait("tensor", s_ld2)
        E("vector", lambda e: e.memset(cst[:, 0:1], 1e-5))
        E("vector", lambda e: e.memset(cst[:, 1:2], 1e-18))
        E("vector", lambda e: e.memset(cst[:, 2:3], 1e-19))
        E("vector", lambda e: e.memset(ones64[:, :], 1.0))
        E("vector", lambda e: e.memset(onesc[:, :], 1.0).then_inc(s_ve.h, 1))
        s_ve.inc()
        p.wait("vector", s_ve)
        E("vector", lambda e: e.tensor_reduce(mq[:, :], qsr[:, :], mybir.AxisListType.X, ALU.max, apply_absolute_value=True).then_inc(s_ve.h, 1))
        s_ve.inc()
        E("vector", lambda e: e.tensor_reduce(mk_[:, :], ksr[:, :], mybir.AxisListType.X, ALU.max, apply_absolute_value=True).then_inc(s_ve.h, 1))
        s_ve.inc()
        p.wait("vector", s_ve)
        E("vector", lambda e: e.tensor_scalar(out=ncol[:, :], in0=mq[:, :], scalar1=mk_[:, 0:1],
                                              scalar2=-8.0, op0=ALU.mult, op1=ALU.mult).then_inc(s_ve.h, 1))
        s_ve.inc()
        p.wait("vector", s_ve)
        E("vector", lambda e: e.tensor_scalar(out=kbias[:, :], in0=kpad[:, :], scalar1=ncol[:, 0:1],
                                              scalar2=None, op0=ALU.add, op1=ALU.bypass).then_inc(s_ve.h, 1))
        s_ve.inc()
        E("vector", lambda e: e.tensor_tensor(csq[:, :].rearrange("p (t d) -> p t d", t=NT), csq[:, :].rearrange("p (t d) -> p t d", t=NT), bcast_free(qsr, 128, NT, 64), ALU.mult))
        E("vector", lambda e: e.tensor_tensor(snq[:, :].rearrange("p (t d) -> p t d", t=NT), snq[:, :].rearrange("p (t d) -> p t d", t=NT), bcast_free(qsrr, 128, NT, 64), ALU.mult))
        E("vector", lambda e: e.tensor_tensor(csk[:, :].rearrange("p (t d) -> p t d", t=NT), csk[:, :].rearrange("p (t d) -> p t d", t=NT), bcast_free(ksr, 128, NT, 64), ALU.mult))
        E("vector", lambda e: e.tensor_tensor(snk[:, :].rearrange("p (t d) -> p t d", t=NT), snk[:, :].rearrange("p (t d) -> p t d", t=NT), bcast_free(ksrr, 128, NT, 64), ALU.mult).then_inc(s_ve.h, 1))
        s_ve.inc()
        p.wait("vector", s_ve)
        p.wait("scalar", s_ve)
        prewarm_tables(p, E, s_sc, sq[:, 0:1], [AF.Exp, AF.Ln, AF.Square, AF.Abs])

        for i in range(NT):
            p.wait("sync", s_ve)
            p.wait("sync", s_te)
            p.wait("sync", s_st)
            p.wait("sync", s_sc)
            E("sync", lambda e, i=i: e.dma_start(out=xx[:, :], in_=xin_h[i * 128:(i + 1) * 128, :]).then_inc(s_lda.h, 16))
            s_lda.inc(16)
            if resid3:
                E("sync", lambda e, i=i: e.dma_start(out=xx2[:, :], in_=p0_h[i * 128:(i + 1) * 128, :]).then_inc(s_lda.h, 16))
                s_lda.inc(16)
                p.wait("vector", s_lda)
                E("vector", lambda e: e.tensor_add(xx[:, :], xx[:, :], xx2[:, :]).then_inc(s_ve.h, 1))
                s_ve.inc()
                p.wait("sync", s_ve)
                E("sync", lambda e, i=i: e.dma_start(out=xx2[:, :], in_=p1_h[i * 128:(i + 1) * 128, :]).then_inc(s_lda.h, 16))
                s_lda.inc(16)
                p.wait("vector", s_lda)
                E("vector", lambda e: e.tensor_add(xx[:, :], xx[:, :], xx2[:, :]).then_inc(s_ve.h, 1))
                s_ve.inc()
                if i >= NT - OWN // 128:
                    io = i - (NT - OWN // 128)
                    p.wait("gpsimd", s_ve)
                    E("gpsimd", lambda e, io=io: e.dma_start(x1out_h[io * 128:(io + 1) * 128, :], xx[:, :]).then_inc(s_st.h, 16))
                    s_st.inc(16)
                    p.wait("gpsimd", s_st)
            p.wait("vector", s_lda)
            p.wait("vector", s_te)
            p.wait("vector", s_sc)
            if do_ln:
                p.wait("scalar", s_lda)
                p.wait("scalar", s_ve)
                mean_ap, rstd_ap = ln_core(p, E, s_ve, s_sc, [(xx[:, :], 1024)], lnjunk, lnc, cst[:, 0:1])
                p.wait("vector", s_sc)
                E("vector", lambda e, mean_ap=mean_ap, rstd_ap=rstd_ap: e.tensor_scalar(
                    out=zn[:, :], in0=xx[:, :], scalar1=mean_ap,
                    scalar2=rstd_ap, op0=ALU.subtract, op1=ALU.mult).then_inc(s_ve.h, 1))
                s_ve.inc()
            else:
                E("vector", lambda e: e.tensor_copy(zn[:, :], xx[:, :]).then_inc(s_ve.h, 1))
                s_ve.inc()
            p.wait("tensor", s_ve)
            for k in range(8):
                slot = k % 4
                p.wait("tensor", s_sc)
                E("tensor", lambda e, k=k, slot=slot: e.transpose(
                    ptp[:, slot * 128:(slot + 1) * 128], zn[:, k * 128:(k + 1) * 128],
                    ident[:, :]).then_inc(s_te.h, 1))
                s_te.inc()
                p.wait("scalar", s_te)
                E("scalar", lambda e, k=k, slot=slot, i=i: e.copy(
                    zT[:, k * TE + i * 128:k * TE + (i + 1) * 128],
                    ptp[:, slot * 128:(slot + 1) * 128]).then_inc(s_sc.h, 1))
                s_sc.inc()
        zT_ready = s_sc.v

        def qkv_mm(ext_tile, dg, bank):
            for kc in range(8):
                if kc == 0:
                    p.wait("tensor", s_sc, zT_ready)
                    p.wait("tensor", s_ve)
                    p.wait("tensor", s_sc)
                off = -1 if kc >= 4 else 0
                col = kc * TE + ext_tile * 128 + off
                E("tensor", lambda e, kc=kc, col=col, dg=dg, bank=bank, last=(kc == 7): (
                    e.matmul(pqk[bank][:, :], zT[:, col:col + 128],
                             wqkv[:, kc * 1536 + dg * 512:kc * 1536 + (dg + 1) * 512],
                             start=(kc == 0), stop=last).then_inc(s_te.h, 1)
                    if last else
                    e.matmul(pqk[bank][:, :], zT[:, col:col + 128],
                             wqkv[:, kc * 1536 + dg * 512:kc * 1536 + (dg + 1) * 512],
                             start=(kc == 0), stop=last)))
            s_te.inc()

        def l2norm_rotary(srcbuf, tile_idx, cs_t, sn_t):
            p.wait("scalar", s_ve)
            E("scalar", lambda e, srcbuf=srcbuf: e.activation(sq[:, :], srcbuf[:, :], AF.Square).then_inc(s_sc.h, 1))
            s_sc.inc()
            p.wait("vector", s_sc)
            E("vector", lambda e: e.tensor_reduce(nrm[:, :], sq[:, :].rearrange("p (h d) -> p h d", h=8), mybir.AxisListType.X, ALU.add).then_inc(s_ve.h, 1))
            s_ve.inc()
            p.wait("scalar", s_ve)
            E("scalar", lambda e: e.activation(rs[:, :], nrm[:, :], AF.Ln, bias=cst[:, 1:2]))
            E("scalar", lambda e: e.activation(rs[:, :], rs[:, :], AF.Exp, scale=-0.5).then_inc(s_sc.h, 1))
            s_sc.inc()
            p.wait("vector", s_sc)
            E("vector", lambda e, srcbuf=srcbuf: e.tensor_tensor(
                sq[:, :].rearrange("p (h d) -> p h d", h=8),
                srcbuf[:, :].rearrange("p (h d) -> p h d", h=8),
                bass.AP(rs.tensor if hasattr(rs, "tensor") else rs, 0, [[8, 128], [1, 8], [0, 64]]),
                ALU.mult))
            E("vector", lambda e: e.tensor_scalar(
                out=rotb[:, :].rearrange("p (h d) -> p h d", h=8)[:, :, 0:32],
                in0=sq[:, :].rearrange("p (h d) -> p h d", h=8)[:, :, 32:64],
                scalar1=-1.0, scalar2=None, op0=ALU.mult, op1=ALU.bypass))
            E("vector", lambda e: e.tensor_scalar(
                out=rotb[:, :].rearrange("p (h d) -> p h d", h=8)[:, :, 32:64],
                in0=sq[:, :].rearrange("p (h d) -> p h d", h=8)[:, :, 0:32],
                scalar1=1.0, scalar2=None, op0=ALU.mult, op1=ALU.bypass))
            E("vector", lambda e, cs_t=cs_t, tile_idx=tile_idx: e.tensor_tensor(
                t1b[:, :].rearrange("p (h d) -> p h d", h=8),
                sq[:, :].rearrange("p (h d) -> p h d", h=8),
                bass.AP(cs_t.tensor if hasattr(cs_t, "tensor") else cs_t, tile_idx * 64, [[NT * 64, 128], [0, 8], [1, 64]]),
                ALU.mult))
            E("vector", lambda e, sn_t=sn_t, tile_idx=tile_idx: e.tensor_tensor(
                rotb[:, :].rearrange("p (h d) -> p h d", h=8),
                rotb[:, :].rearrange("p (h d) -> p h d", h=8),
                bass.AP(sn_t.tensor if hasattr(sn_t, "tensor") else sn_t, tile_idx * 64, [[NT * 64, 128], [0, 8], [1, 64]]),
                ALU.mult))
            E("vector", lambda e: e.tensor_add(sq[:, :], t1b[:, :], rotb[:, :]).then_inc(s_ve.h, 1))
            s_ve.inc()

        def cast_transpose(dst, dstcol_fn, j):
            p.wait("scalar", s_ve)
            p.wait("scalar", s_te)
            E("scalar", lambda e: e.copy(qkb[:, :], sq[:, :]).then_inc(s_sc.h, 1))
            s_sc.inc()
            for cb in range(4):
                slot = cb
                p.wait("tensor", s_sc)
                E("tensor", lambda e, cb=cb, slot=slot: e.transpose(
                    ptp[:, slot * 128:(slot + 1) * 128], qkb[:, cb * 128:(cb + 1) * 128],
                    ident[:, :]).then_inc(s_te.h, 1))
                s_te.inc()
                p.wait("scalar", s_te)
                dcol = dstcol_fn(cb, j)
                E("scalar", lambda e, dcol=dcol, slot=slot, dst=dst: e.copy(
                    dst[:, dcol:dcol + 128], ptp[:, slot * 128:(slot + 1) * 128]).then_inc(s_sc.h, 1))
                s_sc.inc()

        for w in range(1, NW):
            for j in range(4):
                ext_tile = w * 4 + j
                qkv_mm(ext_tile, 2, 0)
                vslot = ext_tile % 8
                p.wait("scalar", s_te)
                E("scalar", lambda e, vslot=vslot: e.copy(v_sb[:, vslot * 512:(vslot + 1) * 512], pqk[0][:, :]).then_inc(s_sc.h, 1))
                s_sc.inc()
                qkv_mm(ext_tile, 1, 1)
                p.wait("scalar", s_te)
                p.wait("scalar", s_ve)
                E("scalar", lambda e: e.copy(k_tm[:, :], pqk[1][:, :]).then_inc(s_sc.h, 1))
                s_sc.inc()
                p.wait("vector", s_sc)
                l2norm_rotary(k_tm, ext_tile, csk, snk)
                cast_transpose(kT, lambda cb, j=j, w=w: (w % 2) * 2048 + cb * 512 + j * 128, j)
                if w >= 2:
                    qkv_mm(ext_tile, 0, 0)
                    p.wait("scalar", s_te)
                    p.wait("scalar", s_ve)
                    E("scalar", lambda e: e.copy(q_tm[:, :], pqk[0][:, :]).then_inc(s_sc.h, 1))
                    s_sc.inc()
                    p.wait("vector", s_sc)
                    l2norm_rotary(q_tm, ext_tile, csq, snq)
                    cast_transpose(qT, lambda cb, j=j: cb * 512 + j * 128, j)
            if w < 2:
                continue
            kvready = s_sc.v

            for hp in range(4):
                for half in range(2):
                    pavb = pav[half]
                    head = 2 * hp + half
                    rowbase = 64 * half
                    for kt in range(8):
                        wk = w - 1 + kt // 4
                        jk = kt % 4
                        ktile_ext = (w - 1) * 4 + kt
                        kcol = (wk % 2) * 2048 + hp * 512 + jk * 128
                        ebuf = eb[kt % 2]
                        scb = psc[kt % 2]
                        p.wait("tensor", s_sc, kvready)
                        p.wait("tensor", s_sc)
                        p.wait("tensor", s_ve)
                        E("tensor", lambda e, kcol=kcol, rowbase=rowbase, scb=scb, hp=hp: e.matmul(
                            scb[:, :], kT[rowbase:rowbase + 64, kcol:kcol + 128],
                            qT[rowbase:rowbase + 64, hp * 512:(hp + 1) * 512],
                            start=True, stop=True).then_inc(s_te.h, 1))
                        s_te.inc()
                        p.wait("vector", s_te)
                        E("vector", lambda e, kt=kt, scb=scb: e.tensor_tensor(
                            scb[:, :], scb[:, :], lmask[:, kt * 512:(kt + 1) * 512], ALU.add).then_inc(s_ve.h, 1))
                        s_ve.inc()
                        p.wait("scalar", s_ve)
                        p.wait("scalar", s_te)
                        E("scalar", lambda e, ebuf=ebuf, scb=scb, ktile_ext=ktile_ext: e.activation(
                            ebuf[:, :], scb[:, :], AF.Exp, scale=8.0, bias=kbias[:, ktile_ext:ktile_ext + 1]).then_inc(s_sc.h, 1))
                        s_sc.inc()
                        p.wait("tensor", s_sc)
                        vslot = ktile_ext % 8
                        E("tensor", lambda e, vslot=vslot, head=head, ebuf=ebuf, pavb=pavb, kt=kt: e.matmul(
                            pavb[0:64, :], v_sb[:, vslot * 512 + head * 64:vslot * 512 + head * 64 + 64],
                            ebuf[:, :], start=(kt == 0), stop=(kt == 7)))
                        E("tensor", lambda e, ebuf=ebuf, kt=kt: e.matmul(
                            pbc[0:1, :], onesc[:, :], ebuf[:, :], start=(kt == 0), stop=(kt == 7)).then_inc(s_te.h, 1))
                        s_te.inc()
                    p.wait("scalar", s_te)
                    p.wait("scalar", s_ve)
                    E("scalar", lambda e, pavb=pavb: e.copy(ctxf[0:64, :], pavb[0:64, :]))
                    E("scalar", lambda e: e.activation(rdf[:, :], pbc[0:1, :], AF.Ln, bias=cst[0:1, 2:3]))
                    E("scalar", lambda e: e.activation(rdb[:, :], rdf[:, :], AF.Exp, scale=-1.0).then_inc(s_sc.h, 1))
                    s_sc.inc()
                    p.wait("tensor", s_sc)
                    p.wait("tensor", s_ve)
                    E("tensor", lambda e: e.matmul(pbc[0:64, :], ones64[:, :], rdb[:, :], start=True, stop=True).then_inc(s_te.h, 1))
                    s_te.inc()
                    p.wait("vector", s_te)
                    E("vector", lambda e, hp=hp, half=half: e.tensor_mul(
                        ctx[64 * half:64 * half + 64, hp * 512:(hp + 1) * 512],
                        ctxf[0:64, :], pbc[0:64, :]).then_inc(s_ve.h, 1))
                    s_ve.inc()
            ctx_ready = s_ve.v

            for oc in range(8):
                bank = oc % 2
                for kc in range(4):
                    if kc == 0:
                        p.wait("tensor", s_ve, ctx_ready)
                        p.wait("tensor", s_ve)
                        p.wait("tensor", s_sc)
                    E("tensor", lambda e, oc=oc, kc=kc, bank=bank, last=(kc == 3): (
                        e.matmul(pqk[bank][:, :], wo[:, kc * 1024 + oc * 128:kc * 1024 + (oc + 1) * 128],
                                 ctx[:, kc * 512:(kc + 1) * 512], start=(kc == 0), stop=last).then_inc(s_te.h, 1)
                        if last else
                        e.matmul(pqk[bank][:, :], wo[:, kc * 1024 + oc * 128:kc * 1024 + (oc + 1) * 128],
                                 ctx[:, kc * 512:(kc + 1) * 512], start=(kc == 0), stop=last)))
                s_te.inc()
                p.wait("vector", s_te)
                p.wait("vector", s_st)
                E("vector", lambda e, oc=oc, bank=bank, w=w: e.tensor_add(
                    za_ring[:, oc * 512:(oc + 1) * 512],
                    pqk[bank][:, :],
                    zT[:, oc * TE + w * 512:oc * TE + (w + 1) * 512]).then_inc(s_ve.h, 1))
                s_ve.inc()
            p.wait("gpsimd", s_ve)
            dst = za_h[:, (w - 2) * 512:(w - 1) * 512].rearrange("(oc p) t -> p oc t", p=128)
            E("gpsimd", lambda e, dst=dst: e.dma_start(
                dst, za_ring[:, :].rearrange("p (oc c) -> p oc c", oc=8)).then_inc(s_st.h, 16))
            s_st.inc(16)
            p.wait("gpsimd", s_st)

        p.wait("gpsimd", s_st)
        with nc.Block() as block:
            p.attach(block)
    return nc


def rotary_tables(TE, estart, NT=None):
    import ml_dtypes
    NT = TE // 128
    pos = np.maximum(np.arange(estart, estart + TE), 0).astype(np.float64)
    half = np.arange(0, 64, 2, dtype=np.float64)
    inv_freq = 1.0 / 10000.0 ** (half / 64.0)
    freqs = pos[:, None] * inv_freq[None, :]
    base = (half + 0.4 * 64) / (1.4 * 64)
    power = (pos - 2048.0) / 256.0
    scale = base[None, :] ** power[:, None]
    cos = np.cos(freqs); sin = np.sin(freqs)
    def dup(a):
        return np.concatenate([a, a], axis=1)
    csq = dup(cos * scale); snq = dup(sin * scale)
    csk = dup(cos / scale); snk = dup(sin / scale)
    def tiles(a):
        return np.ascontiguousarray(a.reshape(NT, 128, 64).transpose(1, 0, 2).reshape(128, NT * 64).astype(np.float32)).astype(ml_dtypes.bfloat16)
    return tiles(csq), tiles(snq), tiles(csk), tiles(snk)


def ffn_a_inputs(inp, ri_or_none, core, layer, TE=3584, OWN=2048):
    import ml_dtypes
    b, h = core // 2, core % 2
    NT = TE // 128
    estart = h * OWN - 1536
    out = {}
    def ext_rows(full):
        r = np.zeros((TE, 1024), np.float32)
        lo = max(estart, 0)
        r[lo - estart:, :] = full[lo:estart + TE, :]
        return r
    if layer == 0:
        out["xin"] = ext_rows(np.asarray(inp["L0_attn_x"][b]))
        out["p0"] = ext_rows(np.ascontiguousarray(inp["_part0"][b]))
        out["p1"] = ext_rows(np.ascontiguousarray(inp["_part1"][b]))
    else:
        out["xin"] = ext_rows(np.ascontiguousarray(inp["_zf0"][b]))
    out["wqkv"] = np.ascontiguousarray(inp["L0_ffn_wqkv"][layer])
    out["wo"] = np.ascontiguousarray(inp["L0_ffn_wo"][layer])
    out["qsr"] = np.ascontiguousarray(np.tile(inp["L0_ffn_q_scale"][layer][None, :], (128, 1)))
    out["ksr"] = np.ascontiguousarray(np.tile(inp["L0_ffn_k_scale"][layer][None, :], (128, 1)))
    qsrot = np.concatenate([inp["L0_ffn_q_scale"][layer][32:], inp["L0_ffn_q_scale"][layer][:32]])
    ksrot = np.concatenate([inp["L0_ffn_k_scale"][layer][32:], inp["L0_ffn_k_scale"][layer][:32]])
    out["qsrr"] = np.ascontiguousarray(np.tile(qsrot[None, :], (128, 1)))
    out["ksrr"] = np.ascontiguousarray(np.tile(ksrot[None, :], (128, 1)))
    csq, snq, csk, snk = rotary_tables(TE, estart)
    out["csq"], out["snq"], out["csk"], out["snk"] = csq, snq, csk, snk
    lm = np.zeros((8, 128, 512), np.float32)
    for kt in range(8):
        i = np.arange(128)[:, None]
        j = np.arange(512)[None, :]
        dd = 512 - kt * 128 - i + j
        lm[kt] = np.where((dd >= 0) & (dd <= 512), 0.0, -30000.0)
    out["lmask"] = np.ascontiguousarray(lm.transpose(1, 0, 2).reshape(128, 8 * 512)).astype(ml_dtypes.bfloat16)
    kp = np.zeros((NT, 128), np.float32)
    npad = max(0, -estart)
    for t in range(NT):
        toks = np.arange(t * 128, t * 128 + 128)
        kp[t] = np.where(toks < npad, -30000.0, 0.0)
    out["kpad"] = np.ascontiguousarray(kp.T)
    out["ident"] = np.eye(128, dtype=np.float32).astype(ml_dtypes.bfloat16)
    return out


def build_ffn_b(final_resid, OWN=2048):
    HID2 = 5460
    HID = 2730
    ACH = [(i * 128, min(128, HID - i * 128)) for i in range((HID + 127) // 128)]
    NCH = OWN // 512
    nc = bass.Bass("TRN2", target_bir_lowering=False)
    d = lambda *a, **k: nc.dram_tensor(*a, **k).ap()
    za_h = d("zain", [OWN + 1, 1024], F32, kind="ExternalInput")
    w1_h = d("w1", [1024, HID2], F32, kind="ExternalInput")
    w2_h = d("w2", [HID, 1024], F32, kind="ExternalInput")
    b1_h = d("b1c", [128, 44], F32, kind="ExternalInput")
    b2_h = d("b2c", [128, 8], F32, kind="ExternalInput")
    lng_h = d("lngc", [128, 8], F32, kind="ExternalInput")
    lnb_h = d("lnbc", [128, 8], F32, kind="ExternalInput")
    ident_h = d("ident", [128, 128], BF16, kind="ExternalInput")
    identf_h = d("identf", [128, 128], F32, kind="ExternalInput")
    if final_resid:
        x1_h = d("x1in", [OWN, 1024], F32, kind="ExternalInput")
    zf_h = d("zf", [1024, OWN], F32, kind="ExternalOutput")


    es = ExitStack()
    sb, ps, sem = mk_ctx(nc, es)
    with es:
        w1 = sb("s_w1", [128, 8 * HID2], BF16)
        w2 = sb("s_w2", [128, 22 * 1024], BF16)
        b1c = sb("s_b1c", [128, 44], F32)
        b2c = sb("s_b2c", [128, 8], F32)
        lngc = sb("s_lngc", [128, 8], F32)
        lnbc = sb("s_lnbc", [128, 8], F32)
        ident = sb("s_ident", [128, 128], BF16)
        identf = sb("s_identf", [128, 128], F32)
        stage = sb("stage", [128, 1408], F32)
        cst = sb("cst", [128, 2], F32)
        za_t = sb("za_t", [128, 1024], F32)
        zsh_t = sb("zsh_t", [128, 1024], F32)
        x1_t = sb("x1_t", [128, 1024], F32)
        zn = sb("zn", [128, 1024], BF16)
        lnjunk = sb("lnjunk", [128, 1024], F32)
        lnc = sb("lnc", [128, 16], F32)
        mv = sb("mv", [128, 4], F32)
        rstd = sb("rstd", [128, 1], F32)
        znT = sb("znT", [128, 8 * 512], BF16)
        zaT = sb("zaT", [128, 8 * 512], F32)
        gl = sb("gl", [128, 512], F32)
        ag = sb("ag", [128, 22 * 512], BF16)
        outb = [sb(f"outb{i}", [128, 512], F32) for i in range(2)]

        pa = ps("pa", [128, 512], F32)
        pg = ps("pg", [128, 512], F32)
        po = [ps(f"po{i}", [128, 512], F32) for i in range(2)]
        ptp = ps("ptp", [128, 512], BF16)
        ptpf = ps("ptpf", [128, 512], F32)

        s_ld = sem("s_ld")
        s_ld2 = sem("s_ld2")
        s_lda = sem("s_lda")
        s_st = sem("s_st")
        s_ve = sem("s_ve")
        s_sc = sem("s_sc")
        s_te = sem("s_te")

        p = Prog(nc)
        E = p.emit

        def load_cast2(whbm, wsb, rowchunks, ncols, colchunk, wsb_rowstride):
            for idx, (r0, nr) in enumerate(rowchunks):
                c0 = 0
                while c0 < ncols:
                    cw = min(colchunk, ncols - c0)
                    p.wait("sync", s_sc)
                    E("sync", lambda e, r0=r0, nr=nr, c0=c0, cw=cw: e.dma_start(
                        out=stage[0:nr, 0:cw], in_=whbm[r0:r0 + nr, c0:c0 + cw]).then_inc(s_ld.h, 16))
                    s_ld.inc(16)
                    p.wait("scalar", s_ld)
                    E("scalar", lambda e, idx=idx, nr=nr, c0=c0, cw=cw, wsb=wsb, stride=wsb_rowstride: e.copy(
                        wsb[0:nr, idx * stride + c0:idx * stride + c0 + cw], stage[0:nr, 0:cw]).then_inc(s_sc.h, 1))
                    s_sc.inc()
                    c0 += cw

        load_cast2(w1_h, w1, [(i * 128, 128) for i in range(8)], HID2, 1408, HID2)
        load_cast2(w2_h, w2, ACH, 1024, 1024, 1024)

        def simple_load(src, dst):
            E("sync", lambda e, s=src, dst=dst: e.dma_start(out=dst, in_=s).then_inc(s_ld2.h, 16))
            s_ld2.inc(16)

        for srch, dst in ((b1_h, b1c), (b2_h, b2c), (lng_h, lngc), (lnb_h, lnbc), (ident_h, ident), (identf_h, identf)):
            simple_load(srch, dst[:, :])
        p.wait("vector", s_ld2)
        p.wait("scalar", s_ld2)
        p.wait("tensor", s_ld2)
        p.wait("tensor", s_ld)
        E("vector", lambda e: e.memset(cst[:, 0:1], 1e-5))
        E("vector", lambda e: e.memset(ag[:, :], 0.0).then_inc(s_ve.h, 1))
        s_ve.inc()
        p.wait("vector", s_ve)
        p.wait("scalar", s_ve)
        prewarm_tables(p, E, s_sc, gl[:, 0:1], [AF.Gelu, AF.Ln, AF.Exp, AF.Square])

        for tc in range(NCH):
            for i in range(4):
                r0 = tc * 512 + i * 128
                p.wait("sync", s_ve)
                p.wait("sync", s_te)
                p.wait("sync", s_sc)
                E("sync", lambda e, r0=r0: e.dma_start(out=za_t[:, :], in_=za_h[r0 + 1:r0 + 129, :]).then_inc(s_lda.h, 16))
                s_lda.inc(16)
                E("sync", lambda e, r0=r0: e.dma_start(out=zsh_t[:, :], in_=za_h[r0:r0 + 128, :]).then_inc(s_lda.h, 16))
                s_lda.inc(16)
                if final_resid:
                    E("sync", lambda e, r0=r0: e.dma_start(out=x1_t[:, :], in_=x1_h[r0:r0 + 128, :]).then_inc(s_lda.h, 16))
                    s_lda.inc(16)
                p.wait("vector", s_lda)
                p.wait("vector", s_sc)
                p.wait("vector", s_te)
                p.wait("scalar", s_lda)
                p.wait("scalar", s_ve)
                mean_ap, rstd_ap = ln_core(p, E, s_ve, s_sc,
                                           [(za_t[:, 0:512], 512), (zsh_t[:, 512:1024], 512)],
                                           lnjunk, lnc, cst[:, 0:1])
                p.wait("vector", s_sc)
                E("vector", lambda e, mean_ap=mean_ap, rstd_ap=rstd_ap: e.tensor_scalar(
                    out=zn[:, 0:512], in0=za_t[:, 0:512], scalar1=mean_ap,
                    scalar2=rstd_ap, op0=ALU.subtract, op1=ALU.mult))
                E("vector", lambda e, mean_ap=mean_ap, rstd_ap=rstd_ap: e.tensor_scalar(
                    out=zn[:, 512:1024], in0=zsh_t[:, 512:1024], scalar1=mean_ap,
                    scalar2=rstd_ap, op0=ALU.subtract, op1=ALU.mult).then_inc(s_ve.h, 1))
                s_ve.inc()
                if final_resid:
                    E("vector", lambda e: e.tensor_add(za_t[:, :], za_t[:, :], x1_t[:, :]).then_inc(s_ve.h, 1))
                    s_ve.inc()
                p.wait("tensor", s_ve)
                for k in range(8):
                    slot = k % 4
                    p.wait("tensor", s_sc)
                    E("tensor", lambda e, k=k, slot=slot: e.transpose(
                        ptp[:, slot * 128:(slot + 1) * 128], zn[:, k * 128:(k + 1) * 128],
                        ident[:, :]).then_inc(s_te.h, 1))
                    s_te.inc()
                    p.wait("scalar", s_te)
                    E("scalar", lambda e, k=k, slot=slot, i=i: e.copy(
                        znT[:, k * 512 + i * 128:k * 512 + (i + 1) * 128],
                        ptp[:, slot * 128:(slot + 1) * 128]).then_inc(s_sc.h, 1))
                    s_sc.inc()
                for k in range(8):
                    slot = k % 4
                    p.wait("tensor", s_sc)
                    E("tensor", lambda e, k=k, slot=slot: e.transpose(
                        ptpf[:, slot * 128:(slot + 1) * 128], za_t[:, k * 128:(k + 1) * 128],
                        identf[:, :]).then_inc(s_te.h, 1))
                    s_te.inc()
                    p.wait("scalar", s_te)
                    E("scalar", lambda e, k=k, slot=slot, i=i: e.copy(
                        zaT[:, k * 512 + i * 128:k * 512 + (i + 1) * 128],
                        ptpf[:, slot * 128:(slot + 1) * 128]).then_inc(s_sc.h, 1))
                    s_sc.inc()
            p.wait("vector", s_sc)
            p.wait("vector", s_te)
            for kc in range(8):
                E("vector", lambda e, kc=kc, last=(kc == 7): (
                    e.tensor_scalar(out=znT[:, kc * 512:(kc + 1) * 512], in0=znT[:, kc * 512:(kc + 1) * 512],
                                    scalar1=lngc[:, kc:kc + 1], scalar2=lnbc[:, kc:kc + 1],
                                    op0=ALU.mult, op1=ALU.add).then_inc(s_ve.h, 1)
                    if last else
                    e.tensor_scalar(out=znT[:, kc * 512:(kc + 1) * 512], in0=znT[:, kc * 512:(kc + 1) * 512],
                                    scalar1=lngc[:, kc:kc + 1], scalar2=lnbc[:, kc:kc + 1],
                                    op0=ALU.mult, op1=ALU.add)))
            s_ve.inc()
            znT_ready_v = s_ve.v
            znT_ready = s_sc.v

            for idx, (c0, cw) in enumerate(ACH):
                for kc in range(8):
                    if kc == 0:
                        p.wait("tensor", s_sc, znT_ready)
                        p.wait("tensor", s_ve, znT_ready_v)
                        p.wait("tensor", s_sc)
                        p.wait("tensor", s_ve)
                    E("tensor", lambda e, kc=kc, c0=c0, cw=cw, last=(kc == 7): (
                        e.matmul(pg[0:cw, :], w1[:, kc * HID2 + HID + c0:kc * HID2 + HID + c0 + cw],
                                 znT[:, kc * 512:(kc + 1) * 512], start=(kc == 0), stop=last).then_inc(s_te.h, 1)
                        if last else
                        e.matmul(pg[0:cw, :], w1[:, kc * HID2 + HID + c0:kc * HID2 + HID + c0 + cw],
                                 znT[:, kc * 512:(kc + 1) * 512], start=(kc == 0), stop=last)))
                s_te.inc()
                p.wait("scalar", s_te)
                p.wait("scalar", s_ve)
                E("scalar", lambda e, idx=idx, cw=cw: e.activation(
                    gl[0:cw, :], pg[0:cw, :], AF.Gelu, bias=b1c[0:cw, 22 + idx:22 + idx + 1]).then_inc(s_sc.h, 1))
                s_sc.inc()
                for kc in range(8):
                    if kc == 0:
                        p.wait("tensor", s_ve)
                    E("tensor", lambda e, kc=kc, c0=c0, cw=cw, last=(kc == 7): (
                        e.matmul(pa[0:cw, :], w1[:, kc * HID2 + c0:kc * HID2 + c0 + cw],
                                 znT[:, kc * 512:(kc + 1) * 512], start=(kc == 0), stop=last).then_inc(s_te.h, 1)
                        if last else
                        e.matmul(pa[0:cw, :], w1[:, kc * HID2 + c0:kc * HID2 + c0 + cw],
                                 znT[:, kc * 512:(kc + 1) * 512], start=(kc == 0), stop=last)))
                s_te.inc()
                p.wait("vector", s_te)
                p.wait("vector", s_sc)
                E("vector", lambda e, idx=idx, cw=cw: e.scalar_tensor_tensor(
                    out=ag[0:cw, idx * 512:(idx + 1) * 512], in0=pa[0:cw, :],
                    scalar=b1c[0:cw, idx:idx + 1], in1=gl[0:cw, :],
                    op0=ALU.add, op1=ALU.mult).then_inc(s_ve.h, 1))
                s_ve.inc()
            ag_ready = s_ve.v

            for oc in range(8):
                bank = oc % 2
                for idx, (c0, cw) in enumerate(ACH):
                    if idx == 0:
                        p.wait("tensor", s_ve, ag_ready)
                        p.wait("tensor", s_ve)
                        p.wait("tensor", s_st)
                    E("tensor", lambda e, idx=idx, cw=cw, oc=oc, bank=bank, last=(idx == len(ACH) - 1): (
                        e.matmul(po[bank][:, :], w2[0:cw, idx * 1024 + oc * 128:idx * 1024 + (oc + 1) * 128],
                                 ag[0:cw, idx * 512:(idx + 1) * 512], start=(idx == 0), stop=last).then_inc(s_te.h, 1)
                        if last else
                        e.matmul(po[bank][:, :], w2[0:cw, idx * 1024 + oc * 128:idx * 1024 + (oc + 1) * 128],
                                 ag[0:cw, idx * 512:(idx + 1) * 512], start=(idx == 0), stop=last)))
                s_te.inc()
                p.wait("vector", s_te)
                p.wait("vector", s_st)
                E("vector", lambda e, oc=oc, bank=bank: e.scalar_tensor_tensor(
                    out=outb[bank][:, :], in0=po[bank][:, :], scalar=b2c[:, oc:oc + 1],
                    in1=zaT[:, oc * 512:(oc + 1) * 512], op0=ALU.add, op1=ALU.add).then_inc(s_ve.h, 1))
                s_ve.inc()
                p.wait("gpsimd", s_ve)
                E("gpsimd", lambda e, oc=oc, bank=bank, tc=tc: e.dma_start(
                    zf_h[oc * 128:(oc + 1) * 128, tc * 512:(tc + 1) * 512], outb[bank][:, :]).then_inc(s_st.h, 16))
                s_st.inc(16)
                p.wait("gpsimd", s_st)

        p.wait("gpsimd", s_st)
        with nc.Block() as block:
            p.attach(block)
    return nc


def ffn_b_inputs(inp, core, layer, final_resid, OWN=2048):
    import ml_dtypes
    b, h = core // 2, core % 2
    s = h * OWN
    out = {}
    za_full = np.ascontiguousarray(inp["_za%d" % layer][b])
    rows = np.zeros((OWN + 1, 1024), np.float32)
    lo = max(s - 1, 0)
    rows[(lo - (s - 1)):, :] = za_full[lo:s + OWN, :]
    out["zain"] = rows
    out["w1"] = np.ascontiguousarray(inp["L0_ffn_w1"][layer])
    out["w2"] = np.ascontiguousarray(inp["L0_ffn_w2"][layer])
    HID = 2730
    b1 = inp["L0_ffn_b1"][layer]
    b1c = np.zeros((128, 44), np.float32)
    for i in range(22):
        c0 = i * 128
        cw = min(128, HID - c0)
        b1c[0:cw, i] = b1[c0:c0 + cw]
        b1c[0:cw, 22 + i] = b1[HID + c0:HID + c0 + cw]
    out["b1c"] = b1c
    out["b2c"] = np.ascontiguousarray(inp["L0_ffn_b2"][layer].reshape(8, 128).T)
    out["lngc"] = np.ascontiguousarray(inp["L0_ffn_ln_g"][layer].reshape(8, 128).T)
    out["lnbc"] = np.ascontiguousarray(inp["L0_ffn_ln_b"][layer].reshape(8, 128).T)
    out["ident"] = np.eye(128, dtype=np.float32).astype(ml_dtypes.bfloat16)
    out["identf"] = np.eye(128, dtype=np.float32)
    if final_resid:
        out["x1in"] = np.ascontiguousarray(inp["_x1"][b][s:s + OWN])
    return out


def build_l1_attn(T=4096, QC=256):
    NQ = 8
    nc = bass.Bass("TRN2", target_bir_lowering=False)
    d = lambda *a, **k: nc.dram_tensor(*a, **k).ap()
    x2_h = d("x2", [T, 1024], F32, kind="ExternalInput")
    x2q_h = d("x2q", [NQ * QC, 1024], F32, kind="ExternalInput")
    wq_h = d("wq", [1024, 512], F32, kind="ExternalInput")
    wk_h = d("wk", [1024, 128], F32, kind="ExternalInput")
    wv_h = d("wv", [1024, 256], F32, kind="ExternalInput")
    wp_h = d("wp", [1024, 1024], F32, kind="ExternalInput")
    bp_h = d("bpc", [128, 8], F32, kind="ExternalInput")
    biasall_h = d("biasall", [128, 8 * 16 * 36], F32, kind="ExternalInput")
    ltz_h = d("ltz", [128, 8 * 4 * 256], BF16, kind="ExternalInput")
    ident_h = d("ident", [128, 128], BF16, kind="ExternalInput")
    identf_h = d("identf", [128, 128], F32, kind="ExternalInput")
    x3_h = d("x3", [1024, NQ * QC], F32, kind="ExternalOutput")

    es = ExitStack()
    sb, ps, sem = mk_ctx(nc, es)
    with es:
        wq = sb("s_wq", [128, 8 * 512], BF16)
        wk = sb("s_wk", [128, 8 * 128], BF16)
        wv = sb("s_wv", [128, 8 * 256], BF16)
        wp = sb("s_wp", [128, 8 * 1024], BF16)
        bpc = sb("s_bpc", [128, 8], F32)
        biasall = sb("s_biasall", [128, 8 * 16 * 36], F32)
        ltz = sb("s_ltz", [128, 8 * 4 * 256], BF16)
        ident = sb("s_ident", [128, 128], BF16)
        identf = sb("s_identf", [128, 128], F32)
        onesc = sb("onesc", [128, 1], BF16)
        ones64 = sb("ones64", [1, 64], BF16)
        stage = sb("stage", [128, 1024], F32)
        cst = sb("cst", [128, 2], F32)
        xx = sb("xx", [128, 1024], F32)
        lnjunk = sb("lnjunk", [128, 1024], F32)
        lnc = sb("lnc", [128, 16], F32)
        mv = sb("mv", [128, 2], F32)
        rstd = sb("rstd", [128, 1], F32)
        zn = sb("zn", [128, 1024], BF16)
        z2T = sb("z2T", [128, 8 * T], BF16)
        kTl = sb("kTl", [64, T], BF16)
        kTh = sb("kTh", [64, T], BF16)
        v_sb = sb("v_sb", [128, 32 * 260], BF16)
        znq = sb("znq", [128, 2 * 1024], BF16)
        znqT = sb("znqT", [128, 8 * 256], BF16)
        qTl = sb("qTl", [64, 4 * 256], BF16)
        qTh = sb("qTh", [64, 4 * 256], BF16)
        x2qT = sb("x2qT", [128, 8 * 256], F32)
        eb = [sb(f"eb{i}", [128, 2 * 256], BF16) for i in range(2)]
        ctxf = sb("ctxf", [128, 256], F32)
        rdf = sb("rdf", [1, 256], F32)
        rdb = sb("rdb", [1, 256], BF16)
        ctx = sb("ctx", [128, 8 * 256], BF16)
        outb = [sb(f"outb{i}", [128, 256], F32) for i in range(2)]

        ps0 = ps("ps0", [128, 512], F32)
        ps1 = ps("ps1", [128, 512], F32)
        pav = [ps(f"pav{i}", [128, 512], F32) for i in range(2)]
        pbc = ps("pbc", [128, 512], F32)
        ptp = ps("ptp", [128, 512], BF16)
        ptpf = ps("ptpf", [128, 512], F32)

        s_ld = sem("s_ld")
        s_ld2 = sem("s_ld2")
        s_lda = sem("s_lda")
        s_st = sem("s_st")
        s_ve = sem("s_ve")
        s_sc = sem("s_sc")
        s_te = sem("s_te")

        p = Prog(nc)
        E = p.emit

        def load_cast(whbm, wsb, nrows, ncols):
            for kc in range(nrows // 128):
                p.wait("sync", s_sc)
                E("sync", lambda e, kc=kc, ncols=ncols: e.dma_start(
                    out=stage[:, 0:ncols], in_=whbm[kc * 128:(kc + 1) * 128, :]).then_inc(s_ld.h, 16))
                s_ld.inc(16)
                p.wait("scalar", s_ld)
                E("scalar", lambda e, kc=kc, ncols=ncols, wsb=wsb: e.copy(
                    wsb[:, kc * ncols:(kc + 1) * ncols], stage[:, 0:ncols]).then_inc(s_sc.h, 1))
                s_sc.inc()

        load_cast(wq_h, wq, 1024, 512)
        load_cast(wk_h, wk, 1024, 128)
        load_cast(wv_h, wv, 1024, 256)
        load_cast(wp_h, wp, 1024, 1024)

        def simple_load(src, dst):
            E("sync", lambda e, s=src, dst=dst: e.dma_start(out=dst, in_=s).then_inc(s_ld2.h, 16))
            s_ld2.inc(16)

        for srch, dst in ((bp_h, bpc), (biasall_h, biasall), (ltz_h, ltz), (ident_h, ident), (identf_h, identf)):
            simple_load(srch, dst[:, :])
        p.wait("vector", s_ld2)
        p.wait("scalar", s_ld2)
        p.wait("tensor", s_ld2)
        p.wait("tensor", s_ld)
        E("vector", lambda e: e.memset(cst[:, 0:1], 1e-5))
        E("vector", lambda e: e.memset(cst[:, 1:2], 1e-19))
        E("vector", lambda e: e.memset(onesc[:, :], 1.0))
        E("vector", lambda e: e.memset(ones64[:, :], 1.0))
        E("vector", lambda e: e.memset(v_sb[:, :], 1.0).then_inc(s_ve.h, 1))
        s_ve.inc()
        p.wait("vector", s_ve)
        p.wait("scalar", s_ve)
        prewarm_tables(p, E, s_sc, lnjunk[:, 0:1], [AF.Exp, AF.Ln, AF.Square])

        def ln_tile(xsrc, dst, i_note=0):
            p.wait("scalar", s_lda)
            p.wait("scalar", s_ve)
            mean_ap, rstd_ap = ln_core(p, E, s_ve, s_sc, [(xsrc[:, :], 1024)], lnjunk, lnc, cst[:, 0:1])
            p.wait("vector", s_sc)
            E("vector", lambda e, dst=dst, mean_ap=mean_ap, rstd_ap=rstd_ap: e.tensor_scalar(
                out=dst, in0=xsrc[:, :], scalar1=mean_ap,
                scalar2=rstd_ap, op0=ALU.subtract, op1=ALU.mult).then_inc(s_ve.h, 1))
            s_ve.inc()

        def transpose8(src_bf, dstfn, use_f32=False):
            idm = identf if use_f32 else ident
            bank = ptpf if use_f32 else ptp
            for k in range(8):
                slot = k % 4
                p.wait("tensor", s_sc)
                E("tensor", lambda e, k=k, slot=slot, src_bf=src_bf, idm=idm, bank=bank: e.transpose(
                    bank[:, slot * 128:(slot + 1) * 128], src_bf[:, k * 128:(k + 1) * 128],
                    idm[:, :]).then_inc(s_te.h, 1))
                s_te.inc()
                p.wait("scalar", s_te)
                dst = dstfn(k)
                E("scalar", lambda e, dst=dst, slot=slot, bank=bank: e.copy(
                    dst, bank[:, slot * 128:(slot + 1) * 128]).then_inc(s_sc.h, 1))
                s_sc.inc()

        for i in range(32):
            p.wait("sync", s_ve)
            p.wait("sync", s_te)
            p.wait("sync", s_sc)
            E("sync", lambda e, i=i: e.dma_start(out=xx[:, :], in_=x2_h[i * 128:(i + 1) * 128, :]).then_inc(s_lda.h, 16))
            s_lda.inc(16)
            p.wait("vector", s_lda)
            p.wait("vector", s_te)
            p.wait("vector", s_sc)
            ln_tile(xx, zn[:, :])
            p.wait("tensor", s_ve)
            transpose8(zn, lambda k, i=i: z2T[:, k * T + i * 128:k * T + (i + 1) * 128])
        z2T_ready = s_sc.v

        for cc in range(8):
            for kc in range(8):
                if kc == 0:
                    p.wait("tensor", s_sc, z2T_ready)
                    p.wait("tensor", s_sc)
                    p.wait("tensor", s_ve)
                E("tensor", lambda e, kc=kc, cc=cc, last=(kc == 7): (
                    e.matmul(ps0[:, :], wk[:, kc * 128:(kc + 1) * 128],
                             z2T[:, kc * T + cc * 512:kc * T + (cc + 1) * 512],
                             start=(kc == 0), stop=last).then_inc(s_te.h, 1)
                    if last else
                    e.matmul(ps0[:, :], wk[:, kc * 128:(kc + 1) * 128],
                             z2T[:, kc * T + cc * 512:kc * T + (cc + 1) * 512],
                             start=(kc == 0), stop=last)))
            s_te.inc()
            p.wait("scalar", s_te)
            E("scalar", lambda e, cc=cc: e.copy(kTl[:, cc * 512:(cc + 1) * 512], ps0[0:64, :]))
            E("scalar", lambda e, cc=cc: e.copy(kTh[:, cc * 512:(cc + 1) * 512], ps0[64:128, :]).then_inc(s_sc.h, 1))
            s_sc.inc()
        for vt in range(32):
            for kc in range(8):
                if kc == 0:
                    p.wait("tensor", s_sc)
                    p.wait("tensor", s_ve)
                E("tensor", lambda e, kc=kc, vt=vt, last=(kc == 7): (
                    e.matmul(ps1[0:128, 0:256], z2T[:, kc * T + vt * 128:kc * T + (vt + 1) * 128],
                             wv[:, kc * 256:(kc + 1) * 256], start=(kc == 0), stop=last).then_inc(s_te.h, 1)
                    if last else
                    e.matmul(ps1[0:128, 0:256], z2T[:, kc * T + vt * 128:kc * T + (vt + 1) * 128],
                             wv[:, kc * 256:(kc + 1) * 256], start=(kc == 0), stop=last)))
            s_te.inc()
            p.wait("scalar", s_te)
            E("scalar", lambda e, vt=vt: e.copy(
                v_sb[:, vt * 260:vt * 260 + 260].rearrange("p (g c) -> p g c", g=4)[:, :, 0:64],
                ps1[0:128, 0:256].rearrange("p (g c) -> p g c", g=4)).then_inc(s_sc.h, 1))
            s_sc.inc()
        kv_ready = s_sc.v

        for qi in range(NQ):
            ktmax = 4 * qi + 4
            for t2 in range(2):
                p.wait("sync", s_ve)
                p.wait("sync", s_te)
                p.wait("sync", s_st)
                p.wait("sync", s_sc)
                E("sync", lambda e, qi=qi, t2=t2: e.dma_start(
                    out=xx[:, :], in_=x2q_h[qi * 256 + t2 * 128:qi * 256 + (t2 + 1) * 128, :]).then_inc(s_lda.h, 16))
                s_lda.inc(16)
                p.wait("vector", s_lda)
                p.wait("vector", s_te)
                p.wait("vector", s_sc)
                ln_tile(xx, znq[:, t2 * 1024:(t2 + 1) * 1024])
                p.wait("tensor", s_ve)
                transpose8(znq[:, t2 * 1024:(t2 + 1) * 1024].rearrange("p c -> p c") if False else znq[:, t2 * 1024:(t2 + 1) * 1024],
                           lambda k, t2=t2: znqT[:, k * 256 + t2 * 128:k * 256 + (t2 + 1) * 128])
                transpose8(xx, lambda k, t2=t2: x2qT[:, k * 256 + t2 * 128:k * 256 + (t2 + 1) * 128], use_f32=True)
            for quad in range(4):
                for kc in range(8):
                    if kc == 0:
                        p.wait("tensor", s_sc)
                        p.wait("tensor", s_ve)
                    E("tensor", lambda e, kc=kc, quad=quad, last=(kc == 7): (
                        e.matmul(ps0[0:128, 0:256], wq[:, kc * 512 + quad * 128:kc * 512 + (quad + 1) * 128],
                                 znqT[:, kc * 256:(kc + 1) * 256], start=(kc == 0), stop=last).then_inc(s_te.h, 1)
                        if last else
                        e.matmul(ps0[0:128, 0:256], wq[:, kc * 512 + quad * 128:kc * 512 + (quad + 1) * 128],
                                 znqT[:, kc * 256:(kc + 1) * 256], start=(kc == 0), stop=last)))
                s_te.inc()
                p.wait("scalar", s_te)
                E("scalar", lambda e, quad=quad: e.copy(qTl[:, quad * 256:(quad + 1) * 256], ps0[0:64, 0:256]))
                E("scalar", lambda e, quad=quad: e.copy(qTh[:, quad * 256:(quad + 1) * 256], ps0[64:128, 0:256]).then_inc(s_sc.h, 1))
                s_sc.inc()
            q_ready = s_sc.v

            for quad in range(4):
                for pr in range(2):
                    heads = (4 * quad + 2 * pr, 4 * quad + 2 * pr + 1)
                    ktt = kTl if pr == 0 else kTh
                    qtt = qTl if pr == 0 else qTh
                    rowb = (0, 32)
                    for hi in range(2):
                        p.wait("tensor", s_ve)
                        p.wait("tensor", s_sc)
                    for kt in range(ktmax):
                        ebuf = eb[kt % 2]
                        for hi in range(2):
                            head = heads[hi]
                            rb = rowb[hi]
                            scb = (ps0, ps1)[hi]
                            p.wait("tensor", s_sc)
                            p.wait("tensor", s_ve)
                            E("tensor", lambda e, rb=rb, kt=kt, quad=quad, scb=scb, ktt=ktt, qtt=qtt: e.matmul(
                                scb[0:128, 0:256], ktt[rb:rb + 32, kt * 128:(kt + 1) * 128],
                                qtt[rb:rb + 32, quad * 256:(quad + 1) * 256],
                                start=True, stop=True).then_inc(s_te.h, 1))
                            s_te.inc()
                        dkt = kt - 4 * qi
                        for hi in range(2):
                            head = heads[hi]
                            scb = (ps0, ps1)[hi]
                            if dkt >= 0:
                                p.wait("vector", s_te)
                                E("vector", lambda e, scb=scb, qi=qi, dkt=dkt: e.tensor_tensor(
                                    scb[0:128, 0:256], scb[0:128, 0:256],
                                    ltz[:, qi * 1024 + dkt * 256:qi * 1024 + (dkt + 1) * 256], ALU.add).then_inc(s_ve.h, 1))
                                s_ve.inc()
                            bcol = qi * 576 + head * 36 + kt
                            p.wait("scalar", s_te)
                            p.wait("scalar", s_ve)
                            E("scalar", lambda e, ebuf=ebuf, hi=hi, scb=scb, bcol=bcol: e.activation(
                                ebuf[:, hi * 256:(hi + 1) * 256], scb[0:128, 0:256], AF.Exp,
                                scale=0.17677669529663687, bias=biasall[:, bcol:bcol + 1]).then_inc(s_sc.h, 1))
                            s_sc.inc()
                        p.wait("tensor", s_sc)
                        for hi in range(2):
                            head = heads[hi]
                            g = head % 4
                            E("tensor", lambda e, kt=kt, g=g, hi=hi, ebuf=ebuf, last=(kt == ktmax - 1): e.matmul(
                                pav[hi][0:65, 0:256], v_sb[:, kt * 260 + g * 65:kt * 260 + g * 65 + 65],
                                ebuf[:, hi * 256:(hi + 1) * 256], start=(kt == 0), stop=last).then_inc(s_te.h, 1))
                            s_te.inc()
                    for hi in range(2):
                        head = heads[hi]
                        p.wait("scalar", s_te)
                        p.wait("scalar", s_ve)
                        E("scalar", lambda e, hi=hi: e.copy(ctxf[0:64, :], pav[hi][0:64, 0:256]))
                        E("scalar", lambda e, hi=hi: e.activation(rdf[:, :], pav[hi][64:65, 0:256], AF.Ln, bias=cst[0:1, 1:2]))
                        E("scalar", lambda e: e.activation(rdb[:, :], rdf[:, :], AF.Exp, scale=-1.0).then_inc(s_sc.h, 1))
                        s_sc.inc()
                        p.wait("tensor", s_sc)
                        p.wait("tensor", s_ve)
                        E("tensor", lambda e: e.matmul(pbc[0:64, 0:256], ones64[:, :], rdb[:, :], start=True, stop=True).then_inc(s_te.h, 1))
                        s_te.inc()
                        p.wait("vector", s_te)
                        cb = head // 2
                        rr = (head % 2) * 64
                        E("vector", lambda e, cb=cb, rr=rr: e.tensor_mul(
                            ctx[rr:rr + 64, cb * 256:(cb + 1) * 256],
                            ctxf[0:64, :], pbc[0:64, 0:256]).then_inc(s_ve.h, 1))
                        s_ve.inc()
            ctx_ready = s_ve.v

            for oc in range(8):
                bank = oc % 2
                pob = (ps0, ps1)[bank]
                for kc in range(8):
                    if kc == 0:
                        p.wait("tensor", s_ve, ctx_ready)
                        p.wait("tensor", s_ve)
                        p.wait("tensor", s_sc)
                        p.wait("tensor", s_st)
                    E("tensor", lambda e, oc=oc, kc=kc, pob=pob, last=(kc == 7): (
                        e.matmul(pob[0:128, 0:256], wp[:, kc * 1024 + oc * 128:kc * 1024 + (oc + 1) * 128],
                                 ctx[:, kc * 256:(kc + 1) * 256], start=(kc == 0), stop=last).then_inc(s_te.h, 1)
                        if last else
                        e.matmul(pob[0:128, 0:256], wp[:, kc * 1024 + oc * 128:kc * 1024 + (oc + 1) * 128],
                                 ctx[:, kc * 256:(kc + 1) * 256], start=(kc == 0), stop=last)))
                s_te.inc()
                p.wait("vector", s_te)
                p.wait("vector", s_st)
                E("vector", lambda e, oc=oc, bank=bank, pob=pob: e.scalar_tensor_tensor(
                    out=outb[bank][:, :], in0=pob[0:128, 0:256], scalar=bpc[:, oc:oc + 1],
                    in1=x2qT[:, oc * 256:(oc + 1) * 256], op0=ALU.add, op1=ALU.add).then_inc(s_ve.h, 1))
                s_ve.inc()
                p.wait("gpsimd", s_ve)
                E("gpsimd", lambda e, oc=oc, bank=bank, qi=qi: e.dma_start(
                    x3_h[oc * 128:(oc + 1) * 128, qi * 256:(qi + 1) * 256], outb[bank][:, :]).then_inc(s_st.h, 16))
                s_st.inc(16)
                p.wait("gpsimd", s_st)

        p.wait("gpsimd", s_st)
        with nc.Block() as block:
            p.attach(block)
    return nc


def l1_attn_inputs(inp, core, T=4096, QC=256):
    import ml_dtypes
    b, pp = core // 2, core % 2
    x2 = np.ascontiguousarray(inp["_x2"][b])
    out = {"x2": x2}
    qcs = [2 * i + pp for i in range(8)]
    out["x2q"] = np.ascontiguousarray(np.concatenate([x2[qc * QC:(qc + 1) * QC] for qc in qcs], axis=0))
    out["wq"] = np.ascontiguousarray(inp["L1_attn_Wq"])
    out["wk"] = np.ascontiguousarray(inp["L1_attn_Wk"])
    out["wv"] = np.ascontiguousarray(inp["L1_attn_Wv"])
    out["wp"] = np.ascontiguousarray(inp["L1_attn_Wp"])
    out["bpc"] = np.ascontiguousarray(inp["L1_attn_bp"].reshape(8, 128).T)
    H = 16
    slopes = 2.0 ** (-8.0 * np.arange(1, H + 1) / H)
    rt32 = np.sqrt(32.0)
    biasall = np.full((8, 16, 36, 128), -30000.0, np.float32)
    ltz = np.zeros((8, 4, 128, 256), np.float32)
    for qi in range(8):
        qc = 2 * qi + pp
        ref = qc * QC + QC - 1
        ktreal = (qc + 1) * QC // 128
        for kt in range(4 * qi + 4):
            if kt >= ktreal:
                continue
            posk = kt * 128 + np.arange(128)
            for h in range(H):
                biasall[qi, h, kt, :] = slopes[h] * (posk - ref) / rt32
        for dkt in range(4):
            kt = 4 * qi + dkt
            posk = kt * 128 + np.arange(128)[:, None]
            posq = qc * QC + np.arange(256)[None, :]
            ltz[qi, dkt] = np.where(posk <= posq, 0.0, -30000.0)
    out["biasall"] = np.ascontiguousarray(biasall.transpose(3, 0, 1, 2).reshape(128, 8 * 16 * 36))
    out["ltz"] = np.ascontiguousarray(ltz.transpose(2, 0, 1, 3).reshape(128, 8 * 4 * 256)).astype(ml_dtypes.bfloat16)
    out["ident"] = np.eye(128, dtype=np.float32).astype(ml_dtypes.bfloat16)
    out["identf"] = np.eye(128, dtype=np.float32)
    return out


def build_l1_ffn(OWN=2048):
    NCH = OWN // 512
    nc = bass.Bass("TRN2", target_bir_lowering=False)
    d = lambda *a, **k: nc.dram_tensor(*a, **k).ap()
    x3_h = d("x3in", [OWN + 1, 1024], F32, kind="ExternalInput")
    wr_h = d("wrt", [1024, 1024], F32, kind="ExternalInput")
    wk_h = d("wkt", [1024, 1024], F32, kind="ExternalInput")
    wv_h = d("wvt", [1024, 1024], F32, kind="ExternalInput")
    mur_h = d("murc", [128, 8], F32, kind="ExternalInput")
    muk_h = d("mukc", [128, 8], F32, kind="ExternalInput")
    ident_h = d("ident", [128, 128], BF16, kind="ExternalInput")
    identf_h = d("identf", [128, 128], F32, kind="ExternalInput")
    out_h = d("xout", [OWN, 1024], F32, kind="ExternalOutput")

    es = ExitStack()
    sb, ps, sem = mk_ctx(nc, es)
    with es:
        wr = sb("s_wr", [128, 8 * 1024], BF16)
        wk = sb("s_wk", [128, 8 * 1024], BF16)
        wv = sb("s_wv", [128, 8 * 1024], BF16)
        murc = sb("s_murc", [128, 8], F32)
        mukc = sb("s_mukc", [128, 8], F32)
        ident = sb("s_ident", [128, 128], BF16)
        identf = sb("s_identf", [128, 128], F32)
        stage = sb("stage", [128, 1024], F32)
        cst = sb("cst", [128, 2], F32)
        xx = sb("xx", [128, 1024], F32)
        xsh = sb("xsh", [128, 1024], F32)
        lnjunk = sb("lnjunk", [128, 1024], F32)
        lnc = sb("lnc", [128, 16], F32)
        mv = sb("mv", [128, 2], F32)
        rstd = sb("rstd", [128, 1], F32)
        zc = sb("zc", [128, 1024], F32)
        zp = sb("zp", [128, 1024], F32)
        zcT = sb("zcT", [128, 8 * 512], F32)
        zpT = sb("zpT", [128, 8 * 512], F32)
        x3T = sb("x3T", [128, 8 * 512], F32)
        dT = sb("dT", [128, 512], F32)
        xr = sb("xr", [128, 8 * 512], BF16)
        xk = sb("xk", [128, 8 * 512], BF16)
        rsg = sb("rsg", [128, 8 * 512], F32)
        kact = sb("kact", [128, 8 * 512], BF16)
        t2b = sb("t2b", [128, 512], F32)
        outf = sb("outf", [128, 8 * 512], F32)
        outt = sb("outt", [128, 4 * 1024], F32)

        pm = [ps(f"pm{i}", [128, 512], F32) for i in range(2)]
        ptp = ps("ptp", [128, 512], BF16)
        ptpf = ps("ptpf", [128, 512], F32)

        s_ld = sem("s_ld")
        s_ld2 = sem("s_ld2")
        s_lda = sem("s_lda")
        s_st = sem("s_st")
        s_ve = sem("s_ve")
        s_sc = sem("s_sc")
        s_te = sem("s_te")

        p = Prog(nc)
        E = p.emit

        def load_cast(whbm, wsb, nrows, ncols):
            for kc in range(nrows // 128):
                p.wait("sync", s_sc)
                E("sync", lambda e, kc=kc, ncols=ncols, whbm=whbm: e.dma_start(
                    out=stage[:, 0:ncols], in_=whbm[kc * 128:(kc + 1) * 128, :]).then_inc(s_ld.h, 16))
                s_ld.inc(16)
                p.wait("scalar", s_ld)
                E("scalar", lambda e, kc=kc, ncols=ncols, wsb=wsb: e.copy(
                    wsb[:, kc * ncols:(kc + 1) * ncols], stage[:, 0:ncols]).then_inc(s_sc.h, 1))
                s_sc.inc()

        load_cast(wr_h, wr, 1024, 1024)
        load_cast(wk_h, wk, 1024, 1024)
        load_cast(wv_h, wv, 1024, 1024)

        def simple_load(src, dst):
            E("sync", lambda e, s=src, dst=dst: e.dma_start(out=dst, in_=s).then_inc(s_ld2.h, 16))
            s_ld2.inc(16)

        for srch, dst in ((mur_h, murc), (muk_h, mukc), (ident_h, ident), (identf_h, identf)):
            simple_load(srch, dst[:, :])
        p.wait("vector", s_ld2)
        p.wait("scalar", s_ld2)
        p.wait("tensor", s_ld2)
        p.wait("tensor", s_ld)
        E("vector", lambda e: e.memset(cst[:, 0:1], 1e-5).then_inc(s_ve.h, 1))
        s_ve.inc()
        p.wait("vector", s_ve)
        p.wait("scalar", s_ve)
        prewarm_tables(p, E, s_sc, lnjunk[:, 0:1], [AF.Sigmoid, AF.Relu, AF.Square, AF.Ln, AF.Exp])

        def ln_halves(xa, xb, dst):
            pass

        for tc in range(NCH):
            for i in range(4):
                r0 = tc * 512 + i * 128
                p.wait("sync", s_ve)
                p.wait("sync", s_te)
                p.wait("sync", s_sc)
                E("sync", lambda e, r0=r0: e.dma_start(out=xx[:, :], in_=x3_h[r0 + 1:r0 + 129, :]).then_inc(s_lda.h, 16))
                s_lda.inc(16)
                E("sync", lambda e, r0=r0: e.dma_start(out=xsh[:, :], in_=x3_h[r0:r0 + 128, :]).then_inc(s_lda.h, 16))
                s_lda.inc(16)
                p.wait("vector", s_lda)
                p.wait("vector", s_te)
                p.wait("vector", s_sc)
                for src_t, dst_t in ((xx, zc), (xsh, zp)):
                    p.wait("scalar", s_lda)
                    p.wait("scalar", s_ve)
                    mean_ap, rstd_ap = ln_core(p, E, s_ve, s_sc, [(src_t[:, :], 1024)], lnjunk, lnc, cst[:, 0:1])
                    p.wait("vector", s_sc)
                    E("vector", lambda e, src_t=src_t, dst_t=dst_t, mean_ap=mean_ap, rstd_ap=rstd_ap: e.tensor_scalar(
                        out=dst_t[:, :], in0=src_t[:, :], scalar1=mean_ap,
                        scalar2=rstd_ap, op0=ALU.subtract, op1=ALU.mult).then_inc(s_ve.h, 1))
                    s_ve.inc()
                p.wait("tensor", s_ve)
                for src_t, dst_t in ((zc, zcT), (zp, zpT), (xx, x3T)):
                    for k in range(8):
                        slot = k % 4
                        p.wait("tensor", s_sc)
                        E("tensor", lambda e, k=k, slot=slot, src_t=src_t: e.transpose(
                            ptpf[:, slot * 128:(slot + 1) * 128], src_t[:, k * 128:(k + 1) * 128],
                            identf[:, :]).then_inc(s_te.h, 1))
                        s_te.inc()
                        p.wait("scalar", s_te)
                        E("scalar", lambda e, k=k, slot=slot, i=i, dst_t=dst_t: e.copy(
                            dst_t[:, k * 512 + i * 128:k * 512 + (i + 1) * 128],
                            ptpf[:, slot * 128:(slot + 1) * 128]).then_inc(s_sc.h, 1))
                        s_sc.inc()
            zT_ready = s_sc.v

            p.wait("vector", s_sc, zT_ready)
            p.wait("vector", s_te)
            for kc in range(8):
                E("vector", lambda e, kc=kc: e.tensor_sub(dT[:, :], zcT[:, kc * 512:(kc + 1) * 512], zpT[:, kc * 512:(kc + 1) * 512]))
                E("vector", lambda e, kc=kc: e.scalar_tensor_tensor(
                    out=xr[:, kc * 512:(kc + 1) * 512], in0=dT[:, :], scalar=murc[:, kc:kc + 1],
                    in1=zpT[:, kc * 512:(kc + 1) * 512], op0=ALU.mult, op1=ALU.add))
                E("vector", lambda e, kc=kc: e.scalar_tensor_tensor(
                    out=xk[:, kc * 512:(kc + 1) * 512], in0=dT[:, :], scalar=mukc[:, kc:kc + 1],
                    in1=zpT[:, kc * 512:(kc + 1) * 512], op0=ALU.mult, op1=ALU.add))
            E("vector", lambda e: e.nop())
            E("vector", lambda e: e.tensor_copy(t2b[:, 0:1], xr[:, 0:1]).then_inc(s_ve.h, 1))
            s_ve.inc()
            mix_ready = s_ve.v

            for oc in range(8):
                bank = oc % 2
                for kc in range(8):
                    if kc == 0:
                        p.wait("tensor", s_ve, mix_ready)
                        p.wait("tensor", s_sc)
                        p.wait("tensor", s_ve)
                    E("tensor", lambda e, oc=oc, kc=kc, bank=bank, last=(kc == 7): e.matmul(
                        pm[bank][:, :], wr[:, kc * 1024 + oc * 128:kc * 1024 + (oc + 1) * 128],
                        xr[:, kc * 512:(kc + 1) * 512], start=(kc == 0), stop=last).then_inc(s_te.h, 1))
                    s_te.inc()
                p.wait("scalar", s_te)
                p.wait("scalar", s_ve)
                E("scalar", lambda e, oc=oc, bank=bank: e.activation(
                    rsg[:, oc * 512:(oc + 1) * 512], pm[bank][:, :], AF.Sigmoid).then_inc(s_sc.h, 1))
                s_sc.inc()
            for oc in range(8):
                bank = oc % 2
                for kc in range(8):
                    if kc == 0:
                        p.wait("tensor", s_sc)
                        p.wait("tensor", s_ve)
                    E("tensor", lambda e, oc=oc, kc=kc, bank=bank, last=(kc == 7): e.matmul(
                        pm[bank][:, :], wk[:, kc * 1024 + oc * 128:kc * 1024 + (oc + 1) * 128],
                        xk[:, kc * 512:(kc + 1) * 512], start=(kc == 0), stop=last).then_inc(s_te.h, 1))
                    s_te.inc()
                p.wait("scalar", s_te)
                p.wait("scalar", s_ve)
                E("scalar", lambda e, oc=oc, bank=bank: e.activation(
                    t2b[:, :], pm[bank][:, :], AF.Relu))
                E("scalar", lambda e, oc=oc: e.activation(
                    kact[:, oc * 512:(oc + 1) * 512], t2b[:, :], AF.Square).then_inc(s_sc.h, 1))
                s_sc.inc()
            kact_ready = s_sc.v
            for oc in range(8):
                bank = oc % 2
                for kc in range(8):
                    if kc == 0:
                        p.wait("tensor", s_sc, kact_ready)
                        p.wait("tensor", s_sc)
                        p.wait("tensor", s_ve)
                    E("tensor", lambda e, oc=oc, kc=kc, bank=bank, last=(kc == 7): e.matmul(
                        pm[bank][:, :], wv[:, kc * 1024 + oc * 128:kc * 1024 + (oc + 1) * 128],
                        kact[:, kc * 512:(kc + 1) * 512], start=(kc == 0), stop=last).then_inc(s_te.h, 1))
                    s_te.inc()
                p.wait("vector", s_te)
                E("vector", lambda e, oc=oc, bank=bank: e.tensor_mul(
                    t2b[:, :], rsg[:, oc * 512:(oc + 1) * 512], pm[bank][:, :]))
                E("vector", lambda e, oc=oc, bank=bank: e.tensor_add(
                    outf[:, oc * 512:(oc + 1) * 512], x3T[:, oc * 512:(oc + 1) * 512], t2b[:, :]).then_inc(s_ve.h, 1))
                s_ve.inc()
            out_ready = s_ve.v

            p.wait("tensor", s_ve, out_ready)
            for i in range(4):
                for k in range(8):
                    slot = k % 4
                    p.wait("tensor", s_sc)
                    p.wait("tensor", s_st)
                    E("tensor", lambda e, i=i, k=k, slot=slot: e.transpose(
                        ptpf[:, slot * 128:(slot + 1) * 128], outf[:, k * 512 + i * 128:k * 512 + (i + 1) * 128],
                        identf[:, :]).then_inc(s_te.h, 1))
                    s_te.inc()
                    p.wait("scalar", s_te)
                    E("scalar", lambda e, i=i, k=k, slot=slot: e.copy(
                        outt[:, i * 1024 + k * 128:i * 1024 + (k + 1) * 128],
                        ptpf[:, slot * 128:(slot + 1) * 128]).then_inc(s_sc.h, 1))
                    s_sc.inc()
                p.wait("gpsimd", s_sc)
                E("gpsimd", lambda e, tc=tc, i=i: e.dma_start(
                    out_h[tc * 512 + i * 128:tc * 512 + (i + 1) * 128, :], outt[:, i * 1024:(i + 1) * 1024]).then_inc(s_st.h, 16))
                s_st.inc(16)
                p.wait("gpsimd", s_st)

        p.wait("gpsimd", s_st)
        with nc.Block() as block:
            p.attach(block)
    return nc


def l1_ffn_inputs(inp, core, OWN=2048):
    import ml_dtypes
    b, h = core // 2, core % 2
    s = h * OWN
    x3 = np.ascontiguousarray(inp["_x3"][b])
    rows = np.zeros((OWN + 1, 1024), np.float32)
    lo = max(s - 1, 0)
    rows[(lo - (s - 1)):, :] = x3[lo:s + OWN, :]
    out = {"x3in": rows}
    out["wrt"] = np.ascontiguousarray(inp["L1_ffn_W_r"].T)
    out["wkt"] = np.ascontiguousarray(inp["L1_ffn_W_k"].T)
    out["wvt"] = np.ascontiguousarray(inp["L1_ffn_W_v"].T)
    out["murc"] = np.ascontiguousarray(inp["L1_ffn_mu_r"].reshape(8, 128).T)
    out["mukc"] = np.ascontiguousarray(inp["L1_ffn_mu_k"].reshape(8, 128).T)
    out["ident"] = np.eye(128, dtype=np.float32).astype(ml_dtypes.bfloat16)
    out["identf"] = np.eye(128, dtype=np.float32)
    return out


def _run2(nc, maps, ncores):
    return run_bass_kernel_spmd(nc, maps, core_ids=ncores)


def kernel(**inputs):
    inp = {k: np.asarray(v) for k, v in inputs.items()}
    ncores = list(range(8))

    nc1 = build_l0_attn()
    res1 = _run2(nc1, [l0_attn_inputs(inp, c) for c in range(8)], ncores)
    inp["_part0"] = np.stack([res1.results[2 * b]["part"].T for b in range(4)])
    inp["_part1"] = np.stack([res1.results[2 * b + 1]["part"].T for b in range(4)])

    nc2 = build_ffn_a(do_ln=True, resid3=True)
    res2 = _run2(nc2, [ffn_a_inputs(inp, None, c, 0) for c in range(8)], ncores)
    za0 = np.zeros((4, 4096, 1024), np.float32)
    x1 = np.zeros((4, 4096, 1024), np.float32)
    for c in range(8):
        b, h = c // 2, c % 2
        za0[b, h * 2048:(h + 1) * 2048] = res2.results[c]["za"][:, 512:2560].T
        x1[b, h * 2048:(h + 1) * 2048] = res2.results[c]["x1out"]
    inp["_za0"] = za0
    inp["_x1"] = x1

    nc3 = build_ffn_b(final_resid=False)
    res3 = _run2(nc3, [ffn_b_inputs(inp, c, 0, False) for c in range(8)], ncores)
    zf0 = np.zeros((4, 4096, 1024), np.float32)
    for c in range(8):
        b, h = c // 2, c % 2
        zf0[b, h * 2048:(h + 1) * 2048] = res3.results[c]["zf"].T
    inp["_zf0"] = zf0

    nc4 = build_ffn_a(do_ln=False, resid3=False)
    res4 = _run2(nc4, [ffn_a_inputs(inp, None, c, 1) for c in range(8)], ncores)
    za1 = np.zeros((4, 4096, 1024), np.float32)
    for c in range(8):
        b, h = c // 2, c % 2
        za1[b, h * 2048:(h + 1) * 2048] = res4.results[c]["za"][:, 512:2560].T
    inp["_za1"] = za1

    nc5 = build_ffn_b(final_resid=True)
    res5 = _run2(nc5, [ffn_b_inputs(inp, c, 1, True) for c in range(8)], ncores)
    x2 = np.zeros((4, 4096, 1024), np.float32)
    for c in range(8):
        b, h = c // 2, c % 2
        x2[b, h * 2048:(h + 1) * 2048] = res5.results[c]["zf"].T
    inp["_x2"] = x2

    nc6 = build_l1_attn()
    res6 = _run2(nc6, [l1_attn_inputs(inp, c) for c in range(8)], ncores)
    x3 = np.zeros((4, 4096, 1024), np.float32)
    for c in range(8):
        b, pp = c // 2, c % 2
        r = res6.results[c]["x3"].T
        for i in range(8):
            qc = 2 * i + pp
            x3[b, qc * 256:(qc + 1) * 256] = r[i * 256:(i + 1) * 256]
    inp["_x3"] = x3

    nc7 = build_l1_ffn()
    res7 = _run2(nc7, [l1_ffn_inputs(inp, c) for c in range(8)], ncores)
    out = np.zeros((4, 4096, 1024), np.float32)
    for c in range(8):
        b, h = c // 2, c % 2
        out[b, h * 2048:(h + 1) * 2048] = res7.results[c]["xout"]
    return out
